# version 01
# baseline (speedup 1.0000x reference)
from contextlib import ExitStack

import numpy as np
import ml_dtypes

import concourse.bass as bass
import concourse.mybir as mybir
from concourse.bass_utils import run_bass_kernel_spmd

F32 = mybir.dt.float32
BF16 = mybir.dt.bfloat16
I32 = mybir.dt.int32
AF = mybir.ActivationFunctionType
ALU = mybir.AluOpType
AX = mybir.AxisListType
bfloat16 = ml_dtypes.bfloat16

B = 2
DM = 1024
NCH = 8
HEADS = 16
HD = 64
NPAIR = 8
NLAT = 8
PAGE = 16
FFND = 2048
NEXP = 8
NFC = 16
EPS = 1e-5
SCALE = 0.125
SEGC = 8

ENGS = ("sync", "gpsimd", "tensor", "scalar", "vector")


def qset(q):
    return sorted([q, 7 - q, 8 + q, 15 - q])


class Flow:

    def __init__(self, nc, sems):
        self.nc = nc
        self.S = sems
        self.c = {k: 0 for k in sems}
        self.which = None
        self.eng = None
        self.qreg = None

    AUTO = {"vector": "s_dve", "scalar": "s_act", "gpsimd_c": "s_mk"}

    def _do(self, engname, fn, waits, inc):
        auto = self.AUTO.get(engname)
        if auto is not None:
            engname = "gpsimd" if engname == "gpsimd_c" else engname
            assert inc is None or inc[0] == auto
            waits = list(waits) + [(auto, self.c[auto])]
            inc = (auto, 1)
        for s, v in waits:
            assert v <= self.c[s], f"wait on future {s}:{v} (at {self.c[s]})"
        if self.which == engname:
            for s, v in waits:
                if v > 0:
                    self.eng.wait_ge(self.S[s], v)
            inst = fn(self.eng)
            if inc is not None:
                inst.then_inc(self.S[inc[0]], inc[1])
        if inc is not None:
            self.c[inc[0]] += inc[1]

    def mm(self, out, lhsT, rhs, start=True, stop=True, waits=(), inc=None):
        self._do("tensor", lambda e: e.matmul(out, lhsT, rhs, start=start, stop=stop),
                 waits, inc)

    def tp(self, out, in_, ident, waits=(), inc=None):
        self._do("tensor", lambda e: e.transpose(out, in_, ident), waits, inc)

    def act(self, out, in_, func, scale=1.0, bias=0.0, waits=(), inc=None):
        self._do("scalar", lambda e: e.activation(out, in_, func, bias=bias, scale=scale),
                 waits, inc)

    def acopy(self, out, in_, waits=(), inc=None):
        self._do("scalar", lambda e: e.copy(out, in_), waits, inc)

    def tt(self, out, a, b, op, waits=(), inc=None):
        self._do("vector", lambda e: e.tensor_tensor(out, a, b, op), waits, inc)

    def ts(self, out, a, s1, op, waits=(), inc=None):
        self._do("vector", lambda e: e.tensor_single_scalar(out, a, s1, op), waits, inc)

    def vcopy(self, out, in_, waits=(), inc=None):
        self._do("vector", lambda e: e.tensor_copy(out, in_), waits, inc)

    def vrecip(self, out, in_, waits=(), inc=None):
        self._do("vector", lambda e: e.reciprocal(out, in_), waits, inc)

    def vmax(self, out, in_, waits=(), inc=None):
        self._do("vector", lambda e: e.reduce_max(out, in_, AX.X), waits, inc)

    def gmemset(self, ap, val, waits=(), inc=None):
        self._do("gpsimd_c", lambda e: e.memset(ap, val), waits, inc)

    def sdma(self, out, in_, waits=(), inc=None):
        self._do("sync", lambda e: e.dma_start(out=out, in_=in_), waits, inc)

    def gdma(self, out, in_, waits=(), inc=None):
        self._do("gpsimd", lambda e: e.dma_start(out=out, in_=in_), waits, inc)

    def wait(self, engname, sem, val):
        assert val <= self.c[sem], f"wait on future {sem}:{val}"
        if self.which == engname and val > 0:
            self.eng.wait_ge(self.S[sem], val)

    def dbg_tap(self, name, dram, src, waits=()):
        sem = "s_dbg_" + name
        self.sdma(dram, src, waits=waits, inc=(sem, 16))
        self.wait("vector", sem, self.c[sem])

    def if_q(self, k):
        return self.eng.If_eq(self.qreg, k)

    def variants(self, fn):
        base = dict(self.c)
        ends = []
        for k in range(4):
            self.c = dict(base)
            with self.if_q(k):
                fn(k)
            ends.append(dict(self.c))
        for o in ends[1:]:
            assert o == ends[0], "variant counter mismatch"
        self.c = ends[0]


def build_nc(T=8192, QT=512, debug=False):
    NQT = T // QT
    assert NQT == 16
    NB = QT // 128
    OWN = 4 * QT
    HALF = OWN // 2
    NTT = HALF // QT

    nc = bass.Bass("TRN2", target_bir_lowering=False)

    xf = nc.dram_tensor("xf", [DM, T], F32, kind="ExternalInput").ap()
    wq = nc.dram_tensor("wq", [DM, DM], F32, kind="ExternalInput").ap()
    wk = nc.dram_tensor("wk", [DM, DM], F32, kind="ExternalInput").ap()
    wv = nc.dram_tensor("wv", [DM, DM], F32, kind="ExternalInput").ap()
    wo = nc.dram_tensor("wo", [DM, DM], F32, kind="ExternalInput").ap()
    wlk = nc.dram_tensor("wlk", [HD, DM], F32, kind="ExternalInput").ap()
    wlv = nc.dram_tensor("wlv", [HD, DM], F32, kind="ExternalInput").ap()
    latk = nc.dram_tensor("latk", [HD, NLAT], F32, kind="ExternalInput").ap()
    latv = nc.dram_tensor("latv", [HD, NLAT], F32, kind="ExternalInput").ap()
    wgt = nc.dram_tensor("wgt", [DM, NEXP], F32, kind="ExternalInput").ap()
    w1 = nc.dram_tensor("w1", [NEXP, DM, FFND], F32, kind="ExternalInput").ap()
    w2 = nc.dram_tensor("w2", [NEXP, DM, FFND], F32, kind="ExternalInput").ap()
    w3 = nc.dram_tensor("w3", [NEXP, FFND, DM], F32, kind="ExternalInput").ap()
    ctab = nc.dram_tensor("ctab", [128, T], F32, kind="ExternalInput").ap()
    stab = nc.dram_tensor("stab", [128, T], F32, kind="ExternalInput").ap()
    pswap = nc.dram_tensor("pswap", [128, 128], F32, kind="ExternalInput").ap()
    dmask = nc.dram_tensor("dmask", [128, NB, QT], F32, kind="ExternalInput").ap()
    identf = nc.dram_tensor("identf", [128, 128], F32, kind="ExternalInput").ap()
    sel16i = nc.dram_tensor("sel16i", [16, 16, 64], F32, kind="ExternalInput").ap()
    sel8i = nc.dram_tensor("sel8i", [NEXP, NEXP, 128], F32, kind="ExternalInput").ap()
    ehi = nc.dram_tensor("ehi", [128, 16, 16], F32, kind="ExternalInput").ap()
    qparam = nc.dram_tensor("qparam", [1, 1], I32, kind="ExternalInput").ap()

    yf = nc.dram_tensor("yf", [DM, OWN], F32, kind="ExternalOutput").ap()

    k_scr = nc.dram_tensor("k_scr", [DM, T], BF16).ap()
    v_scr = nc.dram_tensor("v_scr", [T, 1040], BF16).ap()
    q_scr = nc.dram_tensor("q_scr", [DM, OWN], BF16).ap()
    x2_scr = nc.dram_tensor("x2_scr", [DM, OWN], F32).ap()

    dbg = {}
    if debug:
        def mkdbg(name, shape, dt=F32):
            dbg[name] = nc.dram_tensor("dbg_" + name, shape, dt, kind="ExternalOutput").ap()
        mkdbg("xn0", [128, NCH, QT], BF16)
        mkdbg("krot0", [128, QT], BF16)
        mkdbg("vext0", [128, 1040], BF16)
        mkdbg("attnU0", [65, QT])
        mkdbg("attnb0", [64, HEADS, QT], BF16)
        mkdbg("x20", [128, NCH, QT])
        mkdbg("wtm0", [128, NEXP])
        mkdbg("hw0", [128, QT], BF16)

    sem_names = [
        "s_c", "s_cg", "s_in0", "s_in1", "s_qin0", "s_qin1",
        "s_xg0", "s_xg1", "s_qg0", "s_qg1",
        "s_kv0", "s_kv1", "s_q0", "s_q1", "s_xr",
        "s_sk0", "s_sk1", "s_sv0", "s_sv1", "s_sqr0", "s_sqr1",
        "s_stx", "s_w1", "s_w2", "s_w3", "s_x2l",
        "s_out", "s_pe", "s_act", "s_dve", "s_mk",
    ]
    if debug:
        sem_names += ["s_dbg_" + n for n in
                      ["xn0", "krot0", "vext0", "attnU0", "attnb0", "x20",
                       "wtm0", "hw0"]]

    blk_tag = [0]

    with ExitStack() as top:
        sems = {n: top.enter_context(nc.semaphore(n)) for n in sem_names}
        F = Flow(nc, sems)

        def run_block(blk, body):
            res = {}
            c0 = dict(F.c)
            blk_tag[0] += 1
            tag = blk_tag[0]

            def section(name):
                def _(eng):
                    F.which, F.eng = name, eng
                    F.c = dict(c0)
                    with ExitStack() as rs:
                        F.qreg = rs.enter_context(eng.register(f"qr{tag}_{name}"))
                        if tag == 1 and name == "sync":
                            eng.dma_start(out=qsb[:, :], in_=qparam[:, :]).then_inc(
                                sems["s_c"], 16)
                        eng.wait_ge(sems["s_c"], 16)
                        eng.reg_load(F.qreg, qsb[0:1, 0:1])
                        if tag > 1:
                            for sem, v in c0.items():
                                if v > 0:
                                    eng.wait_ge(sems[sem], v)
                        body()
                    res[name] = dict(F.c)
                return _

            for name in ENGS:
                getattr(blk, name)(section(name))
            vals = list(res.values())
            for v in vals[1:]:
                assert v == vals[0], f"engine counter mismatch"
            F.c = vals[0]
            F.which = F.eng = None

        e = top.enter_context
        qsb = e(nc.sbuf_tensor("qsb", [1, 1], I32))
        ones_bf = e(nc.sbuf_tensor("ones_bf", [128, 1], BF16))
        ones1f = e(nc.sbuf_tensor("ones1f", [1, 128], F32))
        eps_sb = e(nc.sbuf_tensor("eps_sb", [1, 1], F32))
        ident_b = e(nc.sbuf_tensor("ident_b", [128, 128], BF16))
        ident_f = e(nc.sbuf_tensor("ident_f", [128, 128], F32))
        lk_sb = e(nc.sbuf_tensor("lk_sb", [128, NPAIR, NLAT], BF16))
        lv_ext = e(nc.sbuf_tensor("lv_ext", [NLAT, 1040], BF16))
        sel16 = e(nc.sbuf_tensor("sel16", [16, 16, 64], F32))
        eh_sb = e(nc.sbuf_tensor("eh_sb", [128, 16, 16], F32))
        pswap_sb = e(nc.sbuf_tensor("pswap_sb", [128, 128], BF16))

        def make_ln(ln, QTW):
            def emit_ln(xb_chunks, xn_out_chunks, entry_waits):
                guard = [("s_dve", F.c["s_dve"]), ("s_act", F.c["s_act"])]
                F.act(ln["sq"][:, :, :], xb_chunks(None), AF.Square,
                      waits=list(entry_waits) + [("s_pe", F.c["s_pe"])],
                      inc=("s_act", 1))
                sq_done = F.c["s_act"]
                for kc in range(NCH):
                    F.mm(ln["pS1"][0:1, 0:QTW], ones_bf[:, 0:1], xb_chunks(kc),
                         start=(kc == 0), stop=(kc == NCH - 1),
                         waits=(list(entry_waits) + guard) if kc == 0 else ())
                for kc in range(NCH):
                    F.mm(ln["pS2"][0:1, 0:QTW], ones_bf[:, 0:1], ln["sq"][:, kc, :],
                         start=(kc == 0), stop=(kc == NCH - 1),
                         waits=[("s_act", sq_done)] if kc == 0 else (),
                         inc=("s_pe", 1) if kc == NCH - 1 else None)
                F.ts(ln["mean"][:, :], ln["pS1"][0:1, 0:QTW], 1.0 / DM, ALU.mult,
                     waits=[("s_pe", F.c["s_pe"])])
                F.ts(ln["tsc"][:, :], ln["pS2"][0:1, 0:QTW], 1.0 / DM, ALU.mult)
                F.tt(ln["var"][:, :], ln["mean"][:, :], ln["mean"][:, :], ALU.mult)
                F.tt(ln["var"][:, :], ln["tsc"][:, :], ln["var"][:, :], ALU.subtract,
                     inc=("s_dve", 1))
                F.act(ln["tsc"][:, :], ln["var"][:, :], AF.Ln, bias=eps_sb[0:1, 0:1],
                      waits=[("s_dve", F.c["s_dve"])])
                F.act(ln["rstd"][:, :], ln["tsc"][:, :], AF.Exp, scale=-0.5,
                      inc=("s_act", 1))
                F.mm(ln["pB1"][:, 0:QTW], ones1f[0:1, :], ln["mean"][0:1, :],
                     waits=[("s_act", F.c["s_act"]), ("s_dve", F.c["s_dve"])])
                F.mm(ln["pB2"][:, 0:QTW], ones1f[0:1, :], ln["rstd"][0:1, :],
                     inc=("s_pe", 1))
                bc = F.c["s_pe"]
                for kc in range(NCH):
                    F.tt(ln["tmp"][:, :], xb_chunks(kc), ln["pB1"][:, 0:QTW],
                         ALU.subtract, waits=[("s_pe", bc)] if kc == 0 else ())
                    F.tt(xn_out_chunks(kc), ln["tmp"][:, :], ln["pB2"][:, 0:QTW],
                         ALU.mult, inc=("s_dve", 1) if kc == NCH - 1 else None)
                return F.c["s_dve"]
            return emit_ln

        with ExitStack() as p1s:
            e1 = p1s.enter_context
            wk_sb = e1(nc.sbuf_tensor("wk_sb", [128, NCH, DM], BF16))
            wv_sb = e1(nc.sbuf_tensor("wv_sb", [128, NCH, DM], BF16))
            wlk_sb = e1(nc.sbuf_tensor("wlk_sb", [HD, DM], BF16))
            wlv_sb = e1(nc.sbuf_tensor("wlv_sb", [HD, DM], BF16))
            latk_sb = e1(nc.sbuf_tensor("latk_sb", [HD, NLAT], BF16))
            latv_sb = e1(nc.sbuf_tensor("latv_sb", [HD, NLAT], BF16))
            xbuf = [e1(nc.sbuf_tensor(f"xbuf{r}", [128, NCH, QT], BF16)) for r in range(2)]
            ctb1 = [e1(nc.sbuf_tensor(f"ctb1{r}", [128, QT], F32)) for r in range(2)]
            stb1 = [e1(nc.sbuf_tensor(f"stb1{r}", [128, QT], F32)) for r in range(2)]
            xn1 = [e1(nc.sbuf_tensor(f"xn1{r}", [128, NCH, QT], BF16)) for r in range(2)]
            ln1 = {
                "sq": e1(nc.sbuf_tensor("sq1", [128, NCH, QT], BF16)),
                "mean": e1(nc.sbuf_tensor("mean1", [1, QT], F32)),
                "var": e1(nc.sbuf_tensor("var1", [1, QT], F32)),
                "tsc": e1(nc.sbuf_tensor("tsc1", [1, QT], F32)),
                "rstd": e1(nc.sbuf_tensor("rstd1", [1, QT], F32)),
                "tmp": e1(nc.sbuf_tensor("tmp1", [128, QT], F32)),
            }
            kbf = [e1(nc.sbuf_tensor(f"kbf{r}", [128, QT], BF16)) for r in range(2)]
            t1_sb = e1(nc.sbuf_tensor("t1_sb", [128, QT], F32))
            t2_sb = e1(nc.sbuf_tensor("t2_sb", [128, QT], F32))
            krot = [e1(nc.sbuf_tensor(f"krot{r}", [128, QT], BF16)) for r in range(2)]
            vext = [e1(nc.sbuf_tensor(f"vext{r}", [128, 1040], BF16)) for r in range(2)]

            pLN1 = e1(nc.psum_tensor("pLN1", [128, 512], F32))
            pLN2 = e1(nc.psum_tensor("pLN2", [128, 512], F32))
            pB1 = e1(nc.psum_tensor("pB1", [128, 512], F32))
            pB2 = e1(nc.psum_tensor("pB2", [128, 512], F32))
            pK = [e1(nc.psum_tensor(f"pK{r}", [128, 512], F32)) for r in range(2)]
            pW = [e1(nc.psum_tensor(f"pW{r}", [128, 512], F32)) for r in range(2)]
            ln1.update(pS1=pLN1, pS2=pLN2, pB1=pB1, pB2=pB2)

            blk1 = e1(nc.Block())
            state = {}

            def body01():
                F.c["s_c"] = max(F.c["s_c"], 16)
                F.sdma(ident_f[:, :], identf[:, :], inc=("s_c", 16))
                F.gdma(wk_sb[:, :, :], wk.rearrange("(c p) n -> p c n", p=128),
                       inc=("s_cg", 16))
                F.gdma(wv_sb[:, :, :], wv.rearrange("(c p) n -> p c n", p=128),
                       inc=("s_cg", 16))
                for dst, src in ((wlk_sb, wlk), (wlv_sb, wlv), (latk_sb, latk),
                                 (latv_sb, latv), (pswap_sb, pswap), (ident_b, identf)):
                    F.gdma(dst[:, :], src[:, :], inc=("s_cg", 16))
                F.sdma(sel16[:, :, :], sel16i[:, :, :], inc=("s_c", 16))
                F.sdma(eh_sb[:, :, :], ehi[:, :, :], inc=("s_c", 16))
                n_cg = F.c["s_cg"]
                n_c = F.c["s_c"]
                F.gmemset(ones_bf[:, :], 1.0)
                F.gmemset(ones1f[:, :], 1.0)
                F.gmemset(eps_sb[:, :], EPS)
                for r in range(2):
                    F.gmemset(vext[r][:, :], 0.0)
                F.gmemset(lv_ext[:, :], 0.0, inc=("s_mk", 1))
                zdone = F.c["s_mk"]
                first = True
                for r in range(2):
                    F.gmemset(vext[r][:, :].rearrange("p (g x) -> p g x", x=65)[:, :, 64:65],
                              1.0, waits=[("s_mk", zdone)] if first else ())
                    first = False
                F.gmemset(lv_ext[:, :].rearrange("p (g x) -> p g x", x=65)[:, :, 64:65],
                          1.0, inc=("s_mk", 1))
                ms_done = ("s_mk", F.c["s_mk"])

                for pc in range(NPAIR):
                    F.mm(pK[0][:, 0:NLAT], wlk_sb[0:HD, 128 * pc:128 * (pc + 1)],
                         latk_sb[0:HD, :],
                         waits=[("s_cg", n_cg), ms_done, ("s_act", F.c["s_act"])],
                         inc=("s_pe", 1))
                    F.acopy(lk_sb[:, pc, :], pK[0][:, 0:NLAT],
                            waits=[("s_pe", F.c["s_pe"])], inc=("s_act", 1))
                F.mm(pK[1][0:NLAT, 0:512], latv_sb[0:HD, :], wlv_sb[0:HD, 0:512],
                     waits=[("s_cg", n_cg), ms_done])
                F.mm(pW[0][0:NLAT, 0:512], latv_sb[0:HD, :], wlv_sb[0:HD, 512:1024],
                     inc=("s_pe", 1))
                lvp = F.c["s_pe"]
                F.vcopy(lv_ext[:, 0:520].rearrange("p (g x) -> p g x", x=65)[:, :, 0:64],
                        pK[1][0:NLAT, 0:512].rearrange("p (g x) -> p g x", x=64),
                        waits=[("s_pe", lvp), ms_done])
                F.vcopy(lv_ext[:, 520:1040].rearrange("p (g x) -> p g x", x=65)[:, :, 0:64],
                        pW[0][0:NLAT, 0:512].rearrange("p (g x) -> p g x", x=64),
                        inc=("s_dve", 1))
                lat_act = F.c["s_act"]
                lat_dve = F.c["s_dve"]

                emit_ln = make_ln(ln1, QT)
                xfr = xf.rearrange("(c p) t -> p c t", p=128)
                kscr_r = k_scr.rearrange("(c p) t -> p c t", p=128)
                in_free = {0: 0, 1: 0}
                vext_free = {0: 0, 1: 0}
                krot_free = {0: 0, 1: 0}
                prs = {"pP_free": {0: lat_dve, 1: lat_dve},
                       "pW_free": {0: lat_dve, 1: lat_dve}}
                for qt in range(NQT):
                    r = qt % 2
                    gen = qt // 2 + 1
                    sname = f"s_in{r}"
                    F.gdma(xbuf[r][:, :, :], xfr[:, :, QT * qt:QT * (qt + 1)],
                           waits=[("s_dve", in_free[r])], inc=(f"s_xg{r}", 16))
                    F.sdma(ctb1[r][:, :], ctab[:, QT * qt:QT * (qt + 1)],
                           waits=[("s_dve", in_free[r])], inc=(sname, 16))
                    F.sdma(stb1[r][:, :], stab[:, QT * qt:QT * (qt + 1)],
                           inc=(sname, 16))
                    entry = [(sname, 32 * gen), (f"s_xg{r}", 16 * gen)]
                    if qt == 0:
                        entry += [("s_cg", n_cg), ("s_c", n_c),
                                  ("s_dve", F.c["s_dve"])]
                    xn_done = emit_ln(
                        lambda kc, _r=r: xbuf[_r][:, :, :] if kc is None
                        else xbuf[_r][:, kc, :],
                        lambda kc, _r=r: xn1[_r][:, kc, :], entry)

                    for pc in range(NPAIR):
                        kr = pc % 2
                        kw = [("s_dve", max(xn_done, prs["pP_free"][kr]))]
                        if qt == 0 and pc < 2:
                            kw.append(("s_act", lat_act))
                        for kc in range(NCH):
                            F.mm(pK[kr][:, 0:QT], wk_sb[:, kc, 128 * pc:128 * (pc + 1)],
                                 xn1[r][:, kc, :],
                                 start=(kc == 0), stop=(kc == NCH - 1),
                                 waits=kw if kc == 0 else (),
                                 inc=("s_pe", 1) if kc == NCH - 1 else None)
                        F.acopy(kbf[kr][:, :], pK[kr][:, 0:QT],
                                waits=[("s_pe", F.c["s_pe"])], inc=("s_act", 1))
                        F.mm(pW[kr][:, 0:QT], pswap_sb[:, :], kbf[kr][:, :],
                             waits=[("s_act", F.c["s_act"]),
                                    ("s_dve", prs["pW_free"][kr])], inc=("s_pe", 1))
                        F.tt(t1_sb[:, :], pK[kr][:, 0:QT], ctb1[r][:, :], ALU.mult,
                             waits=[("s_pe", F.c["s_pe"])])
                        F.tt(t2_sb[:, :], pW[kr][:, 0:QT], stb1[r][:, :], ALU.mult)
                        F.tt(krot[kr][:, :], t1_sb[:, :], t2_sb[:, :], ALU.add,
                             waits=[(f"s_sk{kr}", krot_free[kr])])
                        prs["pP_free"][kr] = F.c["s_dve"]
                        prs["pW_free"][kr] = F.c["s_dve"]
                        F.sdma(kscr_r[:, pc, QT * qt:QT * (qt + 1)], krot[kr][:, :],
                               waits=[("s_dve", F.c["s_dve"])], inc=(f"s_sk{kr}", 16))
                        krot_free[kr] = F.c[f"s_sk{kr}"]
                        if debug and qt == 0 and pc == 0:
                            F.dbg_tap("krot0", dbg["krot0"][:, :], krot[kr][:, :],
                                   waits=[("s_dve", F.c["s_dve"])])
                    for tb in range(NB):
                        vr = tb % 2
                        for vt in range(2):
                            pv = pLN1 if vt == 0 else pLN2
                            for kc in range(NCH):
                                F.mm(pv[:, 0:512],
                                     xn1[r][:, kc, 128 * tb:128 * (tb + 1)],
                                     wv_sb[:, kc, 512 * vt:512 * (vt + 1)],
                                     start=(kc == 0), stop=(kc == NCH - 1),
                                     waits=[("s_dve", F.c["s_dve"]),
                                            (f"s_sv{vr}", vext_free[vr])]
                                     if kc == 0 else (),
                                     inc=("s_pe", 1) if kc == NCH - 1 else None)
                            F.vcopy(
                                vext[vr][:, 520 * vt:520 * (vt + 1)]
                                .rearrange("p (g x) -> p g x", x=65)[:, :, 0:64],
                                pv[:, 0:512].rearrange("p (g x) -> p g x", x=64),
                                waits=[("s_pe", F.c["s_pe"])],
                                inc=("s_dve", 1) if vt == 1 else None)
                        F.sdma(v_scr[QT * qt + 128 * tb:QT * qt + 128 * (tb + 1), :],
                               vext[vr][:, :],
                               waits=[("s_dve", F.c["s_dve"])], inc=(f"s_sv{vr}", 16))
                        vext_free[vr] = F.c[f"s_sv{vr}"]
                        if debug and qt == 0 and tb == 0:
                            F.dbg_tap("vext0", dbg["vext0"][:, :], vext[vr][:, :],
                                   waits=[("s_dve", F.c["s_dve"])])
                    in_free[r] = F.c["s_dve"]
                    if debug and qt == 0:
                        F.dbg_tap("xn0", dbg["xn0"][:, :, :], xn1[r][:, :, :],
                               waits=[("s_dve", xn_done)])

            run_block(blk1, body01)
            p1_st = [("s_sk0", F.c["s_sk0"]), ("s_sk1", F.c["s_sk1"]),
                     ("s_sv0", F.c["s_sv0"]), ("s_sv1", F.c["s_sv1"])]

        with ExitStack() as p2a:
            e2 = p2a.enter_context
            wq_sb = e2(nc.sbuf_tensor("wq_sb", [128, NCH, DM], BF16))
            xqb = [e2(nc.sbuf_tensor(f"xqb{r}", [128, NCH, QT], BF16)) for r in range(2)]
            ctb2 = [e2(nc.sbuf_tensor(f"ctb2{r}", [128, QT], F32)) for r in range(2)]
            stb2 = [e2(nc.sbuf_tensor(f"stb2{r}", [128, QT], F32)) for r in range(2)]
            xn2t = e2(nc.sbuf_tensor("xn2t", [128, NCH, QT], BF16))
            ln2 = {
                "sq": e2(nc.sbuf_tensor("sq2", [128, NCH, QT], BF16)),
                "mean": e2(nc.sbuf_tensor("mean2", [1, QT], F32)),
                "var": e2(nc.sbuf_tensor("var2", [1, QT], F32)),
                "tsc": e2(nc.sbuf_tensor("tsc2", [1, QT], F32)),
                "rstd": e2(nc.sbuf_tensor("rstd2", [1, QT], F32)),
                "tmp": e2(nc.sbuf_tensor("tmp2", [128, QT], F32)),
            }
            qbf = [e2(nc.sbuf_tensor(f"qbf{r}", [128, QT], BF16)) for r in range(2)]
            tq1 = e2(nc.sbuf_tensor("tq1", [128, QT], F32))
            tq2 = e2(nc.sbuf_tensor("tq2", [128, QT], F32))
            qrot = [e2(nc.sbuf_tensor(f"qrot{r}", [128, QT], BF16)) for r in range(2)]

            pLQ1 = e2(nc.psum_tensor("pLQ1", [128, QT], F32))
            pLQ2 = e2(nc.psum_tensor("pLQ2", [128, QT], F32))
            pBQ1 = e2(nc.psum_tensor("pBQ1", [128, QT], F32))
            pBQ2 = e2(nc.psum_tensor("pBQ2", [128, QT], F32))
            pKQ = [e2(nc.psum_tensor(f"pKQ{r}", [128, QT], F32)) for r in range(2)]
            pWQ = [e2(nc.psum_tensor(f"pWQ{r}", [128, QT], F32)) for r in range(2)]
            ln2.update(pS1=pLQ1, pS2=pLQ2, pB1=pBQ1, pB2=pBQ2)

            blk2 = e2(nc.Block())

            def body2a():
                F.gdma(wq_sb[:, :, :], wq.rearrange("(c p) n -> p c n", p=128),
                       inc=("s_cg", 16))
                n_cg2 = F.c["s_cg"]
                emit_ln = make_ln(ln2, QT)
                xfr = xf.rearrange("(c p) t -> p c t", p=128)
                qscr_r = q_scr.rearrange("(c p) t -> p c t", p=128)

                def var_body(vq):
                    qs = qset(vq)
                    in_free = {0: 0, 1: 0}
                    qrot_free = {0: 0, 1: 0}
                    prs = {"pP_free": {0: 0, 1: 0}, "pW_free": {0: 0, 1: 0}}
                    for i, Q in enumerate(qs):
                        r = i % 2
                        gen = i // 2 + 1
                        sname = f"s_qin{r}"
                        F.gdma(xqb[r][:, :, :], xfr[:, :, QT * Q:QT * (Q + 1)],
                               waits=[("s_dve", in_free[r])], inc=(f"s_qg{r}", 16))
                        F.sdma(ctb2[r][:, :], ctab[:, QT * Q:QT * (Q + 1)],
                               waits=[("s_dve", in_free[r])], inc=(sname, 16))
                        F.sdma(stb2[r][:, :], stab[:, QT * Q:QT * (Q + 1)],
                               inc=(sname, 16))
                        entry = [(sname, 32 * gen), (f"s_qg{r}", 16 * gen)]
                        if i == 0:
                            entry += [("s_cg", n_cg2)]
                        xn_done = emit_ln(
                            lambda kc, _r=r: xqb[_r][:, :, :] if kc is None
                            else xqb[_r][:, kc, :],
                            lambda kc: xn2t[:, kc, :], entry)
                        for pc in range(NPAIR):
                            kr = pc % 2
                            for kc in range(NCH):
                                F.mm(pKQ[kr][:, :], wq_sb[:, kc, 128 * pc:128 * (pc + 1)],
                                     xn2t[:, kc, :],
                                     start=(kc == 0), stop=(kc == NCH - 1),
                                     waits=[("s_dve", max(xn_done, prs["pP_free"][kr]))]
                                     if kc == 0 else (),
                                     inc=("s_pe", 1) if kc == NCH - 1 else None)
                            F.acopy(qbf[kr][:, :], pKQ[kr][:, :],
                                    waits=[("s_pe", F.c["s_pe"])], inc=("s_act", 1))
                            F.mm(pWQ[kr][:, :], pswap_sb[:, :], qbf[kr][:, :],
                                 waits=[("s_act", F.c["s_act"]),
                                        ("s_dve", prs["pW_free"][kr])], inc=("s_pe", 1))
                            F.tt(tq1[:, :], pKQ[kr][:, :], ctb2[r][:, :], ALU.mult,
                                 waits=[("s_pe", F.c["s_pe"])])
                            F.tt(tq2[:, :], pWQ[kr][:, :], stb2[r][:, :], ALU.mult)
                            F.tt(qrot[kr][:, :], tq1[:, :], tq2[:, :], ALU.add,
                                 waits=[(f"s_sqr{kr}", qrot_free[kr])])
                            prs["pP_free"][kr] = F.c["s_dve"]
                            prs["pW_free"][kr] = F.c["s_dve"]
                            F.sdma(qscr_r[:, pc, QT * i:QT * (i + 1)], qrot[kr][:, :],
                                   waits=[("s_dve", F.c["s_dve"])],
                                   inc=(f"s_sqr{kr}", 16))
                            qrot_free[kr] = F.c[f"s_sqr{kr}"]
                        in_free[r] = F.c["s_dve"]

                F.variants(var_body)

            run_block(blk2, body2a)
            q_st = [("s_sqr0", F.c["s_sqr0"]), ("s_sqr1", F.c["s_sqr1"])]

        with ExitStack() as p2b:
            e3 = p2b.enter_context
            wo_sb = e3(nc.sbuf_tensor("wo_sb", [64, HEADS, DM], BF16))
            dmask_sb = e3(nc.sbuf_tensor("dmask_sb", [128, NB, QT], BF16))
            qsl = [e3(nc.sbuf_tensor(f"qsl{r}", [128, NCH, QT], BF16)) for r in range(2)]
            kseg = [e3(nc.sbuf_tensor(f"kseg{r}", [128, NCH, SEGC * 128], BF16))
                    for r in range(2)]
            vseg = [e3(nc.sbuf_tensor(f"vseg{r}", [128, SEGC, 1040], BF16))
                    for r in range(2)]
            probs = [[e3(nc.sbuf_tensor(f"probs{r}_{hh}", [128, QT], BF16))
                      for hh in range(2)] for r in range(2)]
            plat = [e3(nc.sbuf_tensor(f"plat{hh}", [NLAT, QT], BF16)) for hh in range(2)]
            attnU = e3(nc.sbuf_tensor("attnU", [65, HEADS, QT], F32))
            attnb = e3(nc.sbuf_tensor("attnb", [64, HEADS, QT], BF16))
            recip_t = e3(nc.sbuf_tensor("recip_t", [16, QT], F32))
            xres = e3(nc.sbuf_tensor("xres", [128, NCH, QT], F32))

            pS = [[e3(nc.psum_tensor(f"pS{r}_{hh}", [128, QT], F32)) for hh in range(2)]
                  for r in range(2)]
            pAV = [e3(nc.psum_tensor(f"pAV{hh}", [65, QT], F32)) for hh in range(2)]
            pDN = e3(nc.psum_tensor("pDN", [64, QT], F32))
            pRC = e3(nc.psum_tensor("pRC", [64, QT], F32))

            blk3 = e3(nc.Block())

            def body2b():
                F.gdma(wo_sb[0:64, :, :], wo.rearrange("(h d) n -> d h n", d=64),
                       inc=("s_cg", 16))
                F.gdma(dmask_sb[:, :, :], dmask[:, :, :], inc=("s_cg", 16))
                n_cg3 = F.c["s_cg"]
                xfr = xf.rearrange("(c p) t -> p c t", p=128)
                qscr_r = q_scr.rearrange("(c p) t -> p c t", p=128)
                kscr_r = k_scr.rearrange("(c p) t -> p c t", p=128)
                vscr_r = v_scr.rearrange("(n p) c -> p n c", p=128)
                x2r = x2_scr.rearrange("(c p) t -> p c t", p=128)

                def var_body(vq):
                    qs = qset(vq)
                    qsl_free = {0: 0, 1: 0}
                    kv_free = {0: 0, 1: 0}
                    kv_gen = {0: 0, 1: 0}
                    slot_exp = {0: 0, 1: 0}
                    xres_free = 0
                    stx_free = 0
                    ps_guard = 0
                    unit = 0
                    for i, Q in enumerate(qs):
                        qr = i % 2
                        F.sdma(qsl[qr][:, :, :], qscr_r[:, :, QT * i:QT * (i + 1)],
                               waits=[("s_pe", qsl_free[qr])] + q_st,
                               inc=(f"s_q{qr}", 16))
                        q_ready = (f"s_q{qr}", 16 * (i // 2 + 1))
                        F.sdma(xres[:, :, :], xfr[:, :, QT * Q:QT * (Q + 1)],
                               waits=[("s_dve", xres_free), ("s_stx", stx_free)],
                               inc=("s_xr", 16))
                        xres_ready = ("s_xr", F.c["s_xr"])

                        extent = NB * (Q + 1)
                        nseg = (extent + SEGC - 1) // SEGC

                        def load_seg(g):
                            sl = g % 2
                            c0, c1 = SEGC * g, min(SEGC * (g + 1), extent)
                            F.sdma(kseg[sl][:, :, 0:128 * (c1 - c0)],
                                   kscr_r[:, :, 128 * c0:128 * c1],
                                   waits=[("s_pe", kv_free[sl])] + p1_st,
                                   inc=(f"s_kv{sl}", 16))
                            F.sdma(vseg[sl][:, 0:c1 - c0, :], vscr_r[:, c0:c1, :],
                                   inc=(f"s_kv{sl}", 16))
                            kv_gen[sl] += 1

                        load_seg(0)
                        for g in range(nseg):
                            sl = g % 2
                            seg_ready = (f"s_kv{sl}", 32 * kv_gen[sl])
                            if g + 1 < nseg:
                                load_seg(g + 1)
                            c0, c1 = SEGC * g, min(SEGC * (g + 1), extent)
                            for pr in range(NPAIR):
                                pair_first = True
                                if g == 0:
                                    ur = unit % 2
                                    for hh in range(2):
                                        h0 = 64 * hh
                                        F.mm(pS[ur][hh][0:NLAT, :],
                                             lk_sb[h0:h0 + 64, pr, :],
                                             qsl[qr][h0:h0 + 64, pr, :],
                                             waits=[q_ready,
                                                    ("s_act", slot_exp[ur]),
                                                    ("s_dve", ps_guard)]
                                             if hh == 0 else (),
                                             inc=("s_pe", 1) if hh == 1 else None)
                                    sc_done = F.c["s_pe"]
                                    for hh in range(2):
                                        F.act(plat[hh][:, :], pS[ur][hh][0:NLAT, :],
                                              AF.Exp, scale=SCALE,
                                              waits=[("s_pe", sc_done)] if hh == 0 else (),
                                              inc=("s_act", 1) if hh == 1 else None)
                                    slot_exp[ur] = F.c["s_act"]
                                    ex_done = F.c["s_act"]
                                    for hh in range(2):
                                        F.mm(pAV[hh][:, :],
                                             lv_ext[:, 130 * pr + 65 * hh:
                                                    130 * pr + 65 * (hh + 1)],
                                             plat[hh][:, :],
                                             start=True, stop=False,
                                             waits=[("s_act", ex_done),
                                                    ("s_dve", F.c["s_dve"])]
                                             if hh == 0 else ())
                                    unit += 1
                                    pair_first = False
                                for sc in range(c0, c1):
                                    ur = unit % 2
                                    for hh in range(2):
                                        h0 = 64 * hh
                                        F.mm(pS[ur][hh][:, :],
                                             kseg[sl][h0:h0 + 64, pr,
                                                      128 * (sc - c0):128 * (sc - c0 + 1)],
                                             qsl[qr][h0:h0 + 64, pr, :],
                                             waits=[q_ready, seg_ready,
                                                    ("s_act", slot_exp[ur])]
                                             if hh == 0 else (),
                                             inc=("s_pe", 1) if hh == 1 else None)
                                    sc_done = F.c["s_pe"]
                                    for hh in range(2):
                                        F.act(probs[ur][hh][:, :], pS[ur][hh][:, :],
                                              AF.Exp, scale=SCALE,
                                              waits=[("s_pe", sc_done)] if hh == 0 else (),
                                              inc=("s_act", 1) if hh == 1 else None)
                                    slot_exp[ur] = F.c["s_act"]
                                    ex_done = F.c["s_act"]
                                    avw = [("s_act", ex_done)]
                                    if sc >= NB * Q:
                                        j = sc - NB * Q
                                        for hh in range(2):
                                            F.tt(probs[ur][hh][:, :], probs[ur][hh][:, :],
                                                 dmask_sb[:, j, :], ALU.mult,
                                                 waits=[("s_act", ex_done),
                                                        ("s_cg", n_cg3)]
                                                 if hh == 0 else (),
                                                 inc=("s_dve", 1) if hh == 1 else None)
                                        avw.append(("s_dve", F.c["s_dve"]))
                                    if pair_first:
                                        avw.append(("s_dve", F.c["s_dve"]))
                                    for hh in range(2):
                                        F.mm(pAV[hh][:, :],
                                             vseg[sl][:, sc - c0,
                                                      130 * pr + 65 * hh:
                                                      130 * pr + 65 * (hh + 1)],
                                             probs[ur][hh][:, :],
                                             start=(g > 0 and sc == c0),
                                             stop=(sc == c1 - 1),
                                             waits=avw if hh == 0 else (),
                                             inc=("s_pe", 1)
                                             if (sc == c1 - 1 and hh == 1) else None)
                                    pair_first = False
                                    unit += 1
                                av_done = F.c["s_pe"]
                                for hh in range(2):
                                    h = 2 * pr + hh
                                    if g == 0:
                                        F.vcopy(attnU[:, h, :], pAV[hh][:, :],
                                                waits=[("s_pe", av_done)]
                                                if hh == 0 else (),
                                                inc=("s_dve", 1) if hh == 1 else None)
                                    else:
                                        F.tt(attnU[:, h, :], attnU[:, h, :],
                                             pAV[hh][:, :], ALU.add,
                                             waits=[("s_pe", av_done)]
                                             if hh == 0 else (),
                                             inc=("s_dve", 1) if hh == 1 else None)
                            kv_free[sl] = F.c["s_pe"]
                        attnU_done = F.c["s_dve"]

                        for h in range(16):
                            F.mm(pDN[0:16, :], eh_sb[64:65, h, :], attnU[64:65, h, :],
                                 start=(h == 0), stop=(h == 15),
                                 waits=[("s_dve", attnU_done)] if h == 0 else (),
                                 inc=("s_pe", 1) if h == 15 else None)
                        F.vrecip(recip_t[:, :], pDN[0:16, :],
                                 waits=[("s_pe", F.c["s_pe"])], inc=("s_dve", 1))
                        rec_done = F.c["s_dve"]
                        if debug and i == 0:
                            F.dbg_tap("attnU0", dbg["attnU0"][:, :], attnU[:, 0, :],
                                   waits=[("s_dve", attnU_done)])
                        for h in range(16):
                            po = pRC if h % 2 == 0 else pDN
                            F.mm(po[:, :], sel16[:, h, :], recip_t[0:16, :],
                                 waits=[("s_dve", rec_done if h < 2 else F.c["s_dve"])],
                                 inc=("s_pe", 1))
                            F.tt(attnb[0:64, h, :], attnU[0:64, h, :], po[0:64, :],
                                 ALU.mult, waits=[("s_pe", F.c["s_pe"])],
                                 inc=("s_dve", 1))
                        nrm_done = F.c["s_dve"]
                        if debug and i == 0:
                            F.dbg_tap("attnb0", dbg["attnb0"][:, :, :], attnb[:, :, :],
                                   waits=[("s_dve", nrm_done)])
                        for oc in range(NCH):
                            po = pS[oc % 2][0]
                            for h in range(16):
                                F.mm(po[:, :], wo_sb[0:64, h, 128 * oc:128 * (oc + 1)],
                                     attnb[0:64, h, :],
                                     start=(h == 0), stop=(h == 15),
                                     waits=[("s_dve", nrm_done), ("s_cg", n_cg3),
                                            ("s_act", F.c["s_act"]),
                                            ("s_dve", F.c["s_dve"])] if h == 0 else (),
                                     inc=("s_pe", 1) if h == 15 else None)
                            xw = [("s_pe", F.c["s_pe"]), xres_ready]
                            if oc == 0:
                                xw.append(("s_stx", stx_free))
                            F.tt(xres[:, oc, :], xres[:, oc, :], po[:, :], ALU.add,
                                 waits=xw, inc=("s_dve", 1))
                        xres_free = F.c["s_dve"]
                        ps_guard = F.c["s_dve"]
                        slot_exp = {0: F.c["s_act"], 1: F.c["s_act"]}
                        qsl_free[qr] = F.c["s_pe"]
                        F.sdma(x2r[:, :, QT * i:QT * (i + 1)], xres[:, :, :],
                               waits=[("s_dve", F.c["s_dve"])], inc=("s_stx", 16))
                        stx_free = F.c["s_stx"]
                        if debug and i == 0:
                            F.dbg_tap("x20", dbg["x20"][:, :, :], xres[:, :, :],
                                   waits=[("s_dve", F.c["s_dve"])])

                F.variants(var_body)

            run_block(blk3, body2b)
            x2_st = F.c["s_stx"]

        with ExitStack() as p3s:
            e4 = p3s.enter_context
            sel8 = e4(nc.sbuf_tensor("sel8", [NEXP, NEXP, 128], BF16))
            w1_sb = e4(nc.sbuf_tensor("w1_sb", [128, NCH, FFND], BF16))
            w2_sb = e4(nc.sbuf_tensor("w2_sb", [128, NCH, FFND], BF16))
            w3_sb = e4(nc.sbuf_tensor("w3_sb", [128, NFC, DM], BF16))
            wg_sb = e4(nc.sbuf_tensor("wg_sb", [128, NCH, NEXP], BF16))
            acc = e4(nc.sbuf_tensor("acc", [128, NCH, HALF], F32))
            x2bh = e4(nc.sbuf_tensor("x2bh", [128, NCH, HALF], BF16))
            xn3 = e4(nc.sbuf_tensor("xn3", [128, NCH, HALF], BF16))
            ln3 = {
                "mean": e4(nc.sbuf_tensor("mean3", [1, QT], F32)),
                "var": e4(nc.sbuf_tensor("var3", [1, QT], F32)),
                "tsc": e4(nc.sbuf_tensor("tsc3", [1, QT], F32)),
                "rstd": e4(nc.sbuf_tensor("rstd3", [1, QT], F32)),
                "tmp": e4(nc.sbuf_tensor("tmp3", [128, QT], F32)),
            }
            lgb = e4(nc.sbuf_tensor("lgb", [NEXP, QT], BF16))
            lgtm = e4(nc.sbuf_tensor("lgtm", [128, NEXP], F32))
            m1t = e4(nc.sbuf_tensor("m1t", [128, 1], F32))
            m2t = e4(nc.sbuf_tensor("m2t", [128, 1], F32))
            eq1t = e4(nc.sbuf_tensor("eq1t", [128, NEXP], F32))
            eq2t = e4(nc.sbuf_tensor("eq2t", [128, NEXP], F32))
            zt = e4(nc.sbuf_tensor("zt", [128, 1], F32))
            rt = e4(nc.sbuf_tensor("rt", [128, 1], F32))
            wtmt = e4(nc.sbuf_tensor("wtmt", [128, NEXP], F32))
            wT_sb = e4(nc.sbuf_tensor("wT_sb", [NEXP, HALF], BF16))
            w_sb = e4(nc.sbuf_tensor("w_sb", [128, NTT, QT], BF16))
            su_sb = [e4(nc.sbuf_tensor(f"su{r}", [128, QT], BF16)) for r in range(2)]
            hh_sb = e4(nc.sbuf_tensor("hh_sb", [128, QT], BF16))
            hw_sb = e4(nc.sbuf_tensor("hw_sb", [128, NFC, QT], BF16))
            ln3["sq"] = hw_sb[:, 0:NCH, :]

            pU = [e4(nc.psum_tensor(f"pU{r}", [128, QT], F32)) for r in range(2)]
            pV = [e4(nc.psum_tensor(f"pV{r}", [128, QT], F32)) for r in range(2)]
            pYF = [e4(nc.psum_tensor(f"pYF{r}", [128, QT], F32)) for r in range(2)]
            pWB = e4(nc.psum_tensor("pWB", [128, QT], F32))
            pTPb = e4(nc.psum_tensor("pTPb", [128, NEXP], BF16))
            ln3.update(pS1=pU[0], pS2=pU[1], pB1=pV[0], pB2=pV[1])

            blk4 = e4(nc.Block())

            def body3():
                F.gdma(wg_sb[:, :, :], wgt.rearrange("(c p) n -> p c n", p=128),
                       inc=("s_cg", 16))
                F.gdma(sel8[:, :, :], sel8i[:, :, :], inc=("s_cg", 16))
                n_cg4 = F.c["s_cg"]
                n_c4 = F.c["s_c"]
                emit_ln = make_ln(ln3, QT)
                x2r = x2_scr.rearrange("(c p) t -> p c t", p=128)
                w1r = [w1[ex, :, :].rearrange("(c p) n -> p c n", p=128)
                       for ex in range(NEXP)]
                w2r = [w2[ex, :, :].rearrange("(c p) n -> p c n", p=128)
                       for ex in range(NEXP)]
                w3r = [w3[ex, :, :].rearrange("(c p) n -> p c n", p=128)
                       for ex in range(NEXP)]

                w_free = {"w1": 0, "w2": 0, "w3": 0}
                wgen = {"w1": 0, "w2": 0, "w3": 0}

                def load_weights(ex):
                    for nm, src, dst in (("w1", w1r[ex], w1_sb), ("w2", w2r[ex], w2_sb),
                                         ("w3", w3r[ex], w3_sb)):
                        F.gdma(dst[:, :, :], src, waits=[("s_pe", w_free[nm])],
                               inc=(f"s_{nm}", 16))
                        wgen[nm] += 1

                for hf in range(2):
                    F.sdma(acc[:, :, :], x2r[:, :, HALF * hf:HALF * (hf + 1)],
                           waits=[("s_stx", x2_st), ("s_out", 16 * NCH * hf)],
                           inc=("s_x2l", 16))
                    x2_ready = ("s_x2l", F.c["s_x2l"])
                    F.vcopy(x2bh[:, :, :], acc[:, :, :], waits=[x2_ready],
                            inc=("s_dve", 1))
                    cast_done = F.c["s_dve"]
                    for tt_i in range(NTT):
                        sl = slice(QT * tt_i, QT * (tt_i + 1))
                        xn_done = emit_ln(
                            lambda kc, _s=sl: x2bh[:, :, _s] if kc is None
                            else x2bh[:, kc, _s],
                            lambda kc, _s=sl: xn3[:, kc, _s],
                            [("s_dve", cast_done)])
                        for kc in range(NCH):
                            F.mm(pU[0][0:NEXP, :], wg_sb[:, kc, :], xn3[:, kc, sl],
                                 start=(kc == 0), stop=(kc == NCH - 1),
                                 waits=[("s_dve", xn_done), ("s_cg", n_cg4),
                                        ("s_act", F.c["s_act"])] if kc == 0 else (),
                                 inc=("s_pe", 1) if kc == NCH - 1 else None)
                        F.acopy(lgb[:, :], pU[0][0:NEXP, :],
                                waits=[("s_pe", F.c["s_pe"])], inc=("s_act", 1))
                        lg_done = F.c["s_act"]
                        for tb in range(QT // 128):
                            F.tp(pTPb[:, :], lgb[0:NEXP, 128 * tb:128 * (tb + 1)],
                                 ident_b[0:NEXP, 0:NEXP],
                                 waits=[("s_act", lg_done),
                                        ("s_dve", F.c["s_dve"])],
                                 inc=("s_pe", 1))
                            F.vcopy(lgtm[:, :], pTPb[:, :],
                                    waits=[("s_pe", F.c["s_pe"])])
                            F.vmax(m1t[:, :], lgtm[:, :])
                            F.ts(eq1t[:, :], lgtm[:, :], m1t[:, 0:1], ALU.is_ge)
                            F.ts(eq2t[:, :], eq1t[:, :], -1e30, ALU.mult)
                            F.tt(eq2t[:, :], lgtm[:, :], eq2t[:, :], ALU.add)
                            F.vmax(m2t[:, :], eq2t[:, :])
                            F.tt(zt[:, :], m2t[:, :], m1t[:, :], ALU.subtract,
                                 inc=("s_dve", 1))
                            F.act(zt[:, :], zt[:, :], AF.Exp,
                                  waits=[("s_dve", F.c["s_dve"])], inc=("s_act", 1))
                            F.ts(rt[:, :], zt[:, :], 1.0, ALU.add,
                                 waits=[("s_act", F.c["s_act"])])
                            F.vrecip(rt[:, :], rt[:, :])
                            F.ts(eq2t[:, :], lgtm[:, :], m2t[:, 0:1], ALU.is_ge)
                            F.tt(eq2t[:, :], eq2t[:, :], eq1t[:, :], ALU.subtract)
                            F.ts(wtmt[:, :], eq2t[:, :], zt[:, 0:1], ALU.mult)
                            F.tt(wtmt[:, :], wtmt[:, :], eq1t[:, :], ALU.add)
                            F.ts(wtmt[:, :], wtmt[:, :], rt[:, 0:1], ALU.mult,
                                 inc=("s_dve", 1))
                            wtm_done = F.c["s_dve"]
                            if debug and hf == 0 and tt_i == 0 and tb == 0:
                                F.dbg_tap("wtm0", dbg["wtm0"][:, :], wtmt[:, :],
                                       waits=[("s_dve", wtm_done)])
                            F.tp(pYF[0][0:NEXP, 0:128], wtmt[:, :],
                                 ident_f[:, :],
                                 waits=[("s_dve", wtm_done)], inc=("s_pe", 1))
                            F.vcopy(wT_sb[:, QT * tt_i + 128 * tb:
                                          QT * tt_i + 128 * (tb + 1)],
                                    pYF[0][0:NEXP, 0:128],
                                    waits=[("s_pe", F.c["s_pe"])], inc=("s_dve", 1))
                    wT_done = F.c["s_dve"]
                    load_weights(0)
                    silu_free = {0: F.c["s_act"], 1: F.c["s_act"]}
                    hh_free = {0: F.c["s_dve"], 1: F.c["s_dve"]}
                    accadd_free = {0: F.c["s_dve"], 1: F.c["s_dve"]}
                    for ex in range(NEXP):
                        w1_ready = ("s_w1", 16 * wgen["w1"])
                        w2_ready = ("s_w2", 16 * wgen["w2"])
                        w3_ready = ("s_w3", 16 * wgen["w3"])
                        for tt_i in range(NTT):
                            F.mm(pWB[:, :], sel8[:, ex, :],
                                 wT_sb[0:NEXP, QT * tt_i:QT * (tt_i + 1)],
                                 waits=[("s_dve", wT_done), ("s_cg", n_cg4),
                                        ("s_act", F.c["s_act"])],
                                 inc=("s_pe", 1))
                            F.acopy(w_sb[:, tt_i, :], pWB[:, :],
                                    waits=[("s_pe", F.c["s_pe"])], inc=("s_act", 1))
                        wsb_done = ("s_act", F.c["s_act"])
                        first_u = {0: True, 1: True}
                        for tt_i in range(NTT):
                            sl = slice(QT * tt_i, QT * (tt_i + 1))
                            for fc in range(NFC):
                                fr = fc % 2
                                uw = [("s_act", silu_free[fr])]
                                if first_u[fr]:
                                    uw += [w1_ready, w2_ready,
                                           ("s_dve", F.c["s_dve"])]
                                    first_u[fr] = False
                                for kc in range(NCH):
                                    F.mm(pU[fr][:, :],
                                         w1_sb[:, kc, 128 * fc:128 * (fc + 1)],
                                         xn3[:, kc, sl],
                                         start=(kc == 0), stop=(kc == NCH - 1),
                                         waits=uw if kc == 0 else ())
                                for kc in range(NCH):
                                    F.mm(pV[fr][:, :],
                                         w2_sb[:, kc, 128 * fc:128 * (fc + 1)],
                                         xn3[:, kc, sl],
                                         start=(kc == 0), stop=(kc == NCH - 1),
                                         waits=[("s_dve", hh_free[fr])]
                                         if kc == 0 else (),
                                         inc=("s_pe", 1) if kc == NCH - 1 else None)
                                uv_done = F.c["s_pe"]
                                F.act(su_sb[fr][:, :], pU[fr][:, :], AF.Silu,
                                      waits=[("s_pe", uv_done)], inc=("s_act", 1))
                                silu_free[fr] = F.c["s_act"]
                                F.tt(hh_sb[:, :], su_sb[fr][:, :], pV[fr][:, :],
                                     ALU.mult,
                                     waits=[("s_act", F.c["s_act"]), wsb_done])
                                F.tt(hw_sb[:, fc, :], hh_sb[:, :], w_sb[:, tt_i, :],
                                     ALU.mult, inc=("s_dve", 1))
                                hh_free[fr] = F.c["s_dve"]
                                if debug and hf == 0 and ex == 0 and tt_i == 0 and fc == 0:
                                    F.dbg_tap("hw0", dbg["hw0"][:, :], hw_sb[:, 0, :],
                                           waits=[("s_dve", F.c["s_dve"])])
                            h_done = F.c["s_dve"]
                            if tt_i == NTT - 1:
                                w_free["w1"] = F.c["s_pe"]
                                w_free["w2"] = F.c["s_pe"]
                            for oc in range(NCH):
                                yr = oc % 2
                                yw = [("s_dve", max(h_done, accadd_free[yr]))]
                                if oc == 0 and tt_i == 0:
                                    yw.append(w3_ready)
                                for fc in range(NFC):
                                    F.mm(pYF[yr][:, :],
                                         w3_sb[:, fc, 128 * oc:128 * (oc + 1)],
                                         hw_sb[:, fc, :],
                                         start=(fc == 0), stop=(fc == NFC - 1),
                                         waits=yw if fc == 0 else (),
                                         inc=("s_pe", 1) if fc == NFC - 1 else None)
                                F.tt(acc[:, oc, sl], acc[:, oc, sl], pYF[yr][:, :],
                                     ALU.add, waits=[("s_pe", F.c["s_pe"])],
                                     inc=("s_dve", 1))
                                accadd_free[yr] = F.c["s_dve"]
                        w_free["w3"] = F.c["s_pe"]
                        if ex + 1 < NEXP:
                            load_weights(ex + 1)
                    out_done = F.c["s_dve"]
                    for kc in range(NCH):
                        F.sdma(yf[128 * kc:128 * (kc + 1), HALF * hf:HALF * (hf + 1)],
                               acc[:, kc, :], waits=[("s_dve", out_done)],
                               inc=("s_out", 16))
                if debug:
                    for n in ["xn0", "krot0", "vext0", "attnU0", "attnb0",
                              "x20", "wtm0", "hw0"]:
                        F.wait("sync", "s_dbg_" + n, F.c["s_dbg_" + n])
                F.wait("sync", "s_out", F.c["s_out"])

            run_block(blk4, body3)

    return nc, dbg


def host_constants(T, QT):
    NB = QT // 128
    p = np.arange(128)
    d = p % HD
    i = d // 2
    om = 10000.0 ** (-(i.astype(np.float64)) / 32.0)
    t = np.arange(T, dtype=np.float64)
    ang = om[:, None] * t[None, :]
    ctab = np.cos(ang).astype(np.float32)
    sgn = np.where(d % 2 == 0, -1.0, 1.0)
    stab = (np.sin(ang) * sgn[:, None]).astype(np.float32)
    pswap = np.zeros((128, 128), np.float32)
    for m in range(128):
        pswap[m ^ 1, m] = 1.0
    s_loc = np.arange(128)
    q_loc = np.arange(QT)
    dmaskv = np.zeros((128, NB, QT), np.float32)
    for j in range(NB):
        iq = q_loc // 128
        p0 = (s_loc[:, None] < ((q_loc[None, :] % 128) // PAGE + 1) * PAGE)
        dmaskv[:, j, :] = np.where(iq[None, :] < j, 0.0,
                                   np.where(iq[None, :] == j, p0, 1.0))
    identf = np.eye(128, dtype=np.float32)
    sel16v = np.zeros((16, 16, 64), np.float32)
    for h in range(16):
        sel16v[h, h, :] = 1.0
    sel8v = np.zeros((8, 8, 128), np.float32)
    for ex in range(8):
        sel8v[ex, ex, :] = 1.0
    ehv = np.zeros((128, 16, 16), np.float32)
    for h in range(16):
        ehv[64, h, h] = 1.0
    return ctab, stab, pswap, dmaskv, identf, sel16v, sel8v, ehv


def make_in_maps(x, Wq, Wk, Wv, lat_k, lat_v, W_lk, W_lv, W_out, Wg, W1, W2, W3,
                 T, QT):
    ctab, stab, pswap, dmaskv, identf, sel16v, sel8v, ehv = host_constants(T, QT)
    shared = {
        "wq": np.ascontiguousarray(Wq, dtype=np.float32),
        "wk": np.ascontiguousarray(Wk, dtype=np.float32),
        "wv": np.ascontiguousarray(Wv, dtype=np.float32),
        "wo": np.ascontiguousarray(W_out, dtype=np.float32),
        "wlk": np.ascontiguousarray(W_lk, dtype=np.float32),
        "wlv": np.ascontiguousarray(W_lv, dtype=np.float32),
        "latk": np.ascontiguousarray(np.asarray(lat_k)[0].T, dtype=np.float32),
        "latv": np.ascontiguousarray(np.asarray(lat_v)[0].T, dtype=np.float32),
        "wgt": np.ascontiguousarray(Wg, dtype=np.float32),
        "w1": np.ascontiguousarray(W1, dtype=np.float32),
        "w2": np.ascontiguousarray(W2, dtype=np.float32),
        "w3": np.ascontiguousarray(W3, dtype=np.float32),
        "ctab": ctab, "stab": stab, "pswap": pswap, "dmask": dmaskv,
        "identf": identf, "sel16i": sel16v, "sel8i": sel8v, "ehi": ehv,
    }
    xfs = [np.ascontiguousarray(np.asarray(x)[b].T, dtype=np.float32)
           for b in range(B)]
    in_maps = []
    for c in range(8):
        b, q = c // 4, c % 4
        m = dict(shared)
        m["xf"] = xfs[b]
        m["qparam"] = np.array([[q]], np.int32)
        in_maps.append(m)
    return in_maps


def assemble(results, T, QT):
    out = np.empty((B, T, DM), np.float32)
    for c in range(8):
        b, q = c // 4, c % 4
        yfv = np.asarray(results[c]["yf"])
        for i, Q in enumerate(qset(q)):
            out[b, QT * Q:QT * (Q + 1), :] = yfv[:, QT * i:QT * (i + 1)].T
    return out


def kernel(L0_attn_x, L0_attn_Wq, L0_attn_Wk, L0_attn_Wv, L0_attn_lat_k,
           L0_attn_lat_v, L0_attn_W_lk, L0_attn_W_lv, L0_attn_W_out,
           L0_ffn_Wg, L0_ffn_W1, L0_ffn_W2, L0_ffn_W3):
    T, QT = 8192, 512
    in_maps = make_in_maps(L0_attn_x, L0_attn_Wq, L0_attn_Wk, L0_attn_Wv,
                           L0_attn_lat_k, L0_attn_lat_v, L0_attn_W_lk,
                           L0_attn_W_lv, L0_attn_W_out, L0_ffn_Wg, L0_ffn_W1,
                           L0_ffn_W2, L0_ffn_W3, T, QT)
    nc, _ = build_nc(T, QT, debug=False)
    res = run_bass_kernel_spmd(nc, in_maps, core_ids=list(range(8)))
    return assemble(res.results, T, QT)


# version 02
# speedup vs baseline: 1.2782x; 1.2782x over previous
from contextlib import ExitStack

import numpy as np
import ml_dtypes

import concourse.bass as bass
import concourse.mybir as mybir
from concourse.bass_utils import run_bass_kernel_spmd

F32 = mybir.dt.float32
BF16 = mybir.dt.bfloat16
I32 = mybir.dt.int32
AF = mybir.ActivationFunctionType
ALU = mybir.AluOpType
AX = mybir.AxisListType
bfloat16 = ml_dtypes.bfloat16

B = 2
DM = 1024
NCH = 8
HEADS = 16
HD = 64
NPAIR = 8
NLAT = 8
PAGE = 16
FFND = 2048
NEXP = 8
NFC = 16
EPS = 1e-5
SCALE = 0.125
SEGC = 8

ENGS = ("sync", "gpsimd", "tensor", "scalar", "vector")


def qset(q):
    return sorted([q, 7 - q, 8 + q, 15 - q])


class Flow:

    def __init__(self, nc, sems):
        self.nc = nc
        self.S = sems
        self.c = {k: 0 for k in sems}
        self.which = None
        self.eng = None
        self.qreg = None

    AUTO = {"vector": "s_dve", "scalar": "s_act", "gpsimd_c": "s_mk"}

    def _do(self, engname, fn, waits, inc):
        auto = self.AUTO.get(engname)
        if auto is not None:
            engname = "gpsimd" if engname == "gpsimd_c" else engname
            assert inc is None or inc[0] == auto
            waits = list(waits) + [(auto, self.c[auto])]
            inc = (auto, 1)
        for s, v in waits:
            assert v <= self.c[s], f"wait on future {s}:{v} (at {self.c[s]})"
        if self.which == engname:
            for s, v in waits:
                if v > 0:
                    self.eng.wait_ge(self.S[s], v)
            inst = fn(self.eng)
            if inc is not None:
                inst.then_inc(self.S[inc[0]], inc[1])
        if inc is not None:
            self.c[inc[0]] += inc[1]

    def mm(self, out, lhsT, rhs, start=True, stop=True, waits=(), inc=None):
        self._do("tensor", lambda e: e.matmul(out, lhsT, rhs, start=start, stop=stop),
                 waits, inc)

    def tp(self, out, in_, ident, waits=(), inc=None):
        self._do("tensor", lambda e: e.transpose(out, in_, ident), waits, inc)

    def act(self, out, in_, func, scale=1.0, bias=0.0, waits=(), inc=None):
        self._do("scalar", lambda e: e.activation(out, in_, func, bias=bias, scale=scale),
                 waits, inc)

    def acopy(self, out, in_, waits=(), inc=None):
        self._do("scalar", lambda e: e.copy(out, in_), waits, inc)

    def tt(self, out, a, b, op, waits=(), inc=None):
        self._do("vector", lambda e: e.tensor_tensor(out, a, b, op), waits, inc)

    def ts(self, out, a, s1, op, waits=(), inc=None):
        self._do("vector", lambda e: e.tensor_single_scalar(out, a, s1, op), waits, inc)

    def vcopy(self, out, in_, waits=(), inc=None):
        self._do("vector", lambda e: e.tensor_copy(out, in_), waits, inc)

    def vrecip(self, out, in_, waits=(), inc=None):
        self._do("vector", lambda e: e.reciprocal(out, in_), waits, inc)

    def vmax(self, out, in_, waits=(), inc=None):
        self._do("vector", lambda e: e.reduce_max(out, in_, AX.X), waits, inc)

    def gmemset(self, ap, val, waits=(), inc=None):
        self._do("gpsimd_c", lambda e: e.memset(ap, val), waits, inc)

    def sdma(self, out, in_, waits=(), inc=None):
        self._do("sync", lambda e: e.dma_start(out=out, in_=in_), waits, inc)

    def gdma(self, out, in_, waits=(), inc=None):
        self._do("gpsimd", lambda e: e.dma_start(out=out, in_=in_), waits, inc)

    def wait(self, engname, sem, val):
        assert val <= self.c[sem], f"wait on future {sem}:{val}"
        if self.which == engname and val > 0:
            self.eng.wait_ge(self.S[sem], val)

    def dbg_tap(self, name, dram, src, waits=()):
        sem = "s_dbg_" + name
        self.sdma(dram, src, waits=waits, inc=(sem, 16))
        self.wait("vector", sem, self.c[sem])
        self.wait("sync", sem, self.c[sem])

    def if_q(self, k):
        return self.eng.If_eq(self.qreg, k)

    def variants(self, fn):
        base = dict(self.c)
        ends = []
        for k in range(4):
            self.c = dict(base)
            with self.if_q(k):
                fn(k)
            ends.append(dict(self.c))
        for o in ends[1:]:
            assert o == ends[0], "variant counter mismatch"
        self.c = ends[0]


def build_nc(T=8192, QT=512, debug=False):
    NQT = T // QT
    assert NQT == 16
    NB = QT // 128
    OWN = 4 * QT
    HALF = OWN // 2
    NTT = HALF // QT

    nc = bass.Bass("TRN2", target_bir_lowering=False)

    xf = nc.dram_tensor("xf", [DM, T], F32, kind="ExternalInput").ap()
    wq = nc.dram_tensor("wq", [DM, DM], F32, kind="ExternalInput").ap()
    wk = nc.dram_tensor("wk", [DM, DM], F32, kind="ExternalInput").ap()
    wv = nc.dram_tensor("wv", [DM, DM], F32, kind="ExternalInput").ap()
    wo = nc.dram_tensor("wo", [DM, DM], F32, kind="ExternalInput").ap()
    wlk = nc.dram_tensor("wlk", [HD, DM], F32, kind="ExternalInput").ap()
    wlv = nc.dram_tensor("wlv", [HD, DM], F32, kind="ExternalInput").ap()
    latk = nc.dram_tensor("latk", [HD, NLAT], F32, kind="ExternalInput").ap()
    latv = nc.dram_tensor("latv", [HD, NLAT], F32, kind="ExternalInput").ap()
    wgt = nc.dram_tensor("wgt", [DM, NEXP], F32, kind="ExternalInput").ap()
    w1 = nc.dram_tensor("w1", [NEXP, DM, FFND], F32, kind="ExternalInput").ap()
    w2 = nc.dram_tensor("w2", [NEXP, DM, FFND], F32, kind="ExternalInput").ap()
    w3 = nc.dram_tensor("w3", [NEXP, FFND, DM], F32, kind="ExternalInput").ap()
    ctab = nc.dram_tensor("ctab", [128, T], F32, kind="ExternalInput").ap()
    stab = nc.dram_tensor("stab", [128, T], F32, kind="ExternalInput").ap()
    pswap = nc.dram_tensor("pswap", [128, 128], F32, kind="ExternalInput").ap()
    dmask = nc.dram_tensor("dmask", [128, NB, QT], F32, kind="ExternalInput").ap()
    identf = nc.dram_tensor("identf", [128, 128], F32, kind="ExternalInput").ap()
    sel16i = nc.dram_tensor("sel16i", [16, 16, 64], F32, kind="ExternalInput").ap()
    sel8i = nc.dram_tensor("sel8i", [NEXP, NEXP, 128], F32, kind="ExternalInput").ap()
    ehi = nc.dram_tensor("ehi", [128, 16, 16], F32, kind="ExternalInput").ap()
    qparam = nc.dram_tensor("qparam", [1, 1], I32, kind="ExternalInput").ap()

    yf = nc.dram_tensor("yf", [DM, OWN], F32, kind="ExternalOutput").ap()

    k_scr = nc.dram_tensor("k_scr", [DM, T], BF16).ap()
    v_scr = nc.dram_tensor("v_scr", [T, 1040], BF16).ap()
    q_scr = nc.dram_tensor("q_scr", [DM, OWN], BF16).ap()
    x2_scr = nc.dram_tensor("x2_scr", [DM, OWN], F32).ap()
    w1b_scr = nc.dram_tensor("w1b_scr", [NEXP, DM, FFND], BF16).ap()
    w2b_scr = nc.dram_tensor("w2b_scr", [NEXP, DM, FFND], BF16).ap()

    dbg = {}
    if debug:
        def mkdbg(name, shape, dt=F32):
            dbg[name] = nc.dram_tensor("dbg_" + name, shape, dt, kind="ExternalOutput").ap()
        mkdbg("xn0", [128, NCH, QT], BF16)
        mkdbg("krot0", [128, QT], BF16)
        mkdbg("vext0", [128, 1040], BF16)
        mkdbg("attnU0", [65, QT])
        mkdbg("attnb0", [64, HEADS, QT], BF16)
        mkdbg("x20", [128, NCH, QT])
        mkdbg("wtm0", [128, NEXP])
        mkdbg("hw0", [128, QT], BF16)

    sem_names = [
        "s_c", "s_cg", "s_in0", "s_in1", "s_qin0", "s_qin1",
        "s_xg0", "s_xg1", "s_qg0", "s_qg1",
        "s_kv0", "s_kv1", "s_q0", "s_q1", "s_xr",
        "s_sk0", "s_sk1", "s_sv0", "s_sv1", "s_sqr0", "s_sqr1",
        "s_stx", "s_w1", "s_w2", "s_w3", "s_x2l", "s_wc",
        "s_out", "s_pe", "s_act", "s_dve", "s_mk",
    ]
    if debug:
        sem_names += ["s_dbg_" + n for n in
                      ["xn0", "krot0", "vext0", "attnU0", "attnb0", "x20",
                       "wtm0", "hw0"]]

    blk_tag = [0]

    with ExitStack() as top:
        sems = {n: top.enter_context(nc.semaphore(n)) for n in sem_names}
        F = Flow(nc, sems)

        def run_block(blk, body):
            res = {}
            c0 = dict(F.c)
            blk_tag[0] += 1
            tag = blk_tag[0]

            def section(name):
                def _(eng):
                    F.which, F.eng = name, eng
                    F.c = dict(c0)
                    with ExitStack() as rs:
                        F.qreg = rs.enter_context(eng.register(f"qr{tag}_{name}"))
                        if tag == 1 and name == "sync":
                            eng.dma_start(out=qsb[:, :], in_=qparam[:, :]).then_inc(
                                sems["s_c"], 16)
                        eng.wait_ge(sems["s_c"], 16)
                        eng.reg_load(F.qreg, qsb[0:1, 0:1])
                        if tag > 1:
                            for sem, v in c0.items():
                                if v > 0:
                                    eng.wait_ge(sems[sem], v)
                        body()
                    res[name] = dict(F.c)
                return _

            for name in ENGS:
                getattr(blk, name)(section(name))
            vals = list(res.values())
            for v in vals[1:]:
                assert v == vals[0], f"engine counter mismatch"
            F.c = vals[0]
            F.which = F.eng = None

        e = top.enter_context
        qsb = e(nc.sbuf_tensor("qsb", [1, 1], I32))
        ones_bf = e(nc.sbuf_tensor("ones_bf", [128, 1], BF16))
        ones1f = e(nc.sbuf_tensor("ones1f", [1, 128], F32))
        eps_sb = e(nc.sbuf_tensor("eps_sb", [1, 1], F32))
        ident_b = e(nc.sbuf_tensor("ident_b", [128, 128], BF16))
        ident_f = e(nc.sbuf_tensor("ident_f", [128, 128], F32))
        lk_sb = e(nc.sbuf_tensor("lk_sb", [128, NPAIR, NLAT], BF16))
        lv_ext = e(nc.sbuf_tensor("lv_ext", [NLAT, 1040], BF16))
        sel16 = e(nc.sbuf_tensor("sel16", [16, 16, 64], BF16))
        eh_sb = e(nc.sbuf_tensor("eh_sb", [128, 16, 16], F32))
        pswap_sb = e(nc.sbuf_tensor("pswap_sb", [128, 128], BF16))

        def make_ln(ln, QTW):
            def emit_ln(xb_chunks, xn_out_chunks, entry_waits):
                guard = [("s_dve", F.c["s_dve"]), ("s_act", F.c["s_act"])]
                F.act(ln["sq"][:, :, :], xb_chunks(None), AF.Square,
                      waits=list(entry_waits) + [("s_pe", F.c["s_pe"])],
                      inc=("s_act", 1))
                sq_done = F.c["s_act"]
                for kc in range(NCH):
                    F.mm(ln["pS1"][0:1, 0:QTW], ones_bf[:, 0:1], xb_chunks(kc),
                         start=(kc == 0), stop=(kc == NCH - 1),
                         waits=(list(entry_waits) + guard) if kc == 0 else ())
                for kc in range(NCH):
                    F.mm(ln["pS2"][0:1, 0:QTW], ones_bf[:, 0:1], ln["sq"][:, kc, :],
                         start=(kc == 0), stop=(kc == NCH - 1),
                         waits=[("s_act", sq_done)] if kc == 0 else (),
                         inc=("s_pe", 1) if kc == NCH - 1 else None)
                F.ts(ln["mean"][:, :], ln["pS1"][0:1, 0:QTW], 1.0 / DM, ALU.mult,
                     waits=[("s_pe", F.c["s_pe"])])
                F.ts(ln["tsc"][:, :], ln["pS2"][0:1, 0:QTW], 1.0 / DM, ALU.mult)
                F.tt(ln["var"][:, :], ln["mean"][:, :], ln["mean"][:, :], ALU.mult)
                F.tt(ln["var"][:, :], ln["tsc"][:, :], ln["var"][:, :], ALU.subtract,
                     inc=("s_dve", 1))
                F.act(ln["tsc"][:, :], ln["var"][:, :], AF.Ln, bias=eps_sb[0:1, 0:1],
                      waits=[("s_dve", F.c["s_dve"])])
                F.act(ln["rstd"][:, :], ln["tsc"][:, :], AF.Exp, scale=-0.5,
                      inc=("s_act", 1))
                F.mm(ln["pB1"][:, 0:QTW], ones1f[0:1, :], ln["mean"][0:1, :],
                     waits=[("s_act", F.c["s_act"]), ("s_dve", F.c["s_dve"])])
                F.mm(ln["pB2"][:, 0:QTW], ones1f[0:1, :], ln["rstd"][0:1, :],
                     inc=("s_pe", 1))
                bc = F.c["s_pe"]
                for kc in range(NCH):
                    F.tt(ln["tmp"][:, :], xb_chunks(kc), ln["pB1"][:, 0:QTW],
                         ALU.subtract, waits=[("s_pe", bc)] if kc == 0 else ())
                    F.tt(xn_out_chunks(kc), ln["tmp"][:, :], ln["pB2"][:, 0:QTW],
                         ALU.mult, inc=("s_dve", 1) if kc == NCH - 1 else None)
                return F.c["s_dve"]
            return emit_ln

        with ExitStack() as p1s:
            e1 = p1s.enter_context
            wk_sb = e1(nc.sbuf_tensor("wk_sb", [128, NCH, DM], BF16))
            wv_sb = e1(nc.sbuf_tensor("wv_sb", [128, NCH, DM], BF16))
            wlk_sb = e1(nc.sbuf_tensor("wlk_sb", [HD, DM], BF16))
            wlv_sb = e1(nc.sbuf_tensor("wlv_sb", [HD, DM], BF16))
            latk_sb = e1(nc.sbuf_tensor("latk_sb", [HD, NLAT], BF16))
            latv_sb = e1(nc.sbuf_tensor("latv_sb", [HD, NLAT], BF16))
            xbuf = [e1(nc.sbuf_tensor(f"xbuf{r}", [128, NCH, QT], F32)) for r in range(2)]
            xb_bf = [e1(nc.sbuf_tensor(f"xb_bf{r}", [128, NCH, QT], BF16)) for r in range(2)]
            ctb1 = [e1(nc.sbuf_tensor(f"ctb1{r}", [128, QT], F32)) for r in range(2)]
            stb1 = [e1(nc.sbuf_tensor(f"stb1{r}", [128, QT], F32)) for r in range(2)]
            xn1 = [e1(nc.sbuf_tensor(f"xn1{r}", [128, NCH, QT], BF16)) for r in range(2)]
            ln1 = {
                "sq": e1(nc.sbuf_tensor("sq1", [128, NCH, QT], BF16)),
                "mean": e1(nc.sbuf_tensor("mean1", [1, QT], F32)),
                "var": e1(nc.sbuf_tensor("var1", [1, QT], F32)),
                "tsc": e1(nc.sbuf_tensor("tsc1", [1, QT], F32)),
                "rstd": e1(nc.sbuf_tensor("rstd1", [1, QT], F32)),
                "tmp": e1(nc.sbuf_tensor("tmp1", [128, QT], F32)),
            }
            kbf = [e1(nc.sbuf_tensor(f"kbf{r}", [128, QT], BF16)) for r in range(2)]
            t1_sb = e1(nc.sbuf_tensor("t1_sb", [128, QT], F32))
            t2_sb = e1(nc.sbuf_tensor("t2_sb", [128, QT], F32))
            krot = [e1(nc.sbuf_tensor(f"krot{r}", [128, QT], BF16)) for r in range(2)]
            vext = [e1(nc.sbuf_tensor(f"vext{r}", [128, 1040], BF16)) for r in range(2)]

            pLN1 = e1(nc.psum_tensor("pLN1", [128, 512], F32))
            pLN2 = e1(nc.psum_tensor("pLN2", [128, 512], F32))
            pB1 = e1(nc.psum_tensor("pB1", [128, 512], F32))
            pB2 = e1(nc.psum_tensor("pB2", [128, 512], F32))
            pK = [e1(nc.psum_tensor(f"pK{r}", [128, 512], F32)) for r in range(2)]
            pW = [e1(nc.psum_tensor(f"pW{r}", [128, 512], F32)) for r in range(2)]
            ln1.update(pS1=pLN1, pS2=pLN2, pB1=pB1, pB2=pB2)

            blk1 = e1(nc.Block())
            state = {}

            def body01():
                F.c["s_c"] = max(F.c["s_c"], 16)
                F.sdma(ident_f[:, :], identf[:, :], inc=("s_c", 16))
                F.gdma(wk_sb[:, :, :], wk.rearrange("(c p) n -> p c n", p=128),
                       inc=("s_cg", 16))
                F.gdma(wv_sb[:, :, :], wv.rearrange("(c p) n -> p c n", p=128),
                       inc=("s_cg", 16))
                for dst, src in ((wlk_sb, wlk), (wlv_sb, wlv), (latk_sb, latk),
                                 (latv_sb, latv), (pswap_sb, pswap), (ident_b, identf)):
                    F.gdma(dst[:, :], src[:, :], inc=("s_cg", 16))
                F.gdma(sel16[:, :, :], sel16i[:, :, :], inc=("s_cg", 16))
                F.sdma(eh_sb[:, :, :], ehi[:, :, :], inc=("s_c", 16))
                n_cg = F.c["s_cg"]
                n_c = F.c["s_c"]
                F.gmemset(ones_bf[:, :], 1.0)
                F.gmemset(ones1f[:, :], 1.0)
                F.gmemset(eps_sb[:, :], EPS)
                for r in range(2):
                    F.gmemset(vext[r][:, :], 0.0)
                F.gmemset(lv_ext[:, :], 0.0, inc=("s_mk", 1))
                zdone = F.c["s_mk"]
                first = True
                for r in range(2):
                    F.gmemset(vext[r][:, :].rearrange("p (g x) -> p g x", x=65)[:, :, 64:65],
                              1.0, waits=[("s_mk", zdone)] if first else ())
                    first = False
                F.gmemset(lv_ext[:, :].rearrange("p (g x) -> p g x", x=65)[:, :, 64:65],
                          1.0, inc=("s_mk", 1))
                ms_done = ("s_mk", F.c["s_mk"])

                for pc in range(NPAIR):
                    F.mm(pK[0][:, 0:NLAT], wlk_sb[0:HD, 128 * pc:128 * (pc + 1)],
                         latk_sb[0:HD, :],
                         waits=[("s_cg", n_cg), ms_done, ("s_act", F.c["s_act"])],
                         inc=("s_pe", 1))
                    F.acopy(lk_sb[:, pc, :], pK[0][:, 0:NLAT],
                            waits=[("s_pe", F.c["s_pe"])], inc=("s_act", 1))
                F.mm(pK[1][0:NLAT, 0:512], latv_sb[0:HD, :], wlv_sb[0:HD, 0:512],
                     waits=[("s_cg", n_cg), ms_done])
                F.mm(pW[0][0:NLAT, 0:512], latv_sb[0:HD, :], wlv_sb[0:HD, 512:1024],
                     inc=("s_pe", 1))
                lvp = F.c["s_pe"]
                F.vcopy(lv_ext[:, 0:520].rearrange("p (g x) -> p g x", x=65)[:, :, 0:64],
                        pK[1][0:NLAT, 0:512].rearrange("p (g x) -> p g x", x=64),
                        waits=[("s_pe", lvp), ms_done])
                F.vcopy(lv_ext[:, 520:1040].rearrange("p (g x) -> p g x", x=65)[:, :, 0:64],
                        pW[0][0:NLAT, 0:512].rearrange("p (g x) -> p g x", x=64),
                        inc=("s_dve", 1))
                lat_act = F.c["s_act"]
                lat_dve = F.c["s_dve"]

                emit_ln = make_ln(ln1, QT)
                xfr = xf.rearrange("(c p) t -> p c t", p=128)
                kscr_r = k_scr.rearrange("(c p) t -> p c t", p=128)
                in_free = {0: 0, 1: 0}
                vext_free = {0: 0, 1: 0}
                krot_free = {0: 0, 1: 0}
                prs = {"pP_free": {0: lat_dve, 1: lat_dve},
                       "pW_free": {0: lat_dve, 1: lat_dve}}
                for qt in range(NQT):
                    r = qt % 2
                    gen = qt // 2 + 1
                    sname = f"s_in{r}"
                    F.sdma(xbuf[r][:, :, :], xfr[:, :, QT * qt:QT * (qt + 1)],
                           waits=[("s_dve", in_free[r])], inc=(sname, 16))
                    F.sdma(ctb1[r][:, :], ctab[:, QT * qt:QT * (qt + 1)],
                           waits=[("s_dve", in_free[r])], inc=(sname, 16))
                    F.sdma(stb1[r][:, :], stab[:, QT * qt:QT * (qt + 1)],
                           inc=(sname, 16))
                    entry0 = [(sname, 48 * gen)]
                    if qt == 0:
                        entry0 += [("s_cg", n_cg), ("s_c", n_c),
                                   ("s_dve", F.c["s_dve"])]
                    F.vcopy(xb_bf[r][:, :, :], xbuf[r][:, :, :], waits=entry0)
                    entry = [("s_dve", F.c["s_dve"])]
                    xn_done = emit_ln(
                        lambda kc, _r=r: xb_bf[_r][:, :, :] if kc is None
                        else xb_bf[_r][:, kc, :],
                        lambda kc, _r=r: xn1[_r][:, kc, :], entry)
                    if qt < NEXP:
                        F.gdma(w1b_scr[qt, :, :], w1[qt, :, :], inc=("s_wc", 16))
                    else:
                        F.gdma(w2b_scr[qt - NEXP, :, :], w2[qt - NEXP, :, :],
                               inc=("s_wc", 16))

                    for pc in range(NPAIR):
                        kr = pc % 2
                        kw = [("s_dve", max(xn_done, prs["pP_free"][kr]))]
                        if qt == 0 and pc < 2:
                            kw.append(("s_act", lat_act))
                        for kc in range(NCH):
                            F.mm(pK[kr][:, 0:QT], wk_sb[:, kc, 128 * pc:128 * (pc + 1)],
                                 xn1[r][:, kc, :],
                                 start=(kc == 0), stop=(kc == NCH - 1),
                                 waits=kw if kc == 0 else (),
                                 inc=("s_pe", 1) if kc == NCH - 1 else None)
                        F.acopy(kbf[kr][:, :], pK[kr][:, 0:QT],
                                waits=[("s_pe", F.c["s_pe"])], inc=("s_act", 1))
                        F.mm(pW[kr][:, 0:QT], pswap_sb[:, :], kbf[kr][:, :],
                             waits=[("s_act", F.c["s_act"]),
                                    ("s_dve", prs["pW_free"][kr])], inc=("s_pe", 1))
                        F.tt(t1_sb[:, :], pK[kr][:, 0:QT], ctb1[r][:, :], ALU.mult,
                             waits=[("s_pe", F.c["s_pe"])])
                        F.tt(t2_sb[:, :], pW[kr][:, 0:QT], stb1[r][:, :], ALU.mult)
                        F.tt(krot[kr][:, :], t1_sb[:, :], t2_sb[:, :], ALU.add,
                             waits=[(f"s_sk{kr}", krot_free[kr])])
                        prs["pP_free"][kr] = F.c["s_dve"]
                        prs["pW_free"][kr] = F.c["s_dve"]
                        F.sdma(kscr_r[:, pc, QT * qt:QT * (qt + 1)], krot[kr][:, :],
                               waits=[("s_dve", F.c["s_dve"])], inc=(f"s_sk{kr}", 16))
                        krot_free[kr] = F.c[f"s_sk{kr}"]
                        if debug and qt == 0 and pc == 0:
                            F.dbg_tap("krot0", dbg["krot0"][:, :], krot[kr][:, :],
                                   waits=[("s_dve", F.c["s_dve"])])
                    for tb in range(NB):
                        vr = tb % 2
                        for vt in range(2):
                            pv = pLN1 if vt == 0 else pLN2
                            for kc in range(NCH):
                                F.mm(pv[:, 0:512],
                                     xn1[r][:, kc, 128 * tb:128 * (tb + 1)],
                                     wv_sb[:, kc, 512 * vt:512 * (vt + 1)],
                                     start=(kc == 0), stop=(kc == NCH - 1),
                                     waits=[("s_dve", F.c["s_dve"]),
                                            (f"s_sv{vr}", vext_free[vr])]
                                     if kc == 0 else (),
                                     inc=("s_pe", 1) if kc == NCH - 1 else None)
                            F.vcopy(
                                vext[vr][:, 520 * vt:520 * (vt + 1)]
                                .rearrange("p (g x) -> p g x", x=65)[:, :, 0:64],
                                pv[:, 0:512].rearrange("p (g x) -> p g x", x=64),
                                waits=[("s_pe", F.c["s_pe"])],
                                inc=("s_dve", 1) if vt == 1 else None)
                        F.sdma(v_scr[QT * qt + 128 * tb:QT * qt + 128 * (tb + 1), :],
                               vext[vr][:, :],
                               waits=[("s_dve", F.c["s_dve"])], inc=(f"s_sv{vr}", 16))
                        vext_free[vr] = F.c[f"s_sv{vr}"]
                        if debug and qt == 0 and tb == 0:
                            F.dbg_tap("vext0", dbg["vext0"][:, :], vext[vr][:, :],
                                   waits=[("s_dve", F.c["s_dve"])])
                    in_free[r] = F.c["s_dve"]
                    if debug and qt == 0:
                        F.dbg_tap("xn0", dbg["xn0"][:, :, :], xn1[r][:, :, :],
                               waits=[("s_dve", xn_done)])

            run_block(blk1, body01)
            p1_st = [("s_sk0", F.c["s_sk0"]), ("s_sk1", F.c["s_sk1"]),
                     ("s_sv0", F.c["s_sv0"]), ("s_sv1", F.c["s_sv1"])]

        with ExitStack() as p2a:
            e2 = p2a.enter_context
            wq_sb = e2(nc.sbuf_tensor("wq_sb", [128, NCH, DM], BF16))
            xqb = [e2(nc.sbuf_tensor(f"xqb{r}", [128, NCH, QT], F32)) for r in range(2)]
            xq_bf = [e2(nc.sbuf_tensor(f"xq_bf{r}", [128, NCH, QT], BF16)) for r in range(2)]
            ctb2 = [e2(nc.sbuf_tensor(f"ctb2{r}", [128, QT], F32)) for r in range(2)]
            stb2 = [e2(nc.sbuf_tensor(f"stb2{r}", [128, QT], F32)) for r in range(2)]
            xn2t = e2(nc.sbuf_tensor("xn2t", [128, NCH, QT], BF16))
            ln2 = {
                "sq": e2(nc.sbuf_tensor("sq2", [128, NCH, QT], BF16)),
                "mean": e2(nc.sbuf_tensor("mean2", [1, QT], F32)),
                "var": e2(nc.sbuf_tensor("var2", [1, QT], F32)),
                "tsc": e2(nc.sbuf_tensor("tsc2", [1, QT], F32)),
                "rstd": e2(nc.sbuf_tensor("rstd2", [1, QT], F32)),
                "tmp": e2(nc.sbuf_tensor("tmp2", [128, QT], F32)),
            }
            qbf = [e2(nc.sbuf_tensor(f"qbf{r}", [128, QT], BF16)) for r in range(2)]
            tq1 = e2(nc.sbuf_tensor("tq1", [128, QT], F32))
            tq2 = e2(nc.sbuf_tensor("tq2", [128, QT], F32))
            qrot = [e2(nc.sbuf_tensor(f"qrot{r}", [128, QT], BF16)) for r in range(2)]

            pLQ1 = e2(nc.psum_tensor("pLQ1", [128, QT], F32))
            pLQ2 = e2(nc.psum_tensor("pLQ2", [128, QT], F32))
            pBQ1 = e2(nc.psum_tensor("pBQ1", [128, QT], F32))
            pBQ2 = e2(nc.psum_tensor("pBQ2", [128, QT], F32))
            pKQ = [e2(nc.psum_tensor(f"pKQ{r}", [128, QT], F32)) for r in range(2)]
            pWQ = [e2(nc.psum_tensor(f"pWQ{r}", [128, QT], F32)) for r in range(2)]
            ln2.update(pS1=pLQ1, pS2=pLQ2, pB1=pBQ1, pB2=pBQ2)

            blk2 = e2(nc.Block())

            def body2a():
                F.gdma(wq_sb[:, :, :], wq.rearrange("(c p) n -> p c n", p=128),
                       inc=("s_cg", 16))
                n_cg2 = F.c["s_cg"]
                emit_ln = make_ln(ln2, QT)
                xfr = xf.rearrange("(c p) t -> p c t", p=128)
                qscr_r = q_scr.rearrange("(c p) t -> p c t", p=128)

                def var_body(vq):
                    qs = qset(vq)
                    in_free = {0: 0, 1: 0}
                    qrot_free = {0: 0, 1: 0}
                    prs = {"pP_free": {0: 0, 1: 0}, "pW_free": {0: 0, 1: 0}}
                    for i, Q in enumerate(qs):
                        r = i % 2
                        gen = i // 2 + 1
                        sname = f"s_qin{r}"
                        F.sdma(xqb[r][:, :, :], xfr[:, :, QT * Q:QT * (Q + 1)],
                               waits=[("s_dve", in_free[r])], inc=(sname, 16))
                        F.sdma(ctb2[r][:, :], ctab[:, QT * Q:QT * (Q + 1)],
                               waits=[("s_dve", in_free[r])], inc=(sname, 16))
                        F.sdma(stb2[r][:, :], stab[:, QT * Q:QT * (Q + 1)],
                               inc=(sname, 16))
                        entry0 = [(sname, 48 * gen)]
                        if i == 0:
                            entry0 += [("s_cg", n_cg2)]
                        F.vcopy(xq_bf[r][:, :, :], xqb[r][:, :, :], waits=entry0)
                        entry = [("s_dve", F.c["s_dve"])]
                        xn_done = emit_ln(
                            lambda kc, _r=r: xq_bf[_r][:, :, :] if kc is None
                            else xq_bf[_r][:, kc, :],
                            lambda kc: xn2t[:, kc, :], entry)
                        for pc in range(NPAIR):
                            kr = pc % 2
                            for kc in range(NCH):
                                F.mm(pKQ[kr][:, :], wq_sb[:, kc, 128 * pc:128 * (pc + 1)],
                                     xn2t[:, kc, :],
                                     start=(kc == 0), stop=(kc == NCH - 1),
                                     waits=[("s_dve", max(xn_done, prs["pP_free"][kr]))]
                                     if kc == 0 else (),
                                     inc=("s_pe", 1) if kc == NCH - 1 else None)
                            F.acopy(qbf[kr][:, :], pKQ[kr][:, :],
                                    waits=[("s_pe", F.c["s_pe"])], inc=("s_act", 1))
                            F.mm(pWQ[kr][:, :], pswap_sb[:, :], qbf[kr][:, :],
                                 waits=[("s_act", F.c["s_act"]),
                                        ("s_dve", prs["pW_free"][kr])], inc=("s_pe", 1))
                            F.tt(tq1[:, :], pKQ[kr][:, :], ctb2[r][:, :], ALU.mult,
                                 waits=[("s_pe", F.c["s_pe"])])
                            F.tt(tq2[:, :], pWQ[kr][:, :], stb2[r][:, :], ALU.mult)
                            F.tt(qrot[kr][:, :], tq1[:, :], tq2[:, :], ALU.add,
                                 waits=[(f"s_sqr{kr}", qrot_free[kr])])
                            prs["pP_free"][kr] = F.c["s_dve"]
                            prs["pW_free"][kr] = F.c["s_dve"]
                            F.sdma(qscr_r[:, pc, QT * i:QT * (i + 1)], qrot[kr][:, :],
                                   waits=[("s_dve", F.c["s_dve"])],
                                   inc=(f"s_sqr{kr}", 16))
                            qrot_free[kr] = F.c[f"s_sqr{kr}"]
                        in_free[r] = F.c["s_dve"]

                F.variants(var_body)

            run_block(blk2, body2a)
            q_st = [("s_sqr0", F.c["s_sqr0"]), ("s_sqr1", F.c["s_sqr1"])]

        with ExitStack() as p2b:
            e3 = p2b.enter_context
            wo_sb = e3(nc.sbuf_tensor("wo_sb", [64, HEADS, DM], BF16))
            dmask_sb = e3(nc.sbuf_tensor("dmask_sb", [128, NB, QT], BF16))
            qsl = [e3(nc.sbuf_tensor(f"qsl{r}", [128, NCH, QT], BF16)) for r in range(2)]
            kseg = [e3(nc.sbuf_tensor(f"kseg{r}", [128, NCH, SEGC * 128], BF16))
                    for r in range(2)]
            vseg = [e3(nc.sbuf_tensor(f"vseg{r}", [128, SEGC, 1040], BF16))
                    for r in range(2)]
            probs = [e3(nc.sbuf_tensor(f"probs{r}", [128, 2 * QT], BF16))
                     for r in range(2)]
            attnU = e3(nc.sbuf_tensor("attnU", [65, HEADS, QT], F32))
            attnb = e3(nc.sbuf_tensor("attnb", [64, HEADS, QT], BF16))
            recip_t = e3(nc.sbuf_tensor("recip_t", [16, QT], BF16))
            recip_f = e3(nc.sbuf_tensor("recip_f", [16, QT], F32))
            xres = e3(nc.sbuf_tensor("xres", [128, NCH, QT], F32))

            pS = [e3(nc.psum_tensor(f"pS{r}", [128, 2 * QT], F32)) for r in range(2)]
            pAV = [e3(nc.psum_tensor(f"pAV{hh}", [128, QT], F32)) for hh in range(2)]
            pDN = e3(nc.psum_tensor("pDN", [64, QT], F32))
            pRC = e3(nc.psum_tensor("pRC", [64, QT], F32))

            blk3 = e3(nc.Block())

            def body2b():
                F.gdma(wo_sb[0:64, :, :], wo.rearrange("(h d) n -> d h n", d=64),
                       inc=("s_cg", 16))
                F.gdma(dmask_sb[:, :, :], dmask[:, :, :], inc=("s_cg", 16))
                n_cg3 = F.c["s_cg"]
                xfr = xf.rearrange("(c p) t -> p c t", p=128)
                qscr_r = q_scr.rearrange("(c p) t -> p c t", p=128)
                kscr_r = k_scr.rearrange("(c p) t -> p c t", p=128)
                vscr_r = v_scr.rearrange("(n p) c -> p n c", p=128)
                x2r = x2_scr.rearrange("(c p) t -> p c t", p=128)

                def var_body(vq):
                    qs = qset(vq)
                    qsl_free = {0: 0, 1: 0}
                    kv_free = {0: 0, 1: 0}
                    kv_gen = {0: 0, 1: 0}
                    slot_exp = {0: 0, 1: 0}
                    xres_free = 0
                    stx_free = 0
                    ps_guard = 0
                    av_evac = 0
                    unit = 0
                    for i, Q in enumerate(qs):
                        qr = i % 2
                        F.sdma(qsl[qr][:, :, :], qscr_r[:, :, QT * i:QT * (i + 1)],
                               waits=[("s_pe", qsl_free[qr])] + q_st,
                               inc=(f"s_q{qr}", 16))
                        q_ready = (f"s_q{qr}", 16 * (i // 2 + 1))
                        F.sdma(xres[:, :, :], xfr[:, :, QT * Q:QT * (Q + 1)],
                               waits=[("s_dve", xres_free), ("s_stx", stx_free)],
                               inc=("s_xr", 16))
                        xres_ready = ("s_xr", F.c["s_xr"])

                        extent = NB * (Q + 1)
                        nseg = (extent + SEGC - 1) // SEGC

                        seg_rdy = {}

                        def load_seg(g):
                            sl = g % 2
                            c0, c1 = SEGC * g, min(SEGC * (g + 1), extent)
                            F.sdma(kseg[sl][:, :, 0:128 * (c1 - c0)],
                                   kscr_r[:, :, 128 * c0:128 * c1],
                                   waits=[("s_pe", kv_free[sl])] + p1_st,
                                   inc=(f"s_kv{sl}", 16))
                            F.sdma(vseg[sl][:, 0:c1 - c0, :], vscr_r[:, c0:c1, :],
                                   inc=(f"s_kv{sl}", 16))
                            kv_gen[sl] += 1
                            seg_rdy[g] = (f"s_kv{sl}", 32 * kv_gen[sl])

                        units = []
                        for g in range(nseg):
                            c0, c1 = SEGC * g, min(SEGC * (g + 1), extent)
                            for pr in range(NPAIR):
                                if g == 0:
                                    units.append(("lat", g, pr, -1, True, False))
                                for sc in range(c0, c1):
                                    units.append(("sc", g, pr, sc,
                                                  g > 0 and sc == c0,
                                                  sc == c1 - 1))

                        load_seg(0)
                        pend = None
                        first_of_qtile = True

                        def emit_av(p):
                            nonlocal av_evac
                            (ur, kind, g, pr, sc, gstart, gstop, avw) = p
                            sl = g % 2
                            if gstart:
                                avw = avw + [("s_dve", av_evac)]
                            for hh in range(2):
                                if kind == "lat":
                                    lhsT = lv_ext[:, 130 * pr + 65 * hh:
                                                  130 * pr + 65 * (hh + 1)]
                                    rhs = probs[ur][0:NLAT, QT * hh:QT * (hh + 1)]
                                else:
                                    lhsT = vseg[sl][:, sc - SEGC * g,
                                                    130 * pr + 65 * hh:
                                                    130 * pr + 65 * (hh + 1)]
                                    rhs = probs[ur][:, QT * hh:QT * (hh + 1)]
                                F.mm(pAV[hh][0:65, :], lhsT, rhs,
                                     start=gstart, stop=gstop,
                                     waits=avw if hh == 0 else (),
                                     inc=("s_pe", 1) if hh == 1 else None)
                            if gstop:
                                kv_free[g % 2] = F.c["s_pe"]
                                av_done = F.c["s_pe"]
                                for hh in range(2):
                                    h = 2 * pr + hh
                                    if g == 0:
                                        F.vcopy(attnU[:, h, :], pAV[hh][0:65, :],
                                                waits=[("s_pe", av_done)]
                                                if hh == 0 else ())
                                    else:
                                        F.tt(attnU[:, h, :], attnU[:, h, :],
                                             pAV[hh][0:65, :], ALU.add,
                                             waits=[("s_pe", av_done)]
                                             if hh == 0 else ())
                                av_evac = F.c["s_dve"]

                        cur_g = -1
                        for (kind, g, pr, sc, gstart, gstop) in units:
                            if g != cur_g:
                                cur_g = g
                                if pend is not None:
                                    emit_av(pend)
                                    pend = None
                                if g + 1 < nseg:
                                    load_seg(g + 1)
                            sl = g % 2
                            seg_ready = seg_rdy[g]
                            ur = unit % 2
                            if kind == "lat":
                                for hh in range(2):
                                    h0 = 64 * hh
                                    F.mm(pS[ur][0:NLAT, QT * hh:QT * (hh + 1)],
                                         lk_sb[h0:h0 + 64, pr, :],
                                         qsl[qr][h0:h0 + 64, pr, :],
                                         waits=[q_ready, ("s_act", slot_exp[ur])]
                                         if hh == 0 else (),
                                         inc=("s_pe", 1) if hh == 1 else None)
                            else:
                                for hh in range(2):
                                    h0 = 64 * hh
                                    F.mm(pS[ur][:, QT * hh:QT * (hh + 1)],
                                         kseg[sl][h0:h0 + 64, pr,
                                                  128 * (sc - SEGC * g):
                                                  128 * (sc - SEGC * g + 1)],
                                         qsl[qr][h0:h0 + 64, pr, :],
                                         waits=[q_ready, seg_ready,
                                                ("s_act", slot_exp[ur])]
                                         if hh == 0 else (),
                                         inc=("s_pe", 1) if hh == 1 else None)
                            sc_done = F.c["s_pe"]
                            if kind == "lat":
                                F.act(probs[ur][0:NLAT, :], pS[ur][0:NLAT, :],
                                      AF.Exp, scale=SCALE,
                                      waits=[("s_pe", sc_done)])
                            else:
                                F.act(probs[ur][:, :], pS[ur][:, :], AF.Exp,
                                      scale=SCALE, waits=[("s_pe", sc_done)])
                            slot_exp[ur] = F.c["s_act"]
                            avw = [("s_act", F.c["s_act"])]
                            if kind == "sc" and sc >= NB * Q:
                                j = sc - NB * Q
                                for hh in range(2):
                                    F.tt(probs[ur][:, QT * hh:QT * (hh + 1)],
                                         probs[ur][:, QT * hh:QT * (hh + 1)],
                                         dmask_sb[:, j, :], ALU.mult,
                                         waits=[("s_act", F.c["s_act"]),
                                                ("s_cg", n_cg3)]
                                         if hh == 0 else ())
                                avw.append(("s_dve", F.c["s_dve"]))
                            if first_of_qtile:
                                avw.append(("s_dve", ps_guard))
                                first_of_qtile = False
                            if pend is not None:
                                emit_av(pend)
                            pend = (ur, kind, g, pr, sc, gstart, gstop, avw)
                            if gstop:
                                kv_free[sl] = F.c["s_pe"] + 2
                            unit += 1
                        emit_av(pend)
                        pend = None
                        attnU_done = F.c["s_dve"]

                        for h in range(16):
                            F.mm(pDN[0:16, :], eh_sb[64:65, h, :], attnU[64:65, h, :],
                                 start=(h == 0), stop=(h == 15),
                                 waits=[("s_dve", attnU_done)] if h == 0 else (),
                                 inc=("s_pe", 1) if h == 15 else None)
                        F.vrecip(recip_f[:, :], pDN[0:16, :],
                                 waits=[("s_pe", F.c["s_pe"])])
                        F.vcopy(recip_t[:, :], recip_f[:, :])
                        rec_done = F.c["s_dve"]
                        if debug and i == 0:
                            F.dbg_tap("attnU0", dbg["attnU0"][:, :], attnU[:, 0, :],
                                   waits=[("s_dve", attnU_done)])
                        for h in range(16):
                            po = pRC if h % 2 == 0 else pDN
                            F.mm(po[:, :], sel16[:, h, :], recip_t[0:16, :],
                                 waits=[("s_dve", rec_done if h < 2 else F.c["s_dve"])],
                                 inc=("s_pe", 1))
                            F.tt(attnb[0:64, h, :], attnU[0:64, h, :], po[0:64, :],
                                 ALU.mult, waits=[("s_pe", F.c["s_pe"])],
                                 inc=("s_dve", 1))
                        nrm_done = F.c["s_dve"]
                        if debug and i == 0:
                            F.dbg_tap("attnb0", dbg["attnb0"][:, :, :], attnb[:, :, :],
                                   waits=[("s_dve", nrm_done)])
                        for oc in range(NCH):
                            po = pAV[oc % 2]
                            for h in range(16):
                                F.mm(po[:, :], wo_sb[0:64, h, 128 * oc:128 * (oc + 1)],
                                     attnb[0:64, h, :],
                                     start=(h == 0), stop=(h == 15),
                                     waits=[("s_dve", nrm_done), ("s_cg", n_cg3),
                                            ("s_act", F.c["s_act"]),
                                            ("s_dve", F.c["s_dve"])] if h == 0 else (),
                                     inc=("s_pe", 1) if h == 15 else None)
                            xw = [("s_pe", F.c["s_pe"]), xres_ready]
                            if oc == 0:
                                xw.append(("s_stx", stx_free))
                            F.tt(xres[:, oc, :], xres[:, oc, :], po[:, :], ALU.add,
                                 waits=xw, inc=("s_dve", 1))
                        xres_free = F.c["s_dve"]
                        ps_guard = F.c["s_dve"]
                        qsl_free[qr] = F.c["s_pe"]
                        F.sdma(x2r[:, :, QT * i:QT * (i + 1)], xres[:, :, :],
                               waits=[("s_dve", F.c["s_dve"])], inc=("s_stx", 16))
                        stx_free = F.c["s_stx"]
                        if debug and i == 0:
                            F.dbg_tap("x20", dbg["x20"][:, :, :], xres[:, :, :],
                                   waits=[("s_dve", F.c["s_dve"])])

                F.variants(var_body)

            run_block(blk3, body2b)
            x2_st = F.c["s_stx"]

        with ExitStack() as p3s:
            e4 = p3s.enter_context
            sel8 = e4(nc.sbuf_tensor("sel8", [NEXP, NEXP, 128], BF16))
            w1_sb = e4(nc.sbuf_tensor("w1_sb", [128, NCH, FFND], BF16))
            w2_sb = e4(nc.sbuf_tensor("w2_sb", [128, NCH, FFND], BF16))
            w3_sb = e4(nc.sbuf_tensor("w3_sb", [128, NFC, DM], BF16))
            wg_sb = e4(nc.sbuf_tensor("wg_sb", [128, NCH, NEXP], BF16))
            acc = e4(nc.sbuf_tensor("acc", [128, NCH, HALF], F32))
            x2bh = e4(nc.sbuf_tensor("x2bh", [128, NCH, HALF], BF16))
            xn3 = e4(nc.sbuf_tensor("xn3", [128, NCH, HALF], BF16))
            ln3 = {
                "mean": e4(nc.sbuf_tensor("mean3", [1, QT], F32)),
                "var": e4(nc.sbuf_tensor("var3", [1, QT], F32)),
                "tsc": e4(nc.sbuf_tensor("tsc3", [1, QT], F32)),
                "rstd": e4(nc.sbuf_tensor("rstd3", [1, QT], F32)),
                "tmp": e4(nc.sbuf_tensor("tmp3", [128, QT], F32)),
            }
            lgb = e4(nc.sbuf_tensor("lgb", [NEXP, QT], BF16))
            lgtm = e4(nc.sbuf_tensor("lgtm", [128, NEXP], F32))
            m1t = e4(nc.sbuf_tensor("m1t", [128, 1], F32))
            m2t = e4(nc.sbuf_tensor("m2t", [128, 1], F32))
            eq1t = e4(nc.sbuf_tensor("eq1t", [128, NEXP], F32))
            eq2t = e4(nc.sbuf_tensor("eq2t", [128, NEXP], F32))
            zt = e4(nc.sbuf_tensor("zt", [128, 1], F32))
            rt = e4(nc.sbuf_tensor("rt", [128, 1], F32))
            wtmt = e4(nc.sbuf_tensor("wtmt", [128, NEXP], F32))
            wT_sb = e4(nc.sbuf_tensor("wT_sb", [NEXP, HALF], BF16))
            w_sb = e4(nc.sbuf_tensor("w_sb", [128, NTT, QT], BF16))
            su_sb = [e4(nc.sbuf_tensor(f"su{r}", [128, QT], BF16)) for r in range(2)]
            hh_sb = e4(nc.sbuf_tensor("hh_sb", [128, QT], BF16))
            hw_sb = e4(nc.sbuf_tensor("hw_sb", [128, NFC, QT], BF16))
            ln3["sq"] = hw_sb[:, 0:NCH, :]

            pU = [e4(nc.psum_tensor(f"pU{r}", [128, QT], F32)) for r in range(2)]
            pV = [e4(nc.psum_tensor(f"pV{r}", [128, QT], F32)) for r in range(2)]
            pYF = [e4(nc.psum_tensor(f"pYF{r}", [128, QT], F32)) for r in range(2)]
            pWB = e4(nc.psum_tensor("pWB", [128, QT], F32))
            pTPb = e4(nc.psum_tensor("pTPb", [128, NEXP], BF16))
            ln3.update(pS1=pU[0], pS2=pU[1], pB1=pV[0], pB2=pV[1])

            blk4 = e4(nc.Block())

            def body3():
                F.gdma(wg_sb[:, :, :], wgt.rearrange("(c p) n -> p c n", p=128),
                       inc=("s_cg", 16))
                F.gdma(sel8[:, :, :], sel8i[:, :, :], inc=("s_cg", 16))
                n_cg4 = F.c["s_cg"]
                n_c4 = F.c["s_c"]
                emit_ln = make_ln(ln3, QT)
                x2r = x2_scr.rearrange("(c p) t -> p c t", p=128)
                w1r = [w1b_scr[ex, :, :].rearrange("(c p) n -> p c n", p=128)
                       for ex in range(NEXP)]
                w2r = [w2b_scr[ex, :, :].rearrange("(c p) n -> p c n", p=128)
                       for ex in range(NEXP)]
                w3r = [w3[ex, :, :].rearrange("(c p) n -> p c n", p=128)
                       for ex in range(NEXP)]
                wc_done = [("s_wc", 16 * 2 * NEXP)]

                w_free = {"w1": 0, "w2": 0, "w3": 0}
                wgen = {"w1": 0, "w2": 0, "w3": 0}

                def load_weights(ex):
                    for nm, src, dst in (("w1", w1r[ex], w1_sb), ("w2", w2r[ex], w2_sb)):
                        F.sdma(dst[:, :, :], src,
                               waits=[("s_pe", w_free[nm])] + wc_done,
                               inc=(f"s_{nm}", 16))
                        wgen[nm] += 1
                    F.gdma(w3_sb[:, :, :], w3r[ex], waits=[("s_pe", w_free["w3"])],
                           inc=("s_w3", 16))
                    wgen["w3"] += 1

                for hf in range(2):
                    F.sdma(acc[:, :, :], x2r[:, :, HALF * hf:HALF * (hf + 1)],
                           waits=[("s_stx", x2_st), ("s_out", 16 * NCH * hf)],
                           inc=("s_x2l", 16))
                    x2_ready = ("s_x2l", F.c["s_x2l"])
                    F.vcopy(x2bh[:, :, :], acc[:, :, :], waits=[x2_ready],
                            inc=("s_dve", 1))
                    cast_done = F.c["s_dve"]
                    for tt_i in range(NTT):
                        sl = slice(QT * tt_i, QT * (tt_i + 1))
                        xn_done = emit_ln(
                            lambda kc, _s=sl: x2bh[:, :, _s] if kc is None
                            else x2bh[:, kc, _s],
                            lambda kc, _s=sl: xn3[:, kc, _s],
                            [("s_dve", cast_done)])
                        for kc in range(NCH):
                            F.mm(pU[0][0:NEXP, :], wg_sb[:, kc, :], xn3[:, kc, sl],
                                 start=(kc == 0), stop=(kc == NCH - 1),
                                 waits=[("s_dve", xn_done), ("s_cg", n_cg4),
                                        ("s_act", F.c["s_act"])] if kc == 0 else (),
                                 inc=("s_pe", 1) if kc == NCH - 1 else None)
                        F.acopy(lgb[:, :], pU[0][0:NEXP, :],
                                waits=[("s_pe", F.c["s_pe"])], inc=("s_act", 1))
                        lg_done = F.c["s_act"]
                        for tb in range(QT // 128):
                            F.tp(pTPb[:, :], lgb[0:NEXP, 128 * tb:128 * (tb + 1)],
                                 ident_b[0:NEXP, 0:NEXP],
                                 waits=[("s_act", lg_done),
                                        ("s_dve", F.c["s_dve"])],
                                 inc=("s_pe", 1))
                            F.vcopy(lgtm[:, :], pTPb[:, :],
                                    waits=[("s_pe", F.c["s_pe"])])
                            F.vmax(m1t[:, :], lgtm[:, :])
                            F.ts(eq1t[:, :], lgtm[:, :], m1t[:, 0:1], ALU.is_ge)
                            F.ts(eq2t[:, :], eq1t[:, :], -1e30, ALU.mult)
                            F.tt(eq2t[:, :], lgtm[:, :], eq2t[:, :], ALU.add)
                            F.vmax(m2t[:, :], eq2t[:, :])
                            F.tt(zt[:, :], m2t[:, :], m1t[:, :], ALU.subtract,
                                 inc=("s_dve", 1))
                            F.act(zt[:, :], zt[:, :], AF.Exp,
                                  waits=[("s_dve", F.c["s_dve"])], inc=("s_act", 1))
                            F.ts(rt[:, :], zt[:, :], 1.0, ALU.add,
                                 waits=[("s_act", F.c["s_act"])])
                            F.vrecip(rt[:, :], rt[:, :])
                            F.ts(eq2t[:, :], lgtm[:, :], m2t[:, 0:1], ALU.is_ge)
                            F.tt(eq2t[:, :], eq2t[:, :], eq1t[:, :], ALU.subtract)
                            F.ts(wtmt[:, :], eq2t[:, :], zt[:, 0:1], ALU.mult)
                            F.tt(wtmt[:, :], wtmt[:, :], eq1t[:, :], ALU.add)
                            F.ts(wtmt[:, :], wtmt[:, :], rt[:, 0:1], ALU.mult,
                                 inc=("s_dve", 1))
                            wtm_done = F.c["s_dve"]
                            if debug and hf == 0 and tt_i == 0 and tb == 0:
                                F.dbg_tap("wtm0", dbg["wtm0"][:, :], wtmt[:, :],
                                       waits=[("s_dve", wtm_done)])
                            F.tp(pYF[0][0:NEXP, 0:128], wtmt[:, :],
                                 ident_f[:, :],
                                 waits=[("s_dve", wtm_done)], inc=("s_pe", 1))
                            F.vcopy(wT_sb[:, QT * tt_i + 128 * tb:
                                          QT * tt_i + 128 * (tb + 1)],
                                    pYF[0][0:NEXP, 0:128],
                                    waits=[("s_pe", F.c["s_pe"])], inc=("s_dve", 1))
                    wT_done = F.c["s_dve"]
                    load_weights(0)
                    silu_free = {0: F.c["s_act"], 1: F.c["s_act"]}
                    hh_free = {0: F.c["s_dve"], 1: F.c["s_dve"]}
                    accadd_free = {0: F.c["s_dve"], 1: F.c["s_dve"]}
                    for ex in range(NEXP):
                        w1_ready = ("s_w1", 16 * wgen["w1"])
                        w2_ready = ("s_w2", 16 * wgen["w2"])
                        w3_ready = ("s_w3", 16 * wgen["w3"])
                        for tt_i in range(NTT):
                            F.mm(pWB[:, :], sel8[:, ex, :],
                                 wT_sb[0:NEXP, QT * tt_i:QT * (tt_i + 1)],
                                 waits=[("s_dve", wT_done), ("s_cg", n_cg4),
                                        ("s_act", F.c["s_act"])],
                                 inc=("s_pe", 1))
                            F.acopy(w_sb[:, tt_i, :], pWB[:, :],
                                    waits=[("s_pe", F.c["s_pe"])], inc=("s_act", 1))
                        wsb_done = ("s_act", F.c["s_act"])
                        first_u = {0: True, 1: True}
                        for tt_i in range(NTT):
                            sl = slice(QT * tt_i, QT * (tt_i + 1))
                            for fc in range(NFC):
                                fr = fc % 2
                                uw = [("s_act", silu_free[fr])]
                                if first_u[fr]:
                                    uw += [w1_ready, w2_ready,
                                           ("s_dve", F.c["s_dve"])]
                                    first_u[fr] = False
                                for kc in range(NCH):
                                    F.mm(pU[fr][:, :],
                                         w1_sb[:, kc, 128 * fc:128 * (fc + 1)],
                                         xn3[:, kc, sl],
                                         start=(kc == 0), stop=(kc == NCH - 1),
                                         waits=uw if kc == 0 else ())
                                for kc in range(NCH):
                                    F.mm(pV[fr][:, :],
                                         w2_sb[:, kc, 128 * fc:128 * (fc + 1)],
                                         xn3[:, kc, sl],
                                         start=(kc == 0), stop=(kc == NCH - 1),
                                         waits=[("s_dve", hh_free[fr])]
                                         if kc == 0 else (),
                                         inc=("s_pe", 1) if kc == NCH - 1 else None)
                                uv_done = F.c["s_pe"]
                                F.act(su_sb[fr][:, :], pU[fr][:, :], AF.Silu,
                                      waits=[("s_pe", uv_done)], inc=("s_act", 1))
                                silu_free[fr] = F.c["s_act"]
                                F.tt(hh_sb[:, :], su_sb[fr][:, :], pV[fr][:, :],
                                     ALU.mult,
                                     waits=[("s_act", F.c["s_act"]), wsb_done])
                                F.tt(hw_sb[:, fc, :], hh_sb[:, :], w_sb[:, tt_i, :],
                                     ALU.mult, inc=("s_dve", 1))
                                hh_free[fr] = F.c["s_dve"]
                                if debug and hf == 0 and ex == 0 and tt_i == 0 and fc == 0:
                                    F.dbg_tap("hw0", dbg["hw0"][:, :], hw_sb[:, 0, :],
                                           waits=[("s_dve", F.c["s_dve"])])
                            h_done = F.c["s_dve"]
                            if tt_i == NTT - 1:
                                w_free["w1"] = F.c["s_pe"]
                                w_free["w2"] = F.c["s_pe"]
                            for oc in range(NCH):
                                yr = oc % 2
                                yw = [("s_dve", max(h_done, accadd_free[yr]))]
                                if oc == 0 and tt_i == 0:
                                    yw.append(w3_ready)
                                for fc in range(NFC):
                                    F.mm(pYF[yr][:, :],
                                         w3_sb[:, fc, 128 * oc:128 * (oc + 1)],
                                         hw_sb[:, fc, :],
                                         start=(fc == 0), stop=(fc == NFC - 1),
                                         waits=yw if fc == 0 else (),
                                         inc=("s_pe", 1) if fc == NFC - 1 else None)
                                F.tt(acc[:, oc, sl], acc[:, oc, sl], pYF[yr][:, :],
                                     ALU.add, waits=[("s_pe", F.c["s_pe"])],
                                     inc=("s_dve", 1))
                                accadd_free[yr] = F.c["s_dve"]
                        w_free["w3"] = F.c["s_pe"]
                        if ex + 1 < NEXP:
                            load_weights(ex + 1)
                    out_done = F.c["s_dve"]
                    for kc in range(NCH):
                        F.sdma(yf[128 * kc:128 * (kc + 1), HALF * hf:HALF * (hf + 1)],
                               acc[:, kc, :], waits=[("s_dve", out_done)],
                               inc=("s_out", 16))
                if debug:
                    for n in ["xn0", "krot0", "vext0", "attnU0", "attnb0",
                              "x20", "wtm0", "hw0"]:
                        F.wait("sync", "s_dbg_" + n, F.c["s_dbg_" + n])
                F.wait("sync", "s_out", F.c["s_out"])

            run_block(blk4, body3)

    return nc, dbg


def host_constants(T, QT):
    NB = QT // 128
    p = np.arange(128)
    d = p % HD
    i = d // 2
    om = 10000.0 ** (-(i.astype(np.float64)) / 32.0)
    t = np.arange(T, dtype=np.float64)
    ang = om[:, None] * t[None, :]
    ctab = np.cos(ang).astype(np.float32)
    sgn = np.where(d % 2 == 0, -1.0, 1.0)
    stab = (np.sin(ang) * sgn[:, None]).astype(np.float32)
    pswap = np.zeros((128, 128), np.float32)
    for m in range(128):
        pswap[m ^ 1, m] = 1.0
    s_loc = np.arange(128)
    q_loc = np.arange(QT)
    dmaskv = np.zeros((128, NB, QT), np.float32)
    for j in range(NB):
        iq = q_loc // 128
        p0 = (s_loc[:, None] < ((q_loc[None, :] % 128) // PAGE + 1) * PAGE)
        dmaskv[:, j, :] = np.where(iq[None, :] < j, 0.0,
                                   np.where(iq[None, :] == j, p0, 1.0))
    identf = np.eye(128, dtype=np.float32)
    sel16v = np.zeros((16, 16, 64), np.float32)
    for h in range(16):
        sel16v[h, h, :] = 1.0
    sel8v = np.zeros((8, 8, 128), np.float32)
    for ex in range(8):
        sel8v[ex, ex, :] = 1.0
    ehv = np.zeros((128, 16, 16), np.float32)
    for h in range(16):
        ehv[64, h, h] = 1.0
    return ctab, stab, pswap, dmaskv, identf, sel16v, sel8v, ehv


def make_in_maps(x, Wq, Wk, Wv, lat_k, lat_v, W_lk, W_lv, W_out, Wg, W1, W2, W3,
                 T, QT):
    ctab, stab, pswap, dmaskv, identf, sel16v, sel8v, ehv = host_constants(T, QT)
    shared = {
        "wq": np.ascontiguousarray(Wq, dtype=np.float32),
        "wk": np.ascontiguousarray(Wk, dtype=np.float32),
        "wv": np.ascontiguousarray(Wv, dtype=np.float32),
        "wo": np.ascontiguousarray(W_out, dtype=np.float32),
        "wlk": np.ascontiguousarray(W_lk, dtype=np.float32),
        "wlv": np.ascontiguousarray(W_lv, dtype=np.float32),
        "latk": np.ascontiguousarray(np.asarray(lat_k)[0].T, dtype=np.float32),
        "latv": np.ascontiguousarray(np.asarray(lat_v)[0].T, dtype=np.float32),
        "wgt": np.ascontiguousarray(Wg, dtype=np.float32),
        "w1": np.ascontiguousarray(W1, dtype=np.float32),
        "w2": np.ascontiguousarray(W2, dtype=np.float32),
        "w3": np.ascontiguousarray(W3, dtype=np.float32),
        "ctab": ctab, "stab": stab, "pswap": pswap, "dmask": dmaskv,
        "identf": identf, "sel16i": sel16v, "sel8i": sel8v, "ehi": ehv,
    }
    xfs = [np.ascontiguousarray(np.asarray(x)[b].T, dtype=np.float32)
           for b in range(B)]
    in_maps = []
    for c in range(8):
        b, q = c // 4, c % 4
        m = dict(shared)
        m["xf"] = xfs[b]
        m["qparam"] = np.array([[q]], np.int32)
        in_maps.append(m)
    return in_maps


def assemble(results, T, QT):
    out = np.empty((B, T, DM), np.float32)
    for c in range(8):
        b, q = c // 4, c % 4
        yfv = np.asarray(results[c]["yf"])
        for i, Q in enumerate(qset(q)):
            out[b, QT * Q:QT * (Q + 1), :] = yfv[:, QT * i:QT * (i + 1)].T
    return out


def kernel(L0_attn_x, L0_attn_Wq, L0_attn_Wk, L0_attn_Wv, L0_attn_lat_k,
           L0_attn_lat_v, L0_attn_W_lk, L0_attn_W_lv, L0_attn_W_out,
           L0_ffn_Wg, L0_ffn_W1, L0_ffn_W2, L0_ffn_W3):
    T, QT = 8192, 512
    in_maps = make_in_maps(L0_attn_x, L0_attn_Wq, L0_attn_Wk, L0_attn_Wv,
                           L0_attn_lat_k, L0_attn_lat_v, L0_attn_W_lk,
                           L0_attn_W_lv, L0_attn_W_out, L0_ffn_Wg, L0_ffn_W1,
                           L0_ffn_W2, L0_ffn_W3, T, QT)
    nc, _ = build_nc(T, QT, debug=False)
    res = run_bass_kernel_spmd(nc, in_maps, core_ids=list(range(8)))
    return assemble(res.results, T, QT)


# version 06
# speedup vs baseline: 1.3846x; 1.0832x over previous
from contextlib import ExitStack

import numpy as np
import ml_dtypes

import concourse.bass as bass
import concourse.mybir as mybir
from concourse.bass_utils import run_bass_kernel_spmd

F32 = mybir.dt.float32
BF16 = mybir.dt.bfloat16
I32 = mybir.dt.int32
AF = mybir.ActivationFunctionType
ALU = mybir.AluOpType
AX = mybir.AxisListType
bfloat16 = ml_dtypes.bfloat16

B = 2
DM = 1024
NCH = 8
HEADS = 16
HD = 64
NPAIR = 8
NLAT = 8
PAGE = 16
FFND = 2048
NEXP = 8
NFC = 16
EPS = 1e-5
SCALE = 0.125
SEGC = 4

ENGS = ("sync", "gpsimd", "tensor", "scalar", "vector")


def qset(q):
    return sorted([q, 7 - q, 8 + q, 15 - q])


class Flow:

    def __init__(self, nc, sems):
        self.nc = nc
        self.S = sems
        self.c = {k: 0 for k in sems}
        self.which = None
        self.eng = None
        self.qreg = None

    AUTO = {"vector": "s_dve", "scalar": "s_act", "gpsimd_c": "s_mk"}

    def _do(self, engname, fn, waits, inc):
        auto = self.AUTO.get(engname)
        if auto is not None:
            engname = "gpsimd" if engname == "gpsimd_c" else engname
            assert inc is None or inc[0] == auto
            waits = list(waits) + [(auto, self.c[auto])]
            inc = (auto, 1)
        for s, v in waits:
            assert v <= self.c[s], f"wait on future {s}:{v} (at {self.c[s]})"
        if self.which == engname:
            for s, v in waits:
                if v > 0:
                    self.eng.wait_ge(self.S[s], v)
            inst = fn(self.eng)
            if inc is not None:
                inst.then_inc(self.S[inc[0]], inc[1])
        if inc is not None:
            self.c[inc[0]] += inc[1]

    def mm(self, out, lhsT, rhs, start=True, stop=True, waits=(), inc=None):
        self._do("tensor", lambda e: e.matmul(out, lhsT, rhs, start=start, stop=stop),
                 waits, inc)

    def tp(self, out, in_, ident, waits=(), inc=None):
        self._do("tensor", lambda e: e.transpose(out, in_, ident), waits, inc)

    def act(self, out, in_, func, scale=1.0, bias=0.0, waits=(), inc=None):
        self._do("scalar", lambda e: e.activation(out, in_, func, bias=bias, scale=scale),
                 waits, inc)

    def acopy(self, out, in_, waits=(), inc=None):
        self._do("scalar", lambda e: e.copy(out, in_), waits, inc)

    def tt(self, out, a, b, op, waits=(), inc=None):
        self._do("vector", lambda e: e.tensor_tensor(out, a, b, op), waits, inc)

    def ts(self, out, a, s1, op, waits=(), inc=None):
        self._do("vector", lambda e: e.tensor_single_scalar(out, a, s1, op), waits, inc)

    def vcopy(self, out, in_, waits=(), inc=None):
        self._do("vector", lambda e: e.tensor_copy(out, in_), waits, inc)

    def vrecip(self, out, in_, waits=(), inc=None):
        self._do("vector", lambda e: e.reciprocal(out, in_), waits, inc)

    def vmax(self, out, in_, waits=(), inc=None):
        self._do("vector", lambda e: e.reduce_max(out, in_, AX.X), waits, inc)

    def gmemset(self, ap, val, waits=(), inc=None):
        self._do("gpsimd_c", lambda e: e.memset(ap, val), waits, inc)

    def sdma(self, out, in_, waits=(), inc=None):
        self._do("sync", lambda e: e.dma_start(out=out, in_=in_), waits, inc)

    def gdma(self, out, in_, waits=(), inc=None):
        self._do("gpsimd", lambda e: e.dma_start(out=out, in_=in_), waits, inc)

    def wait(self, engname, sem, val):
        assert val <= self.c[sem], f"wait on future {sem}:{val}"
        if self.which == engname and val > 0:
            self.eng.wait_ge(self.S[sem], val)

    def dbg_tap(self, name, dram, src, waits=()):
        sem = "s_dbg_" + name
        self.sdma(dram, src, waits=waits, inc=(sem, 16))
        self.wait("vector", sem, self.c[sem])
        self.wait("sync", sem, self.c[sem])

    def if_q(self, k):
        return self.eng.If_eq(self.qreg, k)

    def variants(self, fn, state=()):
        import copy as _copy
        base = dict(self.c)
        sbase = [_copy.deepcopy(d) for d in state]
        ends = []
        sends = []
        for k in range(4):
            self.c = dict(base)
            for d, b in zip(state, sbase):
                d.clear()
                d.update(_copy.deepcopy(b))
            with self.if_q(k):
                fn(k)
            ends.append(dict(self.c))
            sends.append([_copy.deepcopy(d) for d in state])
        for o in ends[1:]:
            assert o == ends[0], "variant counter mismatch"
        for o in sends[1:]:
            assert o == sends[0], "variant state mismatch"
        self.c = ends[0]


def build_nc(T=8192, QT=512, debug=False):
    NQT = T // QT
    assert NQT == 16
    NB = QT // 128
    OWN = 4 * QT
    HALF = OWN // 2
    NTT = HALF // QT

    nc = bass.Bass("TRN2", target_bir_lowering=False)

    xf = nc.dram_tensor("xf", [DM, T], F32, kind="ExternalInput").ap()
    wq = nc.dram_tensor("wq", [DM, DM], F32, kind="ExternalInput").ap()
    wk = nc.dram_tensor("wk", [DM, DM], F32, kind="ExternalInput").ap()
    wv = nc.dram_tensor("wv", [DM, DM], F32, kind="ExternalInput").ap()
    wo = nc.dram_tensor("wo", [DM, DM], F32, kind="ExternalInput").ap()
    wlk = nc.dram_tensor("wlk", [HD, DM], F32, kind="ExternalInput").ap()
    wlv = nc.dram_tensor("wlv", [HD, DM], F32, kind="ExternalInput").ap()
    latk = nc.dram_tensor("latk", [HD, NLAT], F32, kind="ExternalInput").ap()
    latv = nc.dram_tensor("latv", [HD, NLAT], F32, kind="ExternalInput").ap()
    wgt = nc.dram_tensor("wgt", [DM, NEXP], F32, kind="ExternalInput").ap()
    w1 = nc.dram_tensor("w1", [NEXP, DM, FFND], F32, kind="ExternalInput").ap()
    w2 = nc.dram_tensor("w2", [NEXP, DM, FFND], F32, kind="ExternalInput").ap()
    w3 = nc.dram_tensor("w3", [NEXP, FFND, DM], F32, kind="ExternalInput").ap()
    ctab = nc.dram_tensor("ctab", [128, T], F32, kind="ExternalInput").ap()
    stab = nc.dram_tensor("stab", [128, T], F32, kind="ExternalInput").ap()
    pswap = nc.dram_tensor("pswap", [128, 128], F32, kind="ExternalInput").ap()
    dmask = nc.dram_tensor("dmask", [128, NB, QT], F32, kind="ExternalInput").ap()
    identf = nc.dram_tensor("identf", [128, 128], F32, kind="ExternalInput").ap()
    sel16i = nc.dram_tensor("sel16i", [16, 16, 64], F32, kind="ExternalInput").ap()
    sel8i = nc.dram_tensor("sel8i", [NEXP, NEXP, 128], F32, kind="ExternalInput").ap()
    ehi = nc.dram_tensor("ehi", [128, 16, 16], F32, kind="ExternalInput").ap()
    qparam = nc.dram_tensor("qparam", [1, 1], I32, kind="ExternalInput").ap()

    yf = nc.dram_tensor("yf", [DM, OWN], F32, kind="ExternalOutput").ap()

    k_scr = nc.dram_tensor("k_scr", [DM, T], BF16).ap()
    v_scr = nc.dram_tensor("v_scr", [T, 1040], BF16).ap()
    q_scr = nc.dram_tensor("q_scr", [DM, OWN], BF16).ap()
    x2_scr = nc.dram_tensor("x2_scr", [DM, OWN], F32).ap()
    w1b_scr = nc.dram_tensor("w1b_scr", [NEXP, DM, FFND], BF16).ap()
    w2b_scr = nc.dram_tensor("w2b_scr", [NEXP, DM, FFND], BF16).ap()

    dbg = {}
    if debug:
        def mkdbg(name, shape, dt=F32):
            dbg[name] = nc.dram_tensor("dbg_" + name, shape, dt, kind="ExternalOutput").ap()
        mkdbg("xn0", [128, NCH, QT], BF16)
        mkdbg("krot0", [128, QT], BF16)
        mkdbg("vext0", [128, 1040], BF16)
        mkdbg("attnU0", [65, QT])
        mkdbg("attnb0", [64, HEADS, QT], BF16)
        mkdbg("x20", [128, NCH, QT])
        mkdbg("wtm0", [128, NEXP])
        mkdbg("hw0", [128, QT], BF16)

    sem_names = [
        "s_c", "s_cg", "s_in0", "s_in1", "s_qin0", "s_qin1",
        "s_xg0", "s_xg1", "s_qg0", "s_qg1",
        "s_kv0", "s_kv1", "s_kv2", "s_q0", "s_q1", "s_xr",
        "s_sk0", "s_sk1", "s_sv0", "s_sv1", "s_sqr0", "s_sqr1",
        "s_stx", "s_w1", "s_w2", "s_w3", "s_x2l", "s_wc",
        "s_out", "s_pe", "s_act", "s_dve", "s_mk",
    ]
    if debug:
        sem_names += ["s_dbg_" + n for n in
                      ["xn0", "krot0", "vext0", "attnU0", "attnb0", "x20",
                       "wtm0", "hw0"]]

    blk_tag = [0]

    with ExitStack() as top:
        sems = {n: top.enter_context(nc.semaphore(n)) for n in sem_names}
        F = Flow(nc, sems)

        def run_block(blk, body):
            res = {}
            c0 = dict(F.c)
            blk_tag[0] += 1
            tag = blk_tag[0]

            def section(name):
                def _(eng):
                    F.which, F.eng = name, eng
                    F.c = dict(c0)
                    with ExitStack() as rs:
                        F.qreg = rs.enter_context(eng.register(f"qr{tag}_{name}"))
                        if tag == 1 and name == "sync":
                            eng.dma_start(out=qsb[:, :], in_=qparam[:, :]).then_inc(
                                sems["s_c"], 16)
                        eng.wait_ge(sems["s_c"], 16)
                        eng.reg_load(F.qreg, qsb[0:1, 0:1])
                        if tag > 1:
                            for sem, v in c0.items():
                                if v > 0:
                                    eng.wait_ge(sems[sem], v)
                        body()
                    res[name] = dict(F.c)
                return _

            for name in ENGS:
                getattr(blk, name)(section(name))
            vals = list(res.values())
            for v in vals[1:]:
                assert v == vals[0], f"engine counter mismatch"
            F.c = vals[0]
            F.which = F.eng = None

        e = top.enter_context
        qsb = e(nc.sbuf_tensor("qsb", [1, 1], I32))
        ones_bf = e(nc.sbuf_tensor("ones_bf", [128, 1], BF16))
        ones1f = e(nc.sbuf_tensor("ones1f", [1, 128], F32))
        eps_sb = e(nc.sbuf_tensor("eps_sb", [1, 1], F32))
        ident_b = e(nc.sbuf_tensor("ident_b", [128, 128], BF16))
        ident_f = e(nc.sbuf_tensor("ident_f", [128, 128], F32))
        lk_sb = e(nc.sbuf_tensor("lk_sb", [128, NPAIR, NLAT], BF16))
        lv_ext = e(nc.sbuf_tensor("lv_ext", [NLAT, 1040], BF16))
        sel16 = e(nc.sbuf_tensor("sel16", [16, 16, 64], BF16))
        eh_sb = e(nc.sbuf_tensor("eh_sb", [128, 16, 16], F32))
        pswap_sb = e(nc.sbuf_tensor("pswap_sb", [128, 128], BF16))

        def make_ln(ln, QTW):
            ln.setdefault("free", 0)
            ln.setdefault("sq_free", 0)
            ln.setdefault("bc_free", 0)

            def emit_ln(xb_chunks, xn_out_chunks, entry_waits):
                guard = [("s_dve", ln["free"])]
                F.act(ln["sq"][:, :, :], xb_chunks(None), AF.Square,
                      waits=list(entry_waits) + [("s_pe", ln["sq_free"])],
                      inc=("s_act", 1))
                sq_done = F.c["s_act"]
                for kc in range(NCH):
                    F.mm(ln["pS1"][0:1, 0:QTW], ones_bf[:, 0:1], xb_chunks(kc),
                         start=(kc == 0), stop=(kc == NCH - 1),
                         waits=(list(entry_waits) + guard) if kc == 0 else ())
                for kc in range(NCH):
                    F.mm(ln["pS2"][0:1, 0:QTW], ones_bf[:, 0:1], ln["sq"][:, kc, :],
                         start=(kc == 0), stop=(kc == NCH - 1),
                         waits=[("s_act", sq_done)] if kc == 0 else (),
                         inc=("s_pe", 1) if kc == NCH - 1 else None)
                ln["sq_free"] = F.c["s_pe"]
                F.ts(ln["mean"][:, :], ln["pS1"][0:1, 0:QTW], 1.0 / DM, ALU.mult,
                     waits=[("s_pe", F.c["s_pe"])])
                F.ts(ln["tsc"][:, :], ln["pS2"][0:1, 0:QTW], 1.0 / DM, ALU.mult)
                F.tt(ln["var"][:, :], ln["mean"][:, :], ln["mean"][:, :], ALU.mult)
                F.tt(ln["var"][:, :], ln["tsc"][:, :], ln["var"][:, :], ALU.subtract,
                     inc=("s_dve", 1))
                ln["free"] = F.c["s_dve"]
                F.act(ln["tsc"][:, :], ln["var"][:, :], AF.Ln, bias=eps_sb[0:1, 0:1],
                      waits=[("s_dve", F.c["s_dve"])])
                F.act(ln["rstd"][:, :], ln["tsc"][:, :], AF.Exp, scale=-0.5,
                      inc=("s_act", 1))
                F.mm(ln["pB1"][:, 0:QTW], ones1f[0:1, :], ln["mean"][0:1, :],
                     waits=[("s_act", F.c["s_act"]), ("s_dve", ln["bc_free"])])
                F.mm(ln["pB2"][:, 0:QTW], ones1f[0:1, :], ln["rstd"][0:1, :],
                     inc=("s_pe", 1))
                bc = F.c["s_pe"]
                for kc in range(NCH):
                    F.tt(ln["tmp"][:, :], xb_chunks(kc), ln["pB1"][:, 0:QTW],
                         ALU.subtract, waits=[("s_pe", bc)] if kc == 0 else ())
                    F.tt(xn_out_chunks(kc), ln["tmp"][:, :], ln["pB2"][:, 0:QTW],
                         ALU.mult, inc=("s_dve", 1) if kc == NCH - 1 else None)
                ln["bc_free"] = F.c["s_dve"]
                return F.c["s_dve"]
            return emit_ln

        with ExitStack() as p1s:
            e1 = p1s.enter_context
            wk_sb = e1(nc.sbuf_tensor("wk_sb", [128, NCH, DM], BF16))
            wv_sb = e1(nc.sbuf_tensor("wv_sb", [128, NCH, DM], BF16))
            wlk_sb = e1(nc.sbuf_tensor("wlk_sb", [HD, DM], BF16))
            wlv_sb = e1(nc.sbuf_tensor("wlv_sb", [HD, DM], BF16))
            latk_sb = e1(nc.sbuf_tensor("latk_sb", [HD, NLAT], BF16))
            latv_sb = e1(nc.sbuf_tensor("latv_sb", [HD, NLAT], BF16))
            xbuf = [e1(nc.sbuf_tensor(f"xbuf{r}", [128, NCH, QT], F32)) for r in range(2)]
            xb_bf = [e1(nc.sbuf_tensor(f"xb_bf{r}", [128, NCH, QT], BF16)) for r in range(2)]
            ctb1 = [e1(nc.sbuf_tensor(f"ctb1{r}", [128, QT], F32)) for r in range(2)]
            stb1 = [e1(nc.sbuf_tensor(f"stb1{r}", [128, QT], F32)) for r in range(2)]
            xn1 = [e1(nc.sbuf_tensor(f"xn1{r}", [128, NCH, QT], BF16)) for r in range(2)]
            ln1 = {
                "sq": e1(nc.sbuf_tensor("sq1", [128, NCH, QT], BF16)),
                "mean": e1(nc.sbuf_tensor("mean1", [1, QT], F32)),
                "var": e1(nc.sbuf_tensor("var1", [1, QT], F32)),
                "tsc": e1(nc.sbuf_tensor("tsc1", [1, QT], F32)),
                "rstd": e1(nc.sbuf_tensor("rstd1", [1, QT], F32)),
                "tmp": e1(nc.sbuf_tensor("tmp1", [128, QT], F32)),
            }
            kbf = [e1(nc.sbuf_tensor(f"kbf{r}", [128, QT], BF16)) for r in range(2)]
            t1_sb = e1(nc.sbuf_tensor("t1_sb", [128, QT], F32))
            t2_sb = e1(nc.sbuf_tensor("t2_sb", [128, QT], F32))
            krot = [e1(nc.sbuf_tensor(f"krot{r}", [128, QT], BF16)) for r in range(2)]
            vext = [e1(nc.sbuf_tensor(f"vext{r}", [128, 1040], BF16)) for r in range(2)]

            pLN1 = e1(nc.psum_tensor("pLN1", [128, 512], F32))
            pLN2 = e1(nc.psum_tensor("pLN2", [128, 512], F32))
            pB1 = e1(nc.psum_tensor("pB1", [128, 512], F32))
            pB2 = e1(nc.psum_tensor("pB2", [128, 512], F32))
            pK = [e1(nc.psum_tensor(f"pK{r}", [128, 512], F32)) for r in range(2)]
            pW = [e1(nc.psum_tensor(f"pW{r}", [128, 512], F32)) for r in range(2)]
            ln1.update(pS1=pLN1, pS2=pLN2, pB1=pB1, pB2=pB2)

            blk1 = e1(nc.Block())
            state = {}

            def body01():
                F.c["s_c"] = max(F.c["s_c"], 16)
                F.sdma(ident_f[:, :], identf[:, :], inc=("s_c", 16))
                F.gdma(wk_sb[:, :, :], wk.rearrange("(c p) n -> p c n", p=128),
                       inc=("s_cg", 16))
                F.gdma(wv_sb[:, :, :], wv.rearrange("(c p) n -> p c n", p=128),
                       inc=("s_cg", 16))
                for dst, src in ((wlk_sb, wlk), (wlv_sb, wlv), (latk_sb, latk),
                                 (latv_sb, latv), (pswap_sb, pswap), (ident_b, identf)):
                    F.gdma(dst[:, :], src[:, :], inc=("s_cg", 16))
                F.gdma(sel16[:, :, :], sel16i[:, :, :], inc=("s_cg", 16))
                F.sdma(eh_sb[:, :, :], ehi[:, :, :], inc=("s_c", 16))
                n_cg = F.c["s_cg"]
                n_c = F.c["s_c"]
                F.gmemset(ones_bf[:, :], 1.0)
                F.gmemset(ones1f[:, :], 1.0)
                F.gmemset(eps_sb[:, :], EPS)
                for r in range(2):
                    F.gmemset(vext[r][:, :], 0.0)
                F.gmemset(lv_ext[:, :], 0.0, inc=("s_mk", 1))
                zdone = F.c["s_mk"]
                first = True
                for r in range(2):
                    F.gmemset(vext[r][:, :].rearrange("p (g x) -> p g x", x=65)[:, :, 64:65],
                              1.0, waits=[("s_mk", zdone)] if first else ())
                    first = False
                F.gmemset(lv_ext[:, :].rearrange("p (g x) -> p g x", x=65)[:, :, 64:65],
                          1.0, inc=("s_mk", 1))
                ms_done = ("s_mk", F.c["s_mk"])

                for pc in range(NPAIR):
                    F.mm(pK[0][:, 0:NLAT], wlk_sb[0:HD, 128 * pc:128 * (pc + 1)],
                         latk_sb[0:HD, :],
                         waits=[("s_cg", n_cg), ms_done, ("s_act", F.c["s_act"])],
                         inc=("s_pe", 1))
                    F.acopy(lk_sb[:, pc, :], pK[0][:, 0:NLAT],
                            waits=[("s_pe", F.c["s_pe"])], inc=("s_act", 1))
                F.mm(pK[1][0:NLAT, 0:512], latv_sb[0:HD, :], wlv_sb[0:HD, 0:512],
                     waits=[("s_cg", n_cg), ms_done])
                F.mm(pW[0][0:NLAT, 0:512], latv_sb[0:HD, :], wlv_sb[0:HD, 512:1024],
                     inc=("s_pe", 1))
                lvp = F.c["s_pe"]
                F.vcopy(lv_ext[:, 0:520].rearrange("p (g x) -> p g x", x=65)[:, :, 0:64],
                        pK[1][0:NLAT, 0:512].rearrange("p (g x) -> p g x", x=64),
                        waits=[("s_pe", lvp), ms_done])
                F.vcopy(lv_ext[:, 520:1040].rearrange("p (g x) -> p g x", x=65)[:, :, 0:64],
                        pW[0][0:NLAT, 0:512].rearrange("p (g x) -> p g x", x=64),
                        inc=("s_dve", 1))
                lat_act = F.c["s_act"]
                lat_dve = F.c["s_dve"]

                ln1.update(free=0, sq_free=0, bc_free=0)
                emit_ln = make_ln(ln1, QT)
                xfr = xf.rearrange("(c p) t -> p c t", p=128)
                kscr_r = k_scr.rearrange("(c p) t -> p c t", p=128)
                in_free = {0: 0, 1: 0}
                vext_free = {0: 0, 1: 0}
                krot_free = {0: 0, 1: 0}
                prs = {"pP_free": {0: lat_dve, 1: lat_dve},
                       "pW_free": {0: lat_dve, 1: lat_dve}}
                for qt in range(NQT):
                    r = qt % 2
                    gen = qt // 2 + 1
                    sname = f"s_in{r}"
                    F.sdma(xbuf[r][:, :, :], xfr[:, :, QT * qt:QT * (qt + 1)],
                           waits=[("s_dve", in_free[r])], inc=(sname, 16))
                    F.sdma(ctb1[r][:, :], ctab[:, QT * qt:QT * (qt + 1)],
                           waits=[("s_dve", in_free[r])], inc=(sname, 16))
                    F.sdma(stb1[r][:, :], stab[:, QT * qt:QT * (qt + 1)],
                           inc=(sname, 16))
                    entry0 = [(sname, 48 * gen)]
                    if qt == 0:
                        entry0 += [("s_cg", n_cg), ("s_c", n_c),
                                   ("s_dve", F.c["s_dve"])]
                    F.vcopy(xb_bf[r][:, :, :], xbuf[r][:, :, :], waits=entry0)
                    entry = [("s_dve", F.c["s_dve"])]
                    xn_done = emit_ln(
                        lambda kc, _r=r: xb_bf[_r][:, :, :] if kc is None
                        else xb_bf[_r][:, kc, :],
                        lambda kc, _r=r: xn1[_r][:, kc, :], entry)
                    if qt < NEXP:
                        F.gdma(w1b_scr[qt, :, :], w1[qt, :, :], inc=("s_wc", 16))
                    else:
                        F.gdma(w2b_scr[qt - NEXP, :, :], w2[qt - NEXP, :, :],
                               inc=("s_wc", 16))

                    for pc in range(NPAIR):
                        kr = pc % 2
                        kw = [("s_dve", max(xn_done, prs["pP_free"][kr]))]
                        if qt == 0 and pc < 2:
                            kw.append(("s_act", lat_act))
                        for kc in range(NCH):
                            F.mm(pK[kr][:, 0:QT], wk_sb[:, kc, 128 * pc:128 * (pc + 1)],
                                 xn1[r][:, kc, :],
                                 start=(kc == 0), stop=(kc == NCH - 1),
                                 waits=kw if kc == 0 else (),
                                 inc=("s_pe", 1) if kc == NCH - 1 else None)
                        F.acopy(kbf[kr][:, :], pK[kr][:, 0:QT],
                                waits=[("s_pe", F.c["s_pe"])], inc=("s_act", 1))
                        F.mm(pW[kr][:, 0:QT], pswap_sb[:, :], kbf[kr][:, :],
                             waits=[("s_act", F.c["s_act"]),
                                    ("s_dve", prs["pW_free"][kr])], inc=("s_pe", 1))
                        F.tt(t1_sb[:, :], pK[kr][:, 0:QT], ctb1[r][:, :], ALU.mult,
                             waits=[("s_pe", F.c["s_pe"])])
                        F.tt(t2_sb[:, :], pW[kr][:, 0:QT], stb1[r][:, :], ALU.mult)
                        F.tt(krot[kr][:, :], t1_sb[:, :], t2_sb[:, :], ALU.add,
                             waits=[(f"s_sk{kr}", krot_free[kr])])
                        prs["pP_free"][kr] = F.c["s_dve"]
                        prs["pW_free"][kr] = F.c["s_dve"]
                        F.gdma(kscr_r[:, pc, QT * qt:QT * (qt + 1)], krot[kr][:, :],
                               waits=[("s_dve", F.c["s_dve"])], inc=(f"s_sk{kr}", 16))
                        krot_free[kr] = F.c[f"s_sk{kr}"]
                        if debug and qt == 0 and pc == 0:
                            F.dbg_tap("krot0", dbg["krot0"][:, :], krot[kr][:, :],
                                   waits=[("s_dve", F.c["s_dve"])])
                    for tb in range(NB):
                        vr = tb % 2
                        for vt in range(2):
                            pv = pB1 if vt == 0 else pB2
                            for kc in range(NCH):
                                F.mm(pv[:, 0:512],
                                     xn1[r][:, kc, 128 * tb:128 * (tb + 1)],
                                     wv_sb[:, kc, 512 * vt:512 * (vt + 1)],
                                     start=(kc == 0), stop=(kc == NCH - 1),
                                     waits=[("s_dve", F.c["s_dve"]),
                                            (f"s_sv{vr}", vext_free[vr])]
                                     if kc == 0 else (),
                                     inc=("s_pe", 1) if kc == NCH - 1 else None)
                            F.vcopy(
                                vext[vr][:, 520 * vt:520 * (vt + 1)]
                                .rearrange("p (g x) -> p g x", x=65)[:, :, 0:64],
                                pv[:, 0:512].rearrange("p (g x) -> p g x", x=64),
                                waits=[("s_pe", F.c["s_pe"])],
                                inc=("s_dve", 1) if vt == 1 else None)
                        F.gdma(v_scr[QT * qt + 128 * tb:QT * qt + 128 * (tb + 1), :],
                               vext[vr][:, :],
                               waits=[("s_dve", F.c["s_dve"])], inc=(f"s_sv{vr}", 16))
                        vext_free[vr] = F.c[f"s_sv{vr}"]
                        ln1["bc_free"] = F.c["s_dve"]
                        if debug and qt == 0 and tb == 0:
                            F.dbg_tap("vext0", dbg["vext0"][:, :], vext[vr][:, :],
                                   waits=[("s_dve", F.c["s_dve"])])
                    in_free[r] = F.c["s_dve"]
                    if debug and qt == 0:
                        F.dbg_tap("xn0", dbg["xn0"][:, :, :], xn1[r][:, :, :],
                               waits=[("s_dve", xn_done)])

            run_block(blk1, body01)
            p1_st = [("s_sk0", F.c["s_sk0"]), ("s_sk1", F.c["s_sk1"]),
                     ("s_sv0", F.c["s_sv0"]), ("s_sv1", F.c["s_sv1"])]

        with ExitStack() as p2a:
            e2 = p2a.enter_context
            wq_sb = e2(nc.sbuf_tensor("wq_sb", [128, NCH, DM], BF16))
            xqb = [e2(nc.sbuf_tensor(f"xqb{r}", [128, NCH, QT], F32)) for r in range(2)]
            xq_bf = [e2(nc.sbuf_tensor(f"xq_bf{r}", [128, NCH, QT], BF16)) for r in range(2)]
            ctb2 = [e2(nc.sbuf_tensor(f"ctb2{r}", [128, QT], F32)) for r in range(2)]
            stb2 = [e2(nc.sbuf_tensor(f"stb2{r}", [128, QT], F32)) for r in range(2)]
            xn2t = e2(nc.sbuf_tensor("xn2t", [128, NCH, QT], BF16))
            ln2 = {
                "sq": e2(nc.sbuf_tensor("sq2", [128, NCH, QT], BF16)),
                "mean": e2(nc.sbuf_tensor("mean2", [1, QT], F32)),
                "var": e2(nc.sbuf_tensor("var2", [1, QT], F32)),
                "tsc": e2(nc.sbuf_tensor("tsc2", [1, QT], F32)),
                "rstd": e2(nc.sbuf_tensor("rstd2", [1, QT], F32)),
                "tmp": e2(nc.sbuf_tensor("tmp2", [128, QT], F32)),
            }
            qbf = [e2(nc.sbuf_tensor(f"qbf{r}", [128, QT], BF16)) for r in range(2)]
            tq1 = e2(nc.sbuf_tensor("tq1", [128, QT], F32))
            tq2 = e2(nc.sbuf_tensor("tq2", [128, QT], F32))
            qrot = [e2(nc.sbuf_tensor(f"qrot{r}", [128, QT], BF16)) for r in range(2)]

            pLQ1 = e2(nc.psum_tensor("pLQ1", [128, QT], F32))
            pLQ2 = e2(nc.psum_tensor("pLQ2", [128, QT], F32))
            pBQ1 = e2(nc.psum_tensor("pBQ1", [128, QT], F32))
            pBQ2 = e2(nc.psum_tensor("pBQ2", [128, QT], F32))
            pKQ = [e2(nc.psum_tensor(f"pKQ{r}", [128, QT], F32)) for r in range(2)]
            pWQ = [e2(nc.psum_tensor(f"pWQ{r}", [128, QT], F32)) for r in range(2)]
            ln2.update(pS1=pLQ1, pS2=pLQ2, pB1=pBQ1, pB2=pBQ2)

            blk2 = e2(nc.Block())

            def body2a():
                F.gdma(wq_sb[:, :, :], wq.rearrange("(c p) n -> p c n", p=128),
                       inc=("s_cg", 16))
                n_cg2 = F.c["s_cg"]
                ln2.update(free=0, sq_free=0, bc_free=0)
                emit_ln = make_ln(ln2, QT)
                xfr = xf.rearrange("(c p) t -> p c t", p=128)
                qscr_r = q_scr.rearrange("(c p) t -> p c t", p=128)

                def var_body(vq):
                    qs = qset(vq)
                    in_free = {0: 0, 1: 0}
                    qrot_free = {0: 0, 1: 0}
                    prs = {"pP_free": {0: 0, 1: 0}, "pW_free": {0: 0, 1: 0}}
                    for i, Q in enumerate(qs):
                        r = i % 2
                        gen = i // 2 + 1
                        sname = f"s_qin{r}"
                        F.sdma(xqb[r][:, :, :], xfr[:, :, QT * Q:QT * (Q + 1)],
                               waits=[("s_dve", in_free[r])], inc=(sname, 16))
                        F.sdma(ctb2[r][:, :], ctab[:, QT * Q:QT * (Q + 1)],
                               waits=[("s_dve", in_free[r])], inc=(sname, 16))
                        F.sdma(stb2[r][:, :], stab[:, QT * Q:QT * (Q + 1)],
                               inc=(sname, 16))
                        entry0 = [(sname, 48 * gen)]
                        if i == 0:
                            entry0 += [("s_cg", n_cg2)]
                        F.vcopy(xq_bf[r][:, :, :], xqb[r][:, :, :], waits=entry0)
                        entry = [("s_dve", F.c["s_dve"])]
                        xn_done = emit_ln(
                            lambda kc, _r=r: xq_bf[_r][:, :, :] if kc is None
                            else xq_bf[_r][:, kc, :],
                            lambda kc: xn2t[:, kc, :], entry)
                        for pc in range(NPAIR):
                            kr = pc % 2
                            for kc in range(NCH):
                                F.mm(pKQ[kr][:, :], wq_sb[:, kc, 128 * pc:128 * (pc + 1)],
                                     xn2t[:, kc, :],
                                     start=(kc == 0), stop=(kc == NCH - 1),
                                     waits=[("s_dve", max(xn_done, prs["pP_free"][kr]))]
                                     if kc == 0 else (),
                                     inc=("s_pe", 1) if kc == NCH - 1 else None)
                            F.acopy(qbf[kr][:, :], pKQ[kr][:, :],
                                    waits=[("s_pe", F.c["s_pe"])], inc=("s_act", 1))
                            F.mm(pWQ[kr][:, :], pswap_sb[:, :], qbf[kr][:, :],
                                 waits=[("s_act", F.c["s_act"]),
                                        ("s_dve", prs["pW_free"][kr])], inc=("s_pe", 1))
                            F.tt(tq1[:, :], pKQ[kr][:, :], ctb2[r][:, :], ALU.mult,
                                 waits=[("s_pe", F.c["s_pe"])])
                            F.tt(tq2[:, :], pWQ[kr][:, :], stb2[r][:, :], ALU.mult)
                            F.tt(qrot[kr][:, :], tq1[:, :], tq2[:, :], ALU.add,
                                 waits=[(f"s_sqr{kr}", qrot_free[kr])])
                            prs["pP_free"][kr] = F.c["s_dve"]
                            prs["pW_free"][kr] = F.c["s_dve"]
                            F.gdma(qscr_r[:, pc, QT * i:QT * (i + 1)], qrot[kr][:, :],
                                   waits=[("s_dve", F.c["s_dve"])],
                                   inc=(f"s_sqr{kr}", 16))
                            qrot_free[kr] = F.c[f"s_sqr{kr}"]
                        in_free[r] = F.c["s_dve"]

                F.variants(var_body, state=[ln2])

            run_block(blk2, body2a)
            q_st = [("s_sqr0", F.c["s_sqr0"]), ("s_sqr1", F.c["s_sqr1"])]

        with ExitStack() as p2b:
            e3 = p2b.enter_context
            wo_sb = e3(nc.sbuf_tensor("wo_sb", [64, HEADS, DM], BF16))
            dmask_sb = e3(nc.sbuf_tensor("dmask_sb", [128, NB, QT], BF16))
            qsl = [e3(nc.sbuf_tensor(f"qsl{r}", [128, NCH, QT], BF16)) for r in range(2)]
            kseg = [e3(nc.sbuf_tensor(f"kseg{r}", [128, NCH, SEGC * 128], BF16))
                    for r in range(3)]
            vseg = [e3(nc.sbuf_tensor(f"vseg{r}", [128, SEGC, 1040], BF16))
                    for r in range(3)]
            probs = [e3(nc.sbuf_tensor(f"probs{r}", [128, 2 * QT], BF16))
                     for r in range(4)]
            attnU = e3(nc.sbuf_tensor("attnU", [65, HEADS, QT], F32))
            attnb = e3(nc.sbuf_tensor("attnb", [64, HEADS, QT], BF16))
            recip_t = e3(nc.sbuf_tensor("recip_t", [16, QT], BF16))
            recip_f = e3(nc.sbuf_tensor("recip_f", [16, QT], F32))
            xres = e3(nc.sbuf_tensor("xres", [128, NCH, QT], F32))

            pS = [e3(nc.psum_tensor(f"pS{r}", [128, 2 * QT], F32)) for r in range(2)]
            pAV = [e3(nc.psum_tensor(f"pAV{hh}", [128, QT], F32)) for hh in range(2)]
            pDN = e3(nc.psum_tensor("pDN", [64, QT], F32))
            pRC = e3(nc.psum_tensor("pRC", [64, QT], F32))

            blk3 = e3(nc.Block())

            def body2b():
                F.gdma(wo_sb[0:64, :, :], wo.rearrange("(h d) n -> d h n", d=64),
                       inc=("s_cg", 16))
                F.gdma(dmask_sb[:, :, :], dmask[:, :, :], inc=("s_cg", 16))
                n_cg3 = F.c["s_cg"]
                xfr = xf.rearrange("(c p) t -> p c t", p=128)
                qscr_r = q_scr.rearrange("(c p) t -> p c t", p=128)
                kscr_r = k_scr.rearrange("(c p) t -> p c t", p=128)
                vscr_r = v_scr.rearrange("(n p) c -> p n c", p=128)
                x2r = x2_scr.rearrange("(c p) t -> p c t", p=128)

                def var_body(vq):
                    qs = qset(vq)
                    qsl_free = {0: 0, 1: 0}
                    kv_free = {0: 0, 1: 0, 2: 0}
                    kv_gen = {0: 0, 1: 0, 2: 0}
                    seg_ctr = [0]
                    sl_of = {}
                    slot_exp = {0: 0, 1: 0}
                    xres_free = 0
                    stx_free = 0
                    ps_guard = 0
                    av_evac = 0
                    unit = 0
                    for i, Q in enumerate(qs):
                        qr = i % 2
                        F.sdma(qsl[qr][:, :, :], qscr_r[:, :, QT * i:QT * (i + 1)],
                               waits=[("s_pe", qsl_free[qr])] + q_st,
                               inc=(f"s_q{qr}", 16))
                        q_ready = (f"s_q{qr}", 16 * (i // 2 + 1))
                        F.sdma(xres[:, :, :], xfr[:, :, QT * Q:QT * (Q + 1)],
                               waits=[("s_dve", xres_free), ("s_stx", stx_free)],
                               inc=("s_xr", 16))
                        xres_ready = ("s_xr", F.c["s_xr"])

                        extent = NB * (Q + 1)
                        nseg = (extent + SEGC - 1) // SEGC

                        seg_rdy = {}

                        def load_seg(g):
                            if g >= nseg:
                                return
                            sl = seg_ctr[0] % 3
                            seg_ctr[0] += 1
                            sl_of[g] = sl
                            c0, c1 = SEGC * g, min(SEGC * (g + 1), extent)
                            F.sdma(kseg[sl][:, :, 0:128 * (c1 - c0)],
                                   kscr_r[:, :, 128 * c0:128 * c1],
                                   waits=[("s_pe", kv_free[sl])] + p1_st,
                                   inc=(f"s_kv{sl}", 16))
                            F.sdma(vseg[sl][:, 0:c1 - c0, :], vscr_r[:, c0:c1, :],
                                   inc=(f"s_kv{sl}", 16))
                            kv_gen[sl] += 1
                            seg_rdy[g] = (f"s_kv{sl}", 32 * kv_gen[sl])

                        units = []
                        for g in range(nseg):
                            c0, c1 = SEGC * g, min(SEGC * (g + 1), extent)
                            for pr in range(NPAIR):
                                if g == 0:
                                    units.append(("lat", g, pr, -1, True, False))
                                for sc in range(c0, c1):
                                    units.append(("sc", g, pr, sc,
                                                  g > 0 and sc == c0,
                                                  sc == c1 - 1))

                        load_seg(0)
                        load_seg(1)
                        pend = []
                        first_of_qtile = True

                        def emit_av(p):
                            nonlocal av_evac
                            (pbr, kind, g, pr, sc, gstart, gstop, avw) = p
                            sl = sl_of[g]
                            if gstart:
                                avw = avw + [("s_dve", av_evac)]
                            for hh in range(2):
                                if kind == "lat":
                                    lhsT = lv_ext[:, 130 * pr + 65 * hh:
                                                  130 * pr + 65 * (hh + 1)]
                                    rhs = probs[pbr][0:NLAT, QT * hh:QT * (hh + 1)]
                                else:
                                    lhsT = vseg[sl][:, sc - SEGC * g,
                                                    130 * pr + 65 * hh:
                                                    130 * pr + 65 * (hh + 1)]
                                    rhs = probs[pbr][:, QT * hh:QT * (hh + 1)]
                                F.mm(pAV[hh][0:65, :], lhsT, rhs,
                                     start=gstart, stop=gstop,
                                     waits=avw if hh == 0 else (),
                                     inc=("s_pe", 1) if hh == 1 else None)
                            if gstop:
                                kv_free[sl] = F.c["s_pe"]
                                av_done = F.c["s_pe"]
                                for hh in range(2):
                                    h = 2 * pr + hh
                                    if g == 0:
                                        F.vcopy(attnU[:, h, :], pAV[hh][0:65, :],
                                                waits=[("s_pe", av_done)]
                                                if hh == 0 else ())
                                    else:
                                        F.tt(attnU[:, h, :], attnU[:, h, :],
                                             pAV[hh][0:65, :], ALU.add,
                                             waits=[("s_pe", av_done)]
                                             if hh == 0 else ())
                                av_evac = F.c["s_dve"]

                        cur_g = 0
                        unit_in_seg = 0
                        for (kind, g, pr, sc, gstart, gstop) in units:
                            if g != cur_g:
                                cur_g = g
                                unit_in_seg = 0
                            if unit_in_seg == 3:
                                load_seg(g + 2)
                            sl = sl_of[g]
                            seg_ready = seg_rdy[g]
                            psr = unit % 2
                            pbr = unit % 4
                            if kind == "lat":
                                for hh in range(2):
                                    h0 = 64 * hh
                                    F.mm(pS[psr][0:NLAT, QT * hh:QT * (hh + 1)],
                                         lk_sb[h0:h0 + 64, pr, :],
                                         qsl[qr][h0:h0 + 64, pr, :],
                                         waits=[q_ready, ("s_act", slot_exp[psr])]
                                         if hh == 0 else (),
                                         inc=("s_pe", 1) if hh == 1 else None)
                            else:
                                for hh in range(2):
                                    h0 = 64 * hh
                                    F.mm(pS[psr][:, QT * hh:QT * (hh + 1)],
                                         kseg[sl][h0:h0 + 64, pr,
                                                  128 * (sc - SEGC * g):
                                                  128 * (sc - SEGC * g + 1)],
                                         qsl[qr][h0:h0 + 64, pr, :],
                                         waits=[q_ready, seg_ready,
                                                ("s_act", slot_exp[psr])]
                                         if hh == 0 else (),
                                         inc=("s_pe", 1) if hh == 1 else None)
                            sc_done = F.c["s_pe"]
                            if kind == "lat":
                                F.act(probs[pbr][0:NLAT, :], pS[psr][0:NLAT, :],
                                      AF.Exp, scale=SCALE,
                                      waits=[("s_pe", sc_done)])
                            else:
                                F.act(probs[pbr][:, :], pS[psr][:, :], AF.Exp,
                                      scale=SCALE, waits=[("s_pe", sc_done)])
                            slot_exp[psr] = F.c["s_act"]
                            avw = [("s_act", F.c["s_act"])]
                            if kind == "sc" and sc >= NB * Q:
                                j = sc - NB * Q
                                for hh in range(2):
                                    F.tt(probs[pbr][:, QT * hh:QT * (hh + 1)],
                                         probs[pbr][:, QT * hh:QT * (hh + 1)],
                                         dmask_sb[:, j, :], ALU.mult,
                                         waits=[("s_act", F.c["s_act"]),
                                                ("s_cg", n_cg3)]
                                         if hh == 0 else ())
                                avw.append(("s_dve", F.c["s_dve"]))
                            if first_of_qtile:
                                avw.append(("s_dve", ps_guard))
                                first_of_qtile = False
                            pend.append((pbr, kind, g, pr, sc, gstart, gstop, avw))
                            if len(pend) > 3:
                                emit_av(pend.pop(0))
                            unit += 1
                            unit_in_seg += 1
                        while pend:
                            emit_av(pend.pop(0))
                        attnU_done = F.c["s_dve"]

                        for h in range(16):
                            F.mm(pDN[0:16, :], eh_sb[64:65, h, :], attnU[64:65, h, :],
                                 start=(h == 0), stop=(h == 15),
                                 waits=[("s_dve", attnU_done)] if h == 0 else (),
                                 inc=("s_pe", 1) if h == 15 else None)
                        F.vrecip(recip_f[:, :], pDN[0:16, :],
                                 waits=[("s_pe", F.c["s_pe"])])
                        F.vcopy(recip_t[:, :], recip_f[:, :])
                        rec_done = F.c["s_dve"]
                        if debug and i == 0:
                            F.dbg_tap("attnU0", dbg["attnU0"][:, :], attnU[:, 0, :],
                                   waits=[("s_dve", attnU_done)])
                        for h in range(16):
                            po = pRC if h % 2 == 0 else pDN
                            F.mm(po[:, :], sel16[:, h, :], recip_t[0:16, :],
                                 waits=[("s_dve", rec_done if h < 2 else F.c["s_dve"])],
                                 inc=("s_pe", 1))
                            F.tt(attnb[0:64, h, :], attnU[0:64, h, :], po[0:64, :],
                                 ALU.mult, waits=[("s_pe", F.c["s_pe"])],
                                 inc=("s_dve", 1))
                        nrm_done = F.c["s_dve"]
                        if debug and i == 0:
                            F.dbg_tap("attnb0", dbg["attnb0"][:, :, :], attnb[:, :, :],
                                   waits=[("s_dve", nrm_done)])
                        for oc in range(NCH):
                            po = pAV[oc % 2]
                            for h in range(16):
                                F.mm(po[:, :], wo_sb[0:64, h, 128 * oc:128 * (oc + 1)],
                                     attnb[0:64, h, :],
                                     start=(h == 0), stop=(h == 15),
                                     waits=[("s_dve", nrm_done), ("s_cg", n_cg3),
                                            ("s_act", F.c["s_act"]),
                                            ("s_dve", F.c["s_dve"])] if h == 0 else (),
                                     inc=("s_pe", 1) if h == 15 else None)
                            xw = [("s_pe", F.c["s_pe"]), xres_ready]
                            if oc == 0:
                                xw.append(("s_stx", stx_free))
                            F.tt(xres[:, oc, :], xres[:, oc, :], po[:, :], ALU.add,
                                 waits=xw, inc=("s_dve", 1))
                        xres_free = F.c["s_dve"]
                        ps_guard = F.c["s_dve"]
                        qsl_free[qr] = F.c["s_pe"]
                        F.gdma(x2r[:, :, QT * i:QT * (i + 1)], xres[:, :, :],
                               waits=[("s_dve", F.c["s_dve"])], inc=("s_stx", 16))
                        stx_free = F.c["s_stx"]
                        if debug and i == 0:
                            F.dbg_tap("x20", dbg["x20"][:, :, :], xres[:, :, :],
                                   waits=[("s_dve", F.c["s_dve"])])

                F.variants(var_body)

            run_block(blk3, body2b)
            x2_st = F.c["s_stx"]

        with ExitStack() as p3s:
            e4 = p3s.enter_context
            sel8 = e4(nc.sbuf_tensor("sel8", [NEXP, NEXP, 128], BF16))
            w1_sb = e4(nc.sbuf_tensor("w1_sb", [128, NCH, FFND], BF16))
            w2_sb = e4(nc.sbuf_tensor("w2_sb", [128, NCH, FFND], BF16))
            w3_sb = e4(nc.sbuf_tensor("w3_sb", [128, NFC, DM], BF16))
            wg_sb = e4(nc.sbuf_tensor("wg_sb", [128, NCH, NEXP], BF16))
            acc = e4(nc.sbuf_tensor("acc", [128, NCH, HALF], F32))
            x2bh = e4(nc.sbuf_tensor("x2bh", [128, NCH, HALF], BF16))
            xn3 = e4(nc.sbuf_tensor("xn3", [128, NCH, HALF], BF16))
            ln3 = {
                "mean": e4(nc.sbuf_tensor("mean3", [1, QT], F32)),
                "var": e4(nc.sbuf_tensor("var3", [1, QT], F32)),
                "tsc": e4(nc.sbuf_tensor("tsc3", [1, QT], F32)),
                "rstd": e4(nc.sbuf_tensor("rstd3", [1, QT], F32)),
                "tmp": e4(nc.sbuf_tensor("tmp3", [128, QT], F32)),
            }
            lgb = e4(nc.sbuf_tensor("lgb", [NEXP, QT], BF16))
            lgtm = e4(nc.sbuf_tensor("lgtm", [128, NEXP], F32))
            m1t = e4(nc.sbuf_tensor("m1t", [128, 1], F32))
            m2t = e4(nc.sbuf_tensor("m2t", [128, 1], F32))
            eq1t = e4(nc.sbuf_tensor("eq1t", [128, NEXP], F32))
            eq2t = e4(nc.sbuf_tensor("eq2t", [128, NEXP], F32))
            zt = e4(nc.sbuf_tensor("zt", [128, 1], F32))
            rt = e4(nc.sbuf_tensor("rt", [128, 1], F32))
            wtmt = e4(nc.sbuf_tensor("wtmt", [128, NEXP], F32))
            wT_sb = e4(nc.sbuf_tensor("wT_sb", [NEXP, HALF], BF16))
            w_sb = e4(nc.sbuf_tensor("w_sb", [128, NTT, QT], BF16))
            su_sb = [e4(nc.sbuf_tensor(f"su{r}", [128, QT], BF16)) for r in range(2)]
            hh_sb = e4(nc.sbuf_tensor("hh_sb", [128, QT], BF16))
            hw_sb = e4(nc.sbuf_tensor("hw_sb", [128, NFC, QT], BF16))
            ln3["sq"] = hw_sb[:, 0:NCH, :]

            pU = [e4(nc.psum_tensor(f"pU{r}", [128, QT], F32)) for r in range(2)]
            pV = [e4(nc.psum_tensor(f"pV{r}", [128, QT], F32)) for r in range(2)]
            pYF = [e4(nc.psum_tensor(f"pYF{r}", [128, QT], F32)) for r in range(2)]
            pWB = e4(nc.psum_tensor("pWB", [128, QT], F32))
            pTPb = e4(nc.psum_tensor("pTPb", [128, NEXP], BF16))
            ln3.update(pS1=pU[0], pS2=pU[1], pB1=pV[0], pB2=pV[1])

            blk4 = e4(nc.Block())

            def body3():
                F.gdma(wg_sb[:, :, :], wgt.rearrange("(c p) n -> p c n", p=128),
                       inc=("s_cg", 16))
                F.gdma(sel8[:, :, :], sel8i[:, :, :], inc=("s_cg", 16))
                n_cg4 = F.c["s_cg"]
                n_c4 = F.c["s_c"]
                ln3.update(free=0, sq_free=0, bc_free=0)
                emit_ln = make_ln(ln3, QT)
                x2r = x2_scr.rearrange("(c p) t -> p c t", p=128)
                w1r = [w1b_scr[ex, :, :].rearrange("(c p) n -> p c n", p=128)
                       for ex in range(NEXP)]
                w2r = [w2b_scr[ex, :, :].rearrange("(c p) n -> p c n", p=128)
                       for ex in range(NEXP)]
                w3r = [w3[ex, :, :].rearrange("(c p) n -> p c n", p=128)
                       for ex in range(NEXP)]
                wc_done = [("s_wc", 16 * 2 * NEXP)]

                w_free = {"w1": 0, "w2": 0, "w3": 0}
                wgen = {"w1": 0, "w2": 0, "w3": 0}

                def load_weights(ex):
                    for nm, src, dst in (("w1", w1r[ex], w1_sb), ("w2", w2r[ex], w2_sb)):
                        F.sdma(dst[:, :, :], src,
                               waits=[("s_pe", w_free[nm])] + wc_done,
                               inc=(f"s_{nm}", 16))
                        wgen[nm] += 1
                    F.gdma(w3_sb[:, :, :], w3r[ex], waits=[("s_pe", w_free["w3"])],
                           inc=("s_w3", 16))
                    wgen["w3"] += 1

                for hf in range(2):
                    F.sdma(acc[:, :, :], x2r[:, :, HALF * hf:HALF * (hf + 1)],
                           waits=[("s_stx", x2_st), ("s_out", 16 * NCH * hf)],
                           inc=("s_x2l", 16))
                    x2_ready = ("s_x2l", F.c["s_x2l"])
                    F.vcopy(x2bh[:, :, :], acc[:, :, :], waits=[x2_ready],
                            inc=("s_dve", 1))
                    cast_done = F.c["s_dve"]
                    for tt_i in range(NTT):
                        sl = slice(QT * tt_i, QT * (tt_i + 1))
                        xn_done = emit_ln(
                            lambda kc, _s=sl: x2bh[:, :, _s] if kc is None
                            else x2bh[:, kc, _s],
                            lambda kc, _s=sl: xn3[:, kc, _s],
                            [("s_dve", cast_done)])
                        for kc in range(NCH):
                            F.mm(pU[0][0:NEXP, :], wg_sb[:, kc, :], xn3[:, kc, sl],
                                 start=(kc == 0), stop=(kc == NCH - 1),
                                 waits=[("s_dve", xn_done), ("s_cg", n_cg4),
                                        ("s_act", F.c["s_act"])] if kc == 0 else (),
                                 inc=("s_pe", 1) if kc == NCH - 1 else None)
                        F.acopy(lgb[:, :], pU[0][0:NEXP, :],
                                waits=[("s_pe", F.c["s_pe"])], inc=("s_act", 1))
                        lg_done = F.c["s_act"]
                        for tb in range(QT // 128):
                            F.tp(pTPb[:, :], lgb[0:NEXP, 128 * tb:128 * (tb + 1)],
                                 ident_b[0:NEXP, 0:NEXP],
                                 waits=[("s_act", lg_done),
                                        ("s_dve", F.c["s_dve"])],
                                 inc=("s_pe", 1))
                            F.vcopy(lgtm[:, :], pTPb[:, :],
                                    waits=[("s_pe", F.c["s_pe"])])
                            F.vmax(m1t[:, :], lgtm[:, :])
                            F.ts(eq1t[:, :], lgtm[:, :], m1t[:, 0:1], ALU.is_ge)
                            F.ts(eq2t[:, :], eq1t[:, :], -1e30, ALU.mult)
                            F.tt(eq2t[:, :], lgtm[:, :], eq2t[:, :], ALU.add)
                            F.vmax(m2t[:, :], eq2t[:, :])
                            F.tt(zt[:, :], m2t[:, :], m1t[:, :], ALU.subtract,
                                 inc=("s_dve", 1))
                            F.act(zt[:, :], zt[:, :], AF.Exp,
                                  waits=[("s_dve", F.c["s_dve"])], inc=("s_act", 1))
                            F.ts(rt[:, :], zt[:, :], 1.0, ALU.add,
                                 waits=[("s_act", F.c["s_act"])])
                            F.vrecip(rt[:, :], rt[:, :])
                            F.ts(eq2t[:, :], lgtm[:, :], m2t[:, 0:1], ALU.is_ge)
                            F.tt(eq2t[:, :], eq2t[:, :], eq1t[:, :], ALU.subtract)
                            F.ts(wtmt[:, :], eq2t[:, :], zt[:, 0:1], ALU.mult)
                            F.tt(wtmt[:, :], wtmt[:, :], eq1t[:, :], ALU.add)
                            F.ts(wtmt[:, :], wtmt[:, :], rt[:, 0:1], ALU.mult,
                                 inc=("s_dve", 1))
                            wtm_done = F.c["s_dve"]
                            if debug and hf == 0 and tt_i == 0 and tb == 0:
                                F.dbg_tap("wtm0", dbg["wtm0"][:, :], wtmt[:, :],
                                       waits=[("s_dve", wtm_done)])
                            F.tp(pYF[0][0:NEXP, 0:128], wtmt[:, :],
                                 ident_f[:, :],
                                 waits=[("s_dve", wtm_done)], inc=("s_pe", 1))
                            F.vcopy(wT_sb[:, QT * tt_i + 128 * tb:
                                          QT * tt_i + 128 * (tb + 1)],
                                    pYF[0][0:NEXP, 0:128],
                                    waits=[("s_pe", F.c["s_pe"])], inc=("s_dve", 1))
                    wT_done = F.c["s_dve"]
                    load_weights(0)
                    silu_free = {0: F.c["s_act"], 1: F.c["s_act"]}
                    hh_free = {0: F.c["s_dve"], 1: F.c["s_dve"]}
                    accadd_free = {0: F.c["s_dve"], 1: F.c["s_dve"]}
                    for ex in range(NEXP):
                        w1_ready = ("s_w1", 16 * wgen["w1"])
                        w2_ready = ("s_w2", 16 * wgen["w2"])
                        w3_ready = ("s_w3", 16 * wgen["w3"])
                        for tt_i in range(NTT):
                            F.mm(pWB[:, :], sel8[:, ex, :],
                                 wT_sb[0:NEXP, QT * tt_i:QT * (tt_i + 1)],
                                 waits=[("s_dve", wT_done), ("s_cg", n_cg4),
                                        ("s_act", F.c["s_act"])],
                                 inc=("s_pe", 1))
                            F.acopy(w_sb[:, tt_i, :], pWB[:, :],
                                    waits=[("s_pe", F.c["s_pe"])], inc=("s_act", 1))
                        wsb_done = ("s_act", F.c["s_act"])
                        first_u = {0: True, 1: True}
                        for tt_i in range(NTT):
                            sl = slice(QT * tt_i, QT * (tt_i + 1))
                            for fc in range(NFC):
                                fr = fc % 2
                                uw = [("s_act", silu_free[fr])]
                                if first_u[fr]:
                                    uw += [w1_ready, w2_ready,
                                           ("s_dve", F.c["s_dve"])]
                                    first_u[fr] = False
                                for kc in range(NCH):
                                    F.mm(pU[fr][:, :],
                                         w1_sb[:, kc, 128 * fc:128 * (fc + 1)],
                                         xn3[:, kc, sl],
                                         start=(kc == 0), stop=(kc == NCH - 1),
                                         waits=uw if kc == 0 else ())
                                    F.mm(pV[fr][:, :],
                                         w2_sb[:, kc, 128 * fc:128 * (fc + 1)],
                                         xn3[:, kc, sl],
                                         start=(kc == 0), stop=(kc == NCH - 1),
                                         waits=[("s_dve", hh_free[fr])]
                                         if kc == 0 else (),
                                         inc=("s_pe", 1) if kc == NCH - 1 else None)
                                uv_done = F.c["s_pe"]
                                F.act(su_sb[fr][:, :], pU[fr][:, :], AF.Silu,
                                      waits=[("s_pe", uv_done)], inc=("s_act", 1))
                                silu_free[fr] = F.c["s_act"]
                                F.tt(hh_sb[:, :], su_sb[fr][:, :], pV[fr][:, :],
                                     ALU.mult,
                                     waits=[("s_act", F.c["s_act"]), wsb_done])
                                F.tt(hw_sb[:, fc, :], hh_sb[:, :], w_sb[:, tt_i, :],
                                     ALU.mult, inc=("s_dve", 1))
                                hh_free[fr] = F.c["s_dve"]
                                if debug and hf == 0 and ex == 0 and tt_i == 0 and fc == 0:
                                    F.dbg_tap("hw0", dbg["hw0"][:, :], hw_sb[:, 0, :],
                                           waits=[("s_dve", F.c["s_dve"])])
                            h_done = F.c["s_dve"]
                            if tt_i == NTT - 1:
                                w_free["w1"] = F.c["s_pe"]
                                w_free["w2"] = F.c["s_pe"]
                            for oc in range(NCH):
                                yr = oc % 2
                                yw = [("s_dve", max(h_done, accadd_free[yr]))]
                                if oc == 0 and tt_i == 0:
                                    yw.append(w3_ready)
                                for fc in range(NFC):
                                    F.mm(pYF[yr][:, :],
                                         w3_sb[:, fc, 128 * oc:128 * (oc + 1)],
                                         hw_sb[:, fc, :],
                                         start=(fc == 0), stop=(fc == NFC - 1),
                                         waits=yw if fc == 0 else (),
                                         inc=("s_pe", 1) if fc == NFC - 1 else None)
                                F.tt(acc[:, oc, sl], acc[:, oc, sl], pYF[yr][:, :],
                                     ALU.add, waits=[("s_pe", F.c["s_pe"])],
                                     inc=("s_dve", 1))
                                accadd_free[yr] = F.c["s_dve"]
                        w_free["w3"] = F.c["s_pe"]
                        if ex + 1 < NEXP:
                            load_weights(ex + 1)
                    out_done = F.c["s_dve"]
                    for kc in range(NCH):
                        F.sdma(yf[128 * kc:128 * (kc + 1), HALF * hf:HALF * (hf + 1)],
                               acc[:, kc, :], waits=[("s_dve", out_done)],
                               inc=("s_out", 16))
                if debug:
                    for n in ["xn0", "krot0", "vext0", "attnU0", "attnb0",
                              "x20", "wtm0", "hw0"]:
                        F.wait("sync", "s_dbg_" + n, F.c["s_dbg_" + n])
                F.wait("sync", "s_out", F.c["s_out"])

            run_block(blk4, body3)

    return nc, dbg


def host_constants(T, QT):
    NB = QT // 128
    p = np.arange(128)
    d = p % HD
    i = d // 2
    om = 10000.0 ** (-(i.astype(np.float64)) / 32.0)
    t = np.arange(T, dtype=np.float64)
    ang = om[:, None] * t[None, :]
    ctab = np.cos(ang).astype(np.float32)
    sgn = np.where(d % 2 == 0, -1.0, 1.0)
    stab = (np.sin(ang) * sgn[:, None]).astype(np.float32)
    pswap = np.zeros((128, 128), np.float32)
    for m in range(128):
        pswap[m ^ 1, m] = 1.0
    s_loc = np.arange(128)
    q_loc = np.arange(QT)
    dmaskv = np.zeros((128, NB, QT), np.float32)
    for j in range(NB):
        iq = q_loc // 128
        p0 = (s_loc[:, None] < ((q_loc[None, :] % 128) // PAGE + 1) * PAGE)
        dmaskv[:, j, :] = np.where(iq[None, :] < j, 0.0,
                                   np.where(iq[None, :] == j, p0, 1.0))
    identf = np.eye(128, dtype=np.float32)
    sel16v = np.zeros((16, 16, 64), np.float32)
    for h in range(16):
        sel16v[h, h, :] = 1.0
    sel8v = np.zeros((8, 8, 128), np.float32)
    for ex in range(8):
        sel8v[ex, ex, :] = 1.0
    ehv = np.zeros((128, 16, 16), np.float32)
    for h in range(16):
        ehv[64, h, h] = 1.0
    return ctab, stab, pswap, dmaskv, identf, sel16v, sel8v, ehv


def make_in_maps(x, Wq, Wk, Wv, lat_k, lat_v, W_lk, W_lv, W_out, Wg, W1, W2, W3,
                 T, QT):
    ctab, stab, pswap, dmaskv, identf, sel16v, sel8v, ehv = host_constants(T, QT)
    shared = {
        "wq": np.ascontiguousarray(Wq, dtype=np.float32),
        "wk": np.ascontiguousarray(Wk, dtype=np.float32),
        "wv": np.ascontiguousarray(Wv, dtype=np.float32),
        "wo": np.ascontiguousarray(W_out, dtype=np.float32),
        "wlk": np.ascontiguousarray(W_lk, dtype=np.float32),
        "wlv": np.ascontiguousarray(W_lv, dtype=np.float32),
        "latk": np.ascontiguousarray(np.asarray(lat_k)[0].T, dtype=np.float32),
        "latv": np.ascontiguousarray(np.asarray(lat_v)[0].T, dtype=np.float32),
        "wgt": np.ascontiguousarray(Wg, dtype=np.float32),
        "w1": np.ascontiguousarray(W1, dtype=np.float32),
        "w2": np.ascontiguousarray(W2, dtype=np.float32),
        "w3": np.ascontiguousarray(W3, dtype=np.float32),
        "ctab": ctab, "stab": stab, "pswap": pswap, "dmask": dmaskv,
        "identf": identf, "sel16i": sel16v, "sel8i": sel8v, "ehi": ehv,
    }
    xfs = [np.ascontiguousarray(np.asarray(x)[b].T, dtype=np.float32)
           for b in range(B)]
    in_maps = []
    for c in range(8):
        b, q = c // 4, c % 4
        m = dict(shared)
        m["xf"] = xfs[b]
        m["qparam"] = np.array([[q]], np.int32)
        in_maps.append(m)
    return in_maps


def assemble(results, T, QT):
    out = np.empty((B, T, DM), np.float32)
    for c in range(8):
        b, q = c // 4, c % 4
        yfv = np.asarray(results[c]["yf"])
        for i, Q in enumerate(qset(q)):
            out[b, QT * Q:QT * (Q + 1), :] = yfv[:, QT * i:QT * (i + 1)].T
    return out


def kernel(L0_attn_x, L0_attn_Wq, L0_attn_Wk, L0_attn_Wv, L0_attn_lat_k,
           L0_attn_lat_v, L0_attn_W_lk, L0_attn_W_lv, L0_attn_W_out,
           L0_ffn_Wg, L0_ffn_W1, L0_ffn_W2, L0_ffn_W3):
    T, QT = 8192, 512
    in_maps = make_in_maps(L0_attn_x, L0_attn_Wq, L0_attn_Wk, L0_attn_Wv,
                           L0_attn_lat_k, L0_attn_lat_v, L0_attn_W_lk,
                           L0_attn_W_lv, L0_attn_W_out, L0_ffn_Wg, L0_ffn_W1,
                           L0_ffn_W2, L0_ffn_W3, T, QT)
    nc, _ = build_nc(T, QT, debug=False)
    res = run_bass_kernel_spmd(nc, in_maps, core_ids=list(range(8)))
    return assemble(res.results, T, QT)


# version 08
# speedup vs baseline: 1.5886x; 1.1473x over previous
from contextlib import ExitStack

import numpy as np
import ml_dtypes

import concourse.bass as bass
import concourse.mybir as mybir
from concourse.bass_utils import run_bass_kernel_spmd

F32 = mybir.dt.float32
BF16 = mybir.dt.bfloat16
I32 = mybir.dt.int32
AF = mybir.ActivationFunctionType
ALU = mybir.AluOpType
AX = mybir.AxisListType
bfloat16 = ml_dtypes.bfloat16

B = 2
DM = 1024
NCH = 8
HEADS = 16
HD = 64
NPAIR = 8
NLAT = 8
PAGE = 16
FFND = 2048
NEXP = 8
NFC = 16
EPS = 1e-5
SCALE = 0.125
SEGC = 4

ENGS = ("sync", "gpsimd", "tensor", "scalar", "vector")
P3MODE = 0xFFFF


def qset(q):
    return sorted([q, 7 - q, 8 + q, 15 - q])


class Flow:

    def __init__(self, nc, sems):
        self.nc = nc
        self.S = sems
        self.c = {k: 0 for k in sems}
        self.which = None
        self.eng = None
        self.qreg = None

    AUTO = {"vector": "s_dve", "scalar": "s_act", "gpsimd_c": "s_mk"}

    def _do(self, engname, fn, waits, inc):
        auto = self.AUTO.get(engname)
        if auto is not None:
            engname = "gpsimd" if engname == "gpsimd_c" else engname
            assert inc is None or inc[0] == auto
            waits = list(waits) + [(auto, self.c[auto])]
            inc = (auto, 1)
        for s, v in waits:
            assert v <= self.c[s], f"wait on future {s}:{v} (at {self.c[s]})"
        if self.which == engname:
            for s, v in waits:
                if v > 0:
                    self.eng.wait_ge(self.S[s], v)
            inst = fn(self.eng)
            if inc is not None:
                inst.then_inc(self.S[inc[0]], inc[1])
        if inc is not None:
            self.c[inc[0]] += inc[1]

    def mm(self, out, lhsT, rhs, start=True, stop=True, waits=(), inc=None):
        self._do("tensor", lambda e: e.matmul(out, lhsT, rhs, start=start, stop=stop),
                 waits, inc)

    def tp(self, out, in_, ident, waits=(), inc=None):
        self._do("tensor", lambda e: e.transpose(out, in_, ident), waits, inc)

    def act(self, out, in_, func, scale=1.0, bias=0.0, waits=(), inc=None):
        self._do("scalar", lambda e: e.activation(out, in_, func, bias=bias, scale=scale),
                 waits, inc)

    def acopy(self, out, in_, waits=(), inc=None):
        self._do("scalar", lambda e: e.copy(out, in_), waits, inc)

    def tt(self, out, a, b, op, waits=(), inc=None):
        self._do("vector", lambda e: e.tensor_tensor(out, a, b, op), waits, inc)

    def ts(self, out, a, s1, op, waits=(), inc=None):
        self._do("vector", lambda e: e.tensor_single_scalar(out, a, s1, op), waits, inc)

    def vcopy(self, out, in_, waits=(), inc=None):
        self._do("vector", lambda e: e.tensor_copy(out, in_), waits, inc)

    def vrecip(self, out, in_, waits=(), inc=None):
        self._do("vector", lambda e: e.reciprocal(out, in_), waits, inc)

    def vmax(self, out, in_, waits=(), inc=None):
        self._do("vector", lambda e: e.reduce_max(out, in_, AX.X), waits, inc)

    def vsum(self, out, in_, waits=(), inc=None):
        self._do("vector", lambda e: e.reduce_sum(out, in_, AX.X), waits, inc)

    def vscan(self, out, d0, d1, init, op0, op1, waits=(), inc=None):
        self._do("vector",
                 lambda e: e.tensor_tensor_scan(out, d0, d1, init, op0, op1),
                 waits, inc)

    def _bound_reg(self, e, bound):
        if getattr(self, "bndval", None) != bound:
            e.reg_mov(self.bndreg, bound)
            self.bndval = bound
        return self.bndreg

    def scat(self, out, idx, in_, bound, waits=(), inc=None):
        self._do("gpsimd", lambda e: e.indirect_dma_start(
            out, bass.IndirectOffsetOnAxis(idx, 0), in_, None,
            bounds_check=self._bound_reg(e, bound), oob_is_err=False), waits, inc)

    def gath(self, out, in_, idx, bound, waits=(), inc=None):
        self._do("gpsimd", lambda e: e.indirect_dma_start(
            out, None, in_, bass.IndirectOffsetOnAxis(idx, 0),
            bounds_check=self._bound_reg(e, bound), oob_is_err=False), waits, inc)

    def gmemset(self, ap, val, waits=(), inc=None):
        self._do("gpsimd_c", lambda e: e.memset(ap, val), waits, inc)

    def vmemset(self, ap, val, waits=(), inc=None):
        self._do("vector", lambda e: e.memset(ap, val), waits, inc)

    def sdma(self, out, in_, waits=(), inc=None):
        self._do("sync", lambda e: e.dma_start(out=out, in_=in_), waits, inc)

    def gdma(self, out, in_, waits=(), inc=None):
        self._do("gpsimd", lambda e: e.dma_start(out=out, in_=in_), waits, inc)

    def wait(self, engname, sem, val):
        assert val <= self.c[sem], f"wait on future {sem}:{val}"
        if self.which == engname and val > 0:
            self.eng.wait_ge(self.S[sem], val)

    def dbg_tap(self, name, dram, src, waits=()):
        sem = "s_dbg_" + name
        self.sdma(dram, src, waits=waits, inc=(sem, 16))
        self.wait("vector", sem, self.c[sem])
        self.wait("sync", sem, self.c[sem])

    def if_q(self, k):
        return self.eng.If_eq(self.qreg, k)

    def variants(self, fn, state=()):
        import copy as _copy
        base = dict(self.c)
        sbase = [_copy.deepcopy(d) for d in state]
        ends = []
        sends = []
        for k in range(4):
            self.c = dict(base)
            for d, b in zip(state, sbase):
                d.clear()
                d.update(_copy.deepcopy(b))
            with self.if_q(k):
                fn(k)
            ends.append(dict(self.c))
            sends.append([_copy.deepcopy(d) for d in state])
        for o in ends[1:]:
            assert o == ends[0], "variant counter mismatch"
        for o in sends[1:]:
            assert o == sends[0], "variant state mismatch"
        self.c = ends[0]


def build_nc(T=8192, QT=512, debug=False):
    NQT = T // QT
    assert NQT == 16
    NB = QT // 128
    OWN = 4 * QT
    HALF = OWN // 2
    NTT = HALF // QT

    nc = bass.Bass("TRN2", target_bir_lowering=False)

    xf = nc.dram_tensor("xf", [DM, T], F32, kind="ExternalInput").ap()
    wq = nc.dram_tensor("wq", [DM, DM], F32, kind="ExternalInput").ap()
    wk = nc.dram_tensor("wk", [DM, DM], F32, kind="ExternalInput").ap()
    wv = nc.dram_tensor("wv", [DM, DM], F32, kind="ExternalInput").ap()
    wo = nc.dram_tensor("wo", [DM, DM], F32, kind="ExternalInput").ap()
    wlk = nc.dram_tensor("wlk", [HD, DM], F32, kind="ExternalInput").ap()
    wlv = nc.dram_tensor("wlv", [HD, DM], F32, kind="ExternalInput").ap()
    latk = nc.dram_tensor("latk", [HD, NLAT], F32, kind="ExternalInput").ap()
    latv = nc.dram_tensor("latv", [HD, NLAT], F32, kind="ExternalInput").ap()
    wgt = nc.dram_tensor("wgt", [DM, NEXP], F32, kind="ExternalInput").ap()
    w1 = nc.dram_tensor("w1", [NEXP, DM, FFND], F32, kind="ExternalInput").ap()
    w2 = nc.dram_tensor("w2", [NEXP, DM, FFND], F32, kind="ExternalInput").ap()
    w3 = nc.dram_tensor("w3", [NEXP, FFND, DM], F32, kind="ExternalInput").ap()
    ctab = nc.dram_tensor("ctab", [128, T], F32, kind="ExternalInput").ap()
    stab = nc.dram_tensor("stab", [128, T], F32, kind="ExternalInput").ap()
    pswap = nc.dram_tensor("pswap", [128, 128], F32, kind="ExternalInput").ap()
    dmask = nc.dram_tensor("dmask", [128, NB, QT], F32, kind="ExternalInput").ap()
    identf = nc.dram_tensor("identf", [128, 128], F32, kind="ExternalInput").ap()
    sel16i = nc.dram_tensor("sel16i", [16, 16, 64], F32, kind="ExternalInput").ap()
    ecap = nc.dram_tensor("ecap", [NEXP, 2], F32, kind="ExternalInput").ap()
    ehi = nc.dram_tensor("ehi", [128, 16, 16], F32, kind="ExternalInput").ap()
    qparam = nc.dram_tensor("qparam", [1, 1], I32, kind="ExternalInput").ap()

    yf = nc.dram_tensor("yf", [DM, OWN], F32, kind="ExternalOutput").ap()

    k_scr = nc.dram_tensor("k_scr", [DM, T], BF16).ap()
    v_scr = nc.dram_tensor("v_scr", [T, 1040], BF16).ap()
    q_scr = nc.dram_tensor("q_scr", [DM, OWN], BF16).ap()
    x2_scr = nc.dram_tensor("x2_scr", [DM, OWN], F32).ap()
    w1b_scr = nc.dram_tensor("w1b_scr", [NEXP, DM, FFND], BF16).ap()
    w2b_scr = nc.dram_tensor("w2b_scr", [NEXP, DM, FFND], BF16).ap()
    g_scr = nc.dram_tensor("g_scr", [NEXP * 768, DM], BF16).ap()
    y_scr = nc.dram_tensor("y_scr", [NEXP * 768, DM], BF16).ap()

    dbg = {}
    if debug:
        def mkdbg(name, shape, dt=F32):
            dbg[name] = nc.dram_tensor("dbg_" + name, shape, dt, kind="ExternalOutput").ap()
        mkdbg("xn0", [128, NCH, QT], BF16)
        mkdbg("krot0", [128, QT], BF16)
        mkdbg("vext0", [128, 1040], BF16)
        mkdbg("attnU0", [65, QT])
        mkdbg("attnb0", [64, HEADS, QT], BF16)
        mkdbg("x20", [128, NCH, QT])
        mkdbg("wtm0", [128, NEXP])
        mkdbg("hw0", [128, QT], BF16)

    sem_names = [
        "s_c", "s_cg", "s_in0", "s_in1", "s_qin0", "s_qin1",
        "s_xg0", "s_xg1", "s_qg0", "s_qg1",
        "s_kv0", "s_kv1", "s_kv2", "s_q0", "s_q1", "s_xr",
        "s_sk0", "s_sk1", "s_sv0", "s_sv1", "s_sqr0", "s_sqr1",
        "s_stx", "s_w1", "s_w2", "s_w3", "s_x2l", "s_wc",
        "s_sc", "s_gt", "s_sy", "s_yg",
        "s_out", "s_pe", "s_act", "s_dve", "s_mk",
    ]
    if debug:
        sem_names += ["s_dbg_" + n for n in
                      ["xn0", "krot0", "vext0", "attnU0", "attnb0", "x20",
                       "wtm0", "hw0"]]

    blk_tag = [0]

    with ExitStack() as top:
        sems = {n: top.enter_context(nc.semaphore(n)) for n in sem_names}
        F = Flow(nc, sems)

        def run_block(blk, body):
            res = {}
            c0 = dict(F.c)
            blk_tag[0] += 1
            tag = blk_tag[0]

            def section(name):
                def _(eng):
                    F.which, F.eng = name, eng
                    F.c = dict(c0)
                    with ExitStack() as rs:
                        F.qreg = rs.enter_context(eng.register(f"qr{tag}_{name}"))
                        if name == "gpsimd":
                            F.bndreg = rs.enter_context(
                                eng.register(f"bnd{tag}"))
                            F.bndval = None
                        if tag == 1 and name == "sync":
                            eng.dma_start(out=qsb[:, :], in_=qparam[:, :]).then_inc(
                                sems["s_c"], 16)
                        eng.wait_ge(sems["s_c"], 16)
                        eng.reg_load(F.qreg, qsb[0:1, 0:1])
                        if tag > 1:
                            for sem, v in c0.items():
                                if v > 0:
                                    eng.wait_ge(sems[sem], v)
                        body()
                    res[name] = dict(F.c)
                return _

            for name in ENGS:
                getattr(blk, name)(section(name))
            vals = list(res.values())
            for v in vals[1:]:
                assert v == vals[0], f"engine counter mismatch"
            F.c = vals[0]
            F.which = F.eng = None

        e = top.enter_context
        qsb = e(nc.sbuf_tensor("qsb", [1, 1], I32))
        ones_bf = e(nc.sbuf_tensor("ones_bf", [128, 1], BF16))
        ones1f = e(nc.sbuf_tensor("ones1f", [1, 128], F32))
        eps_sb = e(nc.sbuf_tensor("eps_sb", [1, 1], F32))
        ident_b = e(nc.sbuf_tensor("ident_b", [128, 128], BF16))
        ident_f = e(nc.sbuf_tensor("ident_f", [128, 128], F32))
        lk_sb = e(nc.sbuf_tensor("lk_sb", [128, NPAIR, NLAT], BF16))
        lv_ext = e(nc.sbuf_tensor("lv_ext", [NLAT, 1040], BF16))
        sel16 = e(nc.sbuf_tensor("sel16", [16, 16, 64], BF16))
        eh_sb = e(nc.sbuf_tensor("eh_sb", [128, 16, 16], F32))
        ecap_sb = e(nc.sbuf_tensor("ecap_sb", [NEXP, 2], F32))
        pswap_sb = e(nc.sbuf_tensor("pswap_sb", [128, 128], BF16))

        def make_ln(ln, QTW):
            ln.setdefault("free", 0)
            ln.setdefault("sq_free", 0)
            ln.setdefault("bc_free", 0)
            ln.setdefault("afree", 0)

            def emit_ln(xb_chunks, xn_out_chunks, entry_waits):
                guard = [("s_dve", ln["free"]), ("s_act", ln["afree"])]
                F.act(ln["sq"][:, :, :], xb_chunks(None), AF.Square,
                      waits=list(entry_waits) + [("s_pe", ln["sq_free"])],
                      inc=("s_act", 1))
                sq_done = F.c["s_act"]
                for kc in range(NCH):
                    F.mm(ln["pS1"][0:1, 0:QTW], ones_bf[:, 0:1], xb_chunks(kc),
                         start=(kc == 0), stop=(kc == NCH - 1),
                         waits=(list(entry_waits) + guard) if kc == 0 else ())
                for kc in range(NCH):
                    F.mm(ln["pS2"][0:1, 0:QTW], ones_bf[:, 0:1], ln["sq"][:, kc, :],
                         start=(kc == 0), stop=(kc == NCH - 1),
                         waits=[("s_act", sq_done)] if kc == 0 else (),
                         inc=("s_pe", 1) if kc == NCH - 1 else None)
                ln["sq_free"] = F.c["s_pe"]
                F.ts(ln["mean"][:, :], ln["pS1"][0:1, 0:QTW], 1.0 / DM, ALU.mult,
                     waits=[("s_pe", F.c["s_pe"])])
                F.ts(ln["tsc"][:, :], ln["pS2"][0:1, 0:QTW], 1.0 / DM, ALU.mult)
                F.tt(ln["var"][:, :], ln["mean"][:, :], ln["mean"][:, :], ALU.mult)
                F.tt(ln["var"][:, :], ln["tsc"][:, :], ln["var"][:, :], ALU.subtract,
                     inc=("s_dve", 1))
                ln["free"] = F.c["s_dve"]
                F.act(ln["tsc"][:, :], ln["var"][:, :], AF.Ln, bias=eps_sb[0:1, 0:1],
                      waits=[("s_dve", F.c["s_dve"])])
                F.act(ln["rstd"][:, :], ln["tsc"][:, :], AF.Exp, scale=-0.5,
                      inc=("s_act", 1))
                F.mm(ln["pB1"][:, 0:QTW], ones1f[0:1, :], ln["mean"][0:1, :],
                     waits=[("s_act", F.c["s_act"]), ("s_dve", ln["bc_free"])])
                F.mm(ln["pB2"][:, 0:QTW], ones1f[0:1, :], ln["rstd"][0:1, :],
                     inc=("s_pe", 1))
                bc = F.c["s_pe"]
                for kc in range(NCH):
                    F.tt(ln["tmp"][:, :], xb_chunks(kc), ln["pB1"][:, 0:QTW],
                         ALU.subtract, waits=[("s_pe", bc)] if kc == 0 else ())
                    F.tt(xn_out_chunks(kc), ln["tmp"][:, :], ln["pB2"][:, 0:QTW],
                         ALU.mult, inc=("s_dve", 1) if kc == NCH - 1 else None)
                ln["bc_free"] = F.c["s_dve"]
                return F.c["s_dve"]
            return emit_ln

        with ExitStack() as p1s:
            e1 = p1s.enter_context
            wk_sb = e1(nc.sbuf_tensor("wk_sb", [128, NCH, DM], BF16))
            wv_sb = e1(nc.sbuf_tensor("wv_sb", [128, NCH, DM], BF16))
            wlk_sb = e1(nc.sbuf_tensor("wlk_sb", [HD, DM], BF16))
            wlv_sb = e1(nc.sbuf_tensor("wlv_sb", [HD, DM], BF16))
            latk_sb = e1(nc.sbuf_tensor("latk_sb", [HD, NLAT], BF16))
            latv_sb = e1(nc.sbuf_tensor("latv_sb", [HD, NLAT], BF16))
            xbuf = [e1(nc.sbuf_tensor(f"xbuf{r}", [128, NCH, QT], F32)) for r in range(2)]
            xb_bf = [e1(nc.sbuf_tensor(f"xb_bf{r}", [128, NCH, QT], BF16)) for r in range(2)]
            ctb1 = [e1(nc.sbuf_tensor(f"ctb1{r}", [128, QT], F32)) for r in range(2)]
            stb1 = [e1(nc.sbuf_tensor(f"stb1{r}", [128, QT], F32)) for r in range(2)]
            xn1 = [e1(nc.sbuf_tensor(f"xn1{r}", [128, NCH, QT], BF16)) for r in range(2)]
            ln1 = {
                "sq": e1(nc.sbuf_tensor("sq1", [128, NCH, QT], BF16)),
                "mean": e1(nc.sbuf_tensor("mean1", [1, QT], F32)),
                "var": e1(nc.sbuf_tensor("var1", [1, QT], F32)),
                "tsc": e1(nc.sbuf_tensor("tsc1", [1, QT], F32)),
                "rstd": e1(nc.sbuf_tensor("rstd1", [1, QT], F32)),
                "tmp": e1(nc.sbuf_tensor("tmp1", [128, QT], F32)),
            }
            kbf = [e1(nc.sbuf_tensor(f"kbf{r}", [128, QT], BF16)) for r in range(2)]
            t1_sb = e1(nc.sbuf_tensor("t1_sb", [128, QT], F32))
            t2_sb = e1(nc.sbuf_tensor("t2_sb", [128, QT], F32))
            krot = [e1(nc.sbuf_tensor(f"krot{r}", [128, QT], BF16)) for r in range(2)]
            vext = [e1(nc.sbuf_tensor(f"vext{r}", [128, 1040], BF16)) for r in range(2)]

            pLN1 = e1(nc.psum_tensor("pLN1", [128, 512], F32))
            pLN2 = e1(nc.psum_tensor("pLN2", [128, 512], F32))
            pB1 = e1(nc.psum_tensor("pB1", [128, 512], F32))
            pB2 = e1(nc.psum_tensor("pB2", [128, 512], F32))
            pK = [e1(nc.psum_tensor(f"pK{r}", [128, 512], F32)) for r in range(2)]
            pW = [e1(nc.psum_tensor(f"pW{r}", [128, 512], F32)) for r in range(2)]
            ln1.update(pS1=pLN1, pS2=pLN2, pB1=pB1, pB2=pB2)

            blk1 = e1(nc.Block())
            state = {}

            def body01():
                F.c["s_c"] = max(F.c["s_c"], 16)
                F.sdma(ident_f[:, :], identf[:, :], inc=("s_c", 16))
                F.gdma(wk_sb[:, :, :], wk.rearrange("(c p) n -> p c n", p=128),
                       inc=("s_cg", 16))
                F.gdma(wv_sb[:, :, :], wv.rearrange("(c p) n -> p c n", p=128),
                       inc=("s_cg", 16))
                for dst, src in ((wlk_sb, wlk), (wlv_sb, wlv), (latk_sb, latk),
                                 (latv_sb, latv), (pswap_sb, pswap), (ident_b, identf)):
                    F.gdma(dst[:, :], src[:, :], inc=("s_cg", 16))
                F.gdma(sel16[:, :, :], sel16i[:, :, :], inc=("s_cg", 16))
                F.sdma(eh_sb[:, :, :], ehi[:, :, :], inc=("s_c", 16))
                n_cg = F.c["s_cg"]
                n_c = F.c["s_c"]
                F.gmemset(ones_bf[:, :], 1.0)
                F.gmemset(ones1f[:, :], 1.0)
                F.gmemset(eps_sb[:, :], EPS)
                for r in range(2):
                    F.gmemset(vext[r][:, :], 0.0)
                F.gmemset(lv_ext[:, :], 0.0, inc=("s_mk", 1))
                zdone = F.c["s_mk"]
                first = True
                for r in range(2):
                    F.gmemset(vext[r][:, :].rearrange("p (g x) -> p g x", x=65)[:, :, 64:65],
                              1.0, waits=[("s_mk", zdone)] if first else ())
                    first = False
                F.gmemset(lv_ext[:, :].rearrange("p (g x) -> p g x", x=65)[:, :, 64:65],
                          1.0, inc=("s_mk", 1))
                ms_done = ("s_mk", F.c["s_mk"])

                for pc in range(NPAIR):
                    F.mm(pK[0][:, 0:NLAT], wlk_sb[0:HD, 128 * pc:128 * (pc + 1)],
                         latk_sb[0:HD, :],
                         waits=[("s_cg", n_cg), ms_done, ("s_act", F.c["s_act"])],
                         inc=("s_pe", 1))
                    F.acopy(lk_sb[:, pc, :], pK[0][:, 0:NLAT],
                            waits=[("s_pe", F.c["s_pe"])], inc=("s_act", 1))
                F.mm(pK[1][0:NLAT, 0:512], latv_sb[0:HD, :], wlv_sb[0:HD, 0:512],
                     waits=[("s_cg", n_cg), ms_done])
                F.mm(pW[0][0:NLAT, 0:512], latv_sb[0:HD, :], wlv_sb[0:HD, 512:1024],
                     inc=("s_pe", 1))
                lvp = F.c["s_pe"]
                F.vcopy(lv_ext[:, 0:520].rearrange("p (g x) -> p g x", x=65)[:, :, 0:64],
                        pK[1][0:NLAT, 0:512].rearrange("p (g x) -> p g x", x=64),
                        waits=[("s_pe", lvp), ms_done])
                F.vcopy(lv_ext[:, 520:1040].rearrange("p (g x) -> p g x", x=65)[:, :, 0:64],
                        pW[0][0:NLAT, 0:512].rearrange("p (g x) -> p g x", x=64),
                        inc=("s_dve", 1))
                lat_act = F.c["s_act"]
                lat_dve = F.c["s_dve"]

                ln1.update(free=0, sq_free=0, bc_free=0, afree=0)
                emit_ln = make_ln(ln1, QT)
                xfr = xf.rearrange("(c p) t -> p c t", p=128)
                kscr_r = k_scr.rearrange("(c p) t -> p c t", p=128)
                in_free = {0: 0, 1: 0}
                vext_free = {0: 0, 1: 0}
                krot_free = {0: 0, 1: 0}
                prs = {"pP_free": {0: lat_dve, 1: lat_dve},
                       "pW_free": {0: lat_dve, 1: lat_dve}}
                for qt in range(NQT):
                    r = qt % 2
                    gen = qt // 2 + 1
                    sname = f"s_in{r}"
                    F.sdma(xbuf[r][:, :, :], xfr[:, :, QT * qt:QT * (qt + 1)],
                           waits=[("s_dve", in_free[r])], inc=(sname, 16))
                    F.sdma(ctb1[r][:, :], ctab[:, QT * qt:QT * (qt + 1)],
                           waits=[("s_dve", in_free[r])], inc=(sname, 16))
                    F.sdma(stb1[r][:, :], stab[:, QT * qt:QT * (qt + 1)],
                           inc=(sname, 16))
                    entry0 = [(sname, 48 * gen)]
                    if qt == 0:
                        entry0 += [("s_cg", n_cg), ("s_c", n_c),
                                   ("s_dve", F.c["s_dve"])]
                    F.vcopy(xb_bf[r][:, :, :], xbuf[r][:, :, :], waits=entry0)
                    entry = [("s_dve", F.c["s_dve"])]
                    xn_done = emit_ln(
                        lambda kc, _r=r: xb_bf[_r][:, :, :] if kc is None
                        else xb_bf[_r][:, kc, :],
                        lambda kc, _r=r: xn1[_r][:, kc, :], entry)
                    if qt < NEXP:
                        F.gdma(w1b_scr[qt, :, :], w1[qt, :, :], inc=("s_wc", 16))
                    else:
                        F.gdma(w2b_scr[qt - NEXP, :, :], w2[qt - NEXP, :, :],
                               inc=("s_wc", 16))

                    for pc in range(NPAIR):
                        kr = pc % 2
                        kw = [("s_dve", max(xn_done, prs["pP_free"][kr]))]
                        if qt == 0 and pc < 2:
                            kw.append(("s_act", lat_act))
                        for kc in range(NCH):
                            F.mm(pK[kr][:, 0:QT], wk_sb[:, kc, 128 * pc:128 * (pc + 1)],
                                 xn1[r][:, kc, :],
                                 start=(kc == 0), stop=(kc == NCH - 1),
                                 waits=kw if kc == 0 else (),
                                 inc=("s_pe", 1) if kc == NCH - 1 else None)
                        F.acopy(kbf[kr][:, :], pK[kr][:, 0:QT],
                                waits=[("s_pe", F.c["s_pe"])], inc=("s_act", 1))
                        F.mm(pW[kr][:, 0:QT], pswap_sb[:, :], kbf[kr][:, :],
                             waits=[("s_act", F.c["s_act"]),
                                    ("s_dve", prs["pW_free"][kr])], inc=("s_pe", 1))
                        F.tt(t1_sb[:, :], pK[kr][:, 0:QT], ctb1[r][:, :], ALU.mult,
                             waits=[("s_pe", F.c["s_pe"])])
                        F.tt(t2_sb[:, :], pW[kr][:, 0:QT], stb1[r][:, :], ALU.mult)
                        F.tt(krot[kr][:, :], t1_sb[:, :], t2_sb[:, :], ALU.add,
                             waits=[(f"s_sk{kr}", krot_free[kr])])
                        prs["pP_free"][kr] = F.c["s_dve"]
                        prs["pW_free"][kr] = F.c["s_dve"]
                        F.gdma(kscr_r[:, pc, QT * qt:QT * (qt + 1)], krot[kr][:, :],
                               waits=[("s_dve", F.c["s_dve"])], inc=(f"s_sk{kr}", 16))
                        krot_free[kr] = F.c[f"s_sk{kr}"]
                        if debug and qt == 0 and pc == 0:
                            F.dbg_tap("krot0", dbg["krot0"][:, :], krot[kr][:, :],
                                   waits=[("s_dve", F.c["s_dve"])])
                    for tb in range(NB):
                        vr = tb % 2
                        for vt in range(2):
                            pv = pB1 if vt == 0 else pB2
                            for kc in range(NCH):
                                F.mm(pv[:, 0:512],
                                     xn1[r][:, kc, 128 * tb:128 * (tb + 1)],
                                     wv_sb[:, kc, 512 * vt:512 * (vt + 1)],
                                     start=(kc == 0), stop=(kc == NCH - 1),
                                     waits=[("s_dve", F.c["s_dve"]),
                                            (f"s_sv{vr}", vext_free[vr])]
                                     if kc == 0 else (),
                                     inc=("s_pe", 1) if kc == NCH - 1 else None)
                            F.vcopy(
                                vext[vr][:, 520 * vt:520 * (vt + 1)]
                                .rearrange("p (g x) -> p g x", x=65)[:, :, 0:64],
                                pv[:, 0:512].rearrange("p (g x) -> p g x", x=64),
                                waits=[("s_pe", F.c["s_pe"])],
                                inc=("s_dve", 1) if vt == 1 else None)
                        F.gdma(v_scr[QT * qt + 128 * tb:QT * qt + 128 * (tb + 1), :],
                               vext[vr][:, :],
                               waits=[("s_dve", F.c["s_dve"])], inc=(f"s_sv{vr}", 16))
                        vext_free[vr] = F.c[f"s_sv{vr}"]
                        ln1["bc_free"] = F.c["s_dve"]
                        if debug and qt == 0 and tb == 0:
                            F.dbg_tap("vext0", dbg["vext0"][:, :], vext[vr][:, :],
                                   waits=[("s_dve", F.c["s_dve"])])
                    in_free[r] = F.c["s_dve"]
                    if debug and qt == 0:
                        F.dbg_tap("xn0", dbg["xn0"][:, :, :], xn1[r][:, :, :],
                               waits=[("s_dve", xn_done)])

            run_block(blk1, body01)
            p1_st = [("s_sk0", F.c["s_sk0"]), ("s_sk1", F.c["s_sk1"]),
                     ("s_sv0", F.c["s_sv0"]), ("s_sv1", F.c["s_sv1"])]

        with ExitStack() as p2a:
            e2 = p2a.enter_context
            wq_sb = e2(nc.sbuf_tensor("wq_sb", [128, NCH, DM], BF16))
            xqb = [e2(nc.sbuf_tensor(f"xqb{r}", [128, NCH, QT], F32)) for r in range(2)]
            xq_bf = [e2(nc.sbuf_tensor(f"xq_bf{r}", [128, NCH, QT], BF16)) for r in range(2)]
            ctb2 = [e2(nc.sbuf_tensor(f"ctb2{r}", [128, QT], F32)) for r in range(2)]
            stb2 = [e2(nc.sbuf_tensor(f"stb2{r}", [128, QT], F32)) for r in range(2)]
            xn2t = e2(nc.sbuf_tensor("xn2t", [128, NCH, QT], BF16))
            ln2 = {
                "sq": e2(nc.sbuf_tensor("sq2", [128, NCH, QT], BF16)),
                "mean": e2(nc.sbuf_tensor("mean2", [1, QT], F32)),
                "var": e2(nc.sbuf_tensor("var2", [1, QT], F32)),
                "tsc": e2(nc.sbuf_tensor("tsc2", [1, QT], F32)),
                "rstd": e2(nc.sbuf_tensor("rstd2", [1, QT], F32)),
                "tmp": e2(nc.sbuf_tensor("tmp2", [128, QT], F32)),
            }
            qbf = [e2(nc.sbuf_tensor(f"qbf{r}", [128, QT], BF16)) for r in range(2)]
            tq1 = e2(nc.sbuf_tensor("tq1", [128, QT], F32))
            tq2 = e2(nc.sbuf_tensor("tq2", [128, QT], F32))
            qrot = [e2(nc.sbuf_tensor(f"qrot{r}", [128, QT], BF16)) for r in range(2)]

            pLQ1 = e2(nc.psum_tensor("pLQ1", [128, QT], F32))
            pLQ2 = e2(nc.psum_tensor("pLQ2", [128, QT], F32))
            pBQ1 = e2(nc.psum_tensor("pBQ1", [128, QT], F32))
            pBQ2 = e2(nc.psum_tensor("pBQ2", [128, QT], F32))
            pKQ = [e2(nc.psum_tensor(f"pKQ{r}", [128, QT], F32)) for r in range(2)]
            pWQ = [e2(nc.psum_tensor(f"pWQ{r}", [128, QT], F32)) for r in range(2)]
            ln2.update(pS1=pLQ1, pS2=pLQ2, pB1=pBQ1, pB2=pBQ2)

            blk2 = e2(nc.Block())

            def body2a():
                F.gdma(wq_sb[:, :, :], wq.rearrange("(c p) n -> p c n", p=128),
                       inc=("s_cg", 16))
                n_cg2 = F.c["s_cg"]
                ln2.update(free=0, sq_free=0, bc_free=0, afree=0)
                emit_ln = make_ln(ln2, QT)
                xfr = xf.rearrange("(c p) t -> p c t", p=128)
                qscr_r = q_scr.rearrange("(c p) t -> p c t", p=128)

                def var_body(vq):
                    qs = qset(vq)
                    in_free = {0: 0, 1: 0}
                    qrot_free = {0: 0, 1: 0}
                    prs = {"pP_free": {0: 0, 1: 0}, "pW_free": {0: 0, 1: 0}}
                    for i, Q in enumerate(qs):
                        r = i % 2
                        gen = i // 2 + 1
                        sname = f"s_qin{r}"
                        F.sdma(xqb[r][:, :, :], xfr[:, :, QT * Q:QT * (Q + 1)],
                               waits=[("s_dve", in_free[r])], inc=(sname, 16))
                        F.sdma(ctb2[r][:, :], ctab[:, QT * Q:QT * (Q + 1)],
                               waits=[("s_dve", in_free[r])], inc=(sname, 16))
                        F.sdma(stb2[r][:, :], stab[:, QT * Q:QT * (Q + 1)],
                               inc=(sname, 16))
                        entry0 = [(sname, 48 * gen)]
                        if i == 0:
                            entry0 += [("s_cg", n_cg2)]
                        F.vcopy(xq_bf[r][:, :, :], xqb[r][:, :, :], waits=entry0)
                        entry = [("s_dve", F.c["s_dve"])]
                        xn_done = emit_ln(
                            lambda kc, _r=r: xq_bf[_r][:, :, :] if kc is None
                            else xq_bf[_r][:, kc, :],
                            lambda kc: xn2t[:, kc, :], entry)
                        for pc in range(NPAIR):
                            kr = pc % 2
                            for kc in range(NCH):
                                F.mm(pKQ[kr][:, :], wq_sb[:, kc, 128 * pc:128 * (pc + 1)],
                                     xn2t[:, kc, :],
                                     start=(kc == 0), stop=(kc == NCH - 1),
                                     waits=[("s_dve", max(xn_done, prs["pP_free"][kr]))]
                                     if kc == 0 else (),
                                     inc=("s_pe", 1) if kc == NCH - 1 else None)
                            F.acopy(qbf[kr][:, :], pKQ[kr][:, :],
                                    waits=[("s_pe", F.c["s_pe"])], inc=("s_act", 1))
                            F.mm(pWQ[kr][:, :], pswap_sb[:, :], qbf[kr][:, :],
                                 waits=[("s_act", F.c["s_act"]),
                                        ("s_dve", prs["pW_free"][kr])], inc=("s_pe", 1))
                            F.tt(tq1[:, :], pKQ[kr][:, :], ctb2[r][:, :], ALU.mult,
                                 waits=[("s_pe", F.c["s_pe"])])
                            F.tt(tq2[:, :], pWQ[kr][:, :], stb2[r][:, :], ALU.mult)
                            F.tt(qrot[kr][:, :], tq1[:, :], tq2[:, :], ALU.add,
                                 waits=[(f"s_sqr{kr}", qrot_free[kr])])
                            prs["pP_free"][kr] = F.c["s_dve"]
                            prs["pW_free"][kr] = F.c["s_dve"]
                            F.gdma(qscr_r[:, pc, QT * i:QT * (i + 1)], qrot[kr][:, :],
                                   waits=[("s_dve", F.c["s_dve"])],
                                   inc=(f"s_sqr{kr}", 16))
                            qrot_free[kr] = F.c[f"s_sqr{kr}"]
                        in_free[r] = F.c["s_dve"]

                F.variants(var_body, state=[ln2])

            run_block(blk2, body2a)
            q_st = [("s_sqr0", F.c["s_sqr0"]), ("s_sqr1", F.c["s_sqr1"])]

        with ExitStack() as p2b:
            e3 = p2b.enter_context
            wo_sb = e3(nc.sbuf_tensor("wo_sb", [64, HEADS, DM], BF16))
            dmask_sb = e3(nc.sbuf_tensor("dmask_sb", [128, NB, QT], BF16))
            qsl = [e3(nc.sbuf_tensor(f"qsl{r}", [128, NCH, QT], BF16)) for r in range(2)]
            kseg = [e3(nc.sbuf_tensor(f"kseg{r}", [128, NCH, SEGC * 128], BF16))
                    for r in range(3)]
            vseg = [e3(nc.sbuf_tensor(f"vseg{r}", [128, SEGC, 1040], BF16))
                    for r in range(3)]
            probs = [e3(nc.sbuf_tensor(f"probs{r}", [128, 2 * QT], BF16))
                     for r in range(3)]
            attnU = e3(nc.sbuf_tensor("attnU", [65, HEADS, QT], F32))
            attnb = e3(nc.sbuf_tensor("attnb", [64, HEADS, QT], BF16))
            recip_t = e3(nc.sbuf_tensor("recip_t", [16, QT], BF16))
            recip_f = e3(nc.sbuf_tensor("recip_f", [16, QT], F32))
            xres = e3(nc.sbuf_tensor("xres", [128, NCH, QT], F32))

            pS = [e3(nc.psum_tensor(f"pS{r}", [128, 2 * QT], F32)) for r in range(2)]
            pAV = [e3(nc.psum_tensor(f"pAV{hh}", [128, QT], F32)) for hh in range(2)]
            pDN = e3(nc.psum_tensor("pDN", [64, QT], F32))
            pRC = e3(nc.psum_tensor("pRC", [64, QT], F32))

            blk3 = e3(nc.Block())

            def body2b():
                F.gdma(wo_sb[0:64, :, :], wo.rearrange("(h d) n -> d h n", d=64),
                       inc=("s_cg", 16))
                F.gdma(dmask_sb[:, :, :], dmask[:, :, :], inc=("s_cg", 16))
                n_cg3 = F.c["s_cg"]
                xfr = xf.rearrange("(c p) t -> p c t", p=128)
                qscr_r = q_scr.rearrange("(c p) t -> p c t", p=128)
                kscr_r = k_scr.rearrange("(c p) t -> p c t", p=128)
                vscr_r = v_scr.rearrange("(n p) c -> p n c", p=128)
                x2r = x2_scr.rearrange("(c p) t -> p c t", p=128)

                def var_body(vq):
                    qs = qset(vq)
                    qsl_free = {0: 0, 1: 0}
                    kv_free = {0: 0, 1: 0, 2: 0}
                    kv_gen = {0: 0, 1: 0, 2: 0}
                    seg_ctr = [0]
                    sl_of = {}
                    slot_exp = {0: 0, 1: 0}
                    xres_free = 0
                    stx_free = 0
                    ps_guard = 0
                    av_evac = 0
                    unit = 0
                    for i, Q in enumerate(qs):
                        qr = i % 2
                        F.sdma(qsl[qr][:, :, :], qscr_r[:, :, QT * i:QT * (i + 1)],
                               waits=[("s_pe", qsl_free[qr])] + q_st,
                               inc=(f"s_q{qr}", 16))
                        q_ready = (f"s_q{qr}", 16 * (i // 2 + 1))
                        F.sdma(xres[:, :, :], xfr[:, :, QT * Q:QT * (Q + 1)],
                               waits=[("s_dve", xres_free), ("s_stx", stx_free)],
                               inc=("s_xr", 16))
                        xres_ready = ("s_xr", F.c["s_xr"])

                        extent = NB * (Q + 1)
                        nseg = (extent + SEGC - 1) // SEGC

                        seg_rdy = {}

                        def load_seg(g):
                            if g >= nseg:
                                return
                            sl = seg_ctr[0] % 3
                            seg_ctr[0] += 1
                            sl_of[g] = sl
                            c0, c1 = SEGC * g, min(SEGC * (g + 1), extent)
                            F.sdma(kseg[sl][:, :, 0:128 * (c1 - c0)],
                                   kscr_r[:, :, 128 * c0:128 * c1],
                                   waits=[("s_pe", kv_free[sl])] + p1_st,
                                   inc=(f"s_kv{sl}", 16))
                            F.sdma(vseg[sl][:, 0:c1 - c0, :], vscr_r[:, c0:c1, :],
                                   inc=(f"s_kv{sl}", 16))
                            kv_gen[sl] += 1
                            seg_rdy[g] = (f"s_kv{sl}", 32 * kv_gen[sl])

                        units = []
                        for g in range(nseg):
                            c0, c1 = SEGC * g, min(SEGC * (g + 1), extent)
                            for pr in range(NPAIR):
                                if g == 0:
                                    units.append(("lat", g, pr, -1, True, False))
                                for sc in range(c0, c1):
                                    units.append(("sc", g, pr, sc,
                                                  g > 0 and sc == c0,
                                                  sc == c1 - 1))

                        load_seg(0)
                        load_seg(1)
                        pend = []
                        first_of_qtile = True

                        def emit_av(p):
                            nonlocal av_evac
                            (pbr, kind, g, pr, sc, gstart, gstop, avw) = p
                            sl = sl_of[g]
                            if gstart:
                                avw = avw + [("s_dve", av_evac)]
                            for hh in range(2):
                                if kind == "lat":
                                    lhsT = lv_ext[:, 130 * pr + 65 * hh:
                                                  130 * pr + 65 * (hh + 1)]
                                    rhs = probs[pbr][0:NLAT, QT * hh:QT * (hh + 1)]
                                else:
                                    lhsT = vseg[sl][:, sc - SEGC * g,
                                                    130 * pr + 65 * hh:
                                                    130 * pr + 65 * (hh + 1)]
                                    rhs = probs[pbr][:, QT * hh:QT * (hh + 1)]
                                F.mm(pAV[hh][0:65, :], lhsT, rhs,
                                     start=gstart, stop=gstop,
                                     waits=avw if hh == 0 else (),
                                     inc=("s_pe", 1) if hh == 1 else None)
                            if gstop:
                                kv_free[sl] = F.c["s_pe"]
                                av_done = F.c["s_pe"]
                                for hh in range(2):
                                    h = 2 * pr + hh
                                    if g == 0:
                                        F.vcopy(attnU[:, h, :], pAV[hh][0:65, :],
                                                waits=[("s_pe", av_done)]
                                                if hh == 0 else ())
                                    else:
                                        F.tt(attnU[:, h, :], attnU[:, h, :],
                                             pAV[hh][0:65, :], ALU.add,
                                             waits=[("s_pe", av_done)]
                                             if hh == 0 else ())
                                av_evac = F.c["s_dve"]

                        cur_g = 0
                        unit_in_seg = 0
                        for (kind, g, pr, sc, gstart, gstop) in units:
                            if g != cur_g:
                                cur_g = g
                                unit_in_seg = 0
                            if unit_in_seg == 2:
                                load_seg(g + 2)
                            sl = sl_of[g]
                            seg_ready = seg_rdy[g]
                            psr = unit % 2
                            pbr = unit % 3
                            if kind == "lat":
                                for hh in range(2):
                                    h0 = 64 * hh
                                    F.mm(pS[psr][0:NLAT, QT * hh:QT * (hh + 1)],
                                         lk_sb[h0:h0 + 64, pr, :],
                                         qsl[qr][h0:h0 + 64, pr, :],
                                         waits=[q_ready, ("s_act", slot_exp[psr])]
                                         if hh == 0 else (),
                                         inc=("s_pe", 1) if hh == 1 else None)
                            else:
                                for hh in range(2):
                                    h0 = 64 * hh
                                    F.mm(pS[psr][:, QT * hh:QT * (hh + 1)],
                                         kseg[sl][h0:h0 + 64, pr,
                                                  128 * (sc - SEGC * g):
                                                  128 * (sc - SEGC * g + 1)],
                                         qsl[qr][h0:h0 + 64, pr, :],
                                         waits=[q_ready, seg_ready,
                                                ("s_act", slot_exp[psr])]
                                         if hh == 0 else (),
                                         inc=("s_pe", 1) if hh == 1 else None)
                            sc_done = F.c["s_pe"]
                            if kind == "lat":
                                F.act(probs[pbr][0:NLAT, :], pS[psr][0:NLAT, :],
                                      AF.Exp, scale=SCALE,
                                      waits=[("s_pe", sc_done)])
                            else:
                                F.act(probs[pbr][:, :], pS[psr][:, :], AF.Exp,
                                      scale=SCALE, waits=[("s_pe", sc_done)])
                            slot_exp[psr] = F.c["s_act"]
                            avw = [("s_act", F.c["s_act"])]
                            if kind == "sc" and sc >= NB * Q:
                                j = sc - NB * Q
                                for hh in range(2):
                                    F.tt(probs[pbr][:, QT * hh:QT * (hh + 1)],
                                         probs[pbr][:, QT * hh:QT * (hh + 1)],
                                         dmask_sb[:, j, :], ALU.mult,
                                         waits=[("s_act", F.c["s_act"]),
                                                ("s_cg", n_cg3)]
                                         if hh == 0 else ())
                                avw.append(("s_dve", F.c["s_dve"]))
                            if first_of_qtile:
                                avw.append(("s_dve", ps_guard))
                                first_of_qtile = False
                            pend.append((pbr, kind, g, pr, sc, gstart, gstop, avw))
                            if len(pend) > 2:
                                emit_av(pend.pop(0))
                            unit += 1
                            unit_in_seg += 1
                        while pend:
                            emit_av(pend.pop(0))
                        attnU_done = F.c["s_dve"]

                        for h in range(16):
                            F.mm(pDN[0:16, :], eh_sb[64:65, h, :], attnU[64:65, h, :],
                                 start=(h == 0), stop=(h == 15),
                                 waits=[("s_dve", attnU_done)] if h == 0 else (),
                                 inc=("s_pe", 1) if h == 15 else None)
                        F.vrecip(recip_f[:, :], pDN[0:16, :],
                                 waits=[("s_pe", F.c["s_pe"])])
                        F.vcopy(recip_t[:, :], recip_f[:, :])
                        rec_done = F.c["s_dve"]
                        if debug and i == 0:
                            F.dbg_tap("attnU0", dbg["attnU0"][:, :], attnU[:, 0, :],
                                   waits=[("s_dve", attnU_done)])
                        for h in range(16):
                            po = pRC if h % 2 == 0 else pDN
                            F.mm(po[:, :], sel16[:, h, :], recip_t[0:16, :],
                                 waits=[("s_dve", rec_done if h < 2 else F.c["s_dve"])],
                                 inc=("s_pe", 1))
                            F.tt(attnb[0:64, h, :], attnU[0:64, h, :], po[0:64, :],
                                 ALU.mult, waits=[("s_pe", F.c["s_pe"])],
                                 inc=("s_dve", 1))
                        nrm_done = F.c["s_dve"]
                        if debug and i == 0:
                            F.dbg_tap("attnb0", dbg["attnb0"][:, :, :], attnb[:, :, :],
                                   waits=[("s_dve", nrm_done)])
                        for oc in range(NCH):
                            po = pAV[oc % 2]
                            for h in range(16):
                                F.mm(po[:, :], wo_sb[0:64, h, 128 * oc:128 * (oc + 1)],
                                     attnb[0:64, h, :],
                                     start=(h == 0), stop=(h == 15),
                                     waits=[("s_dve", nrm_done), ("s_cg", n_cg3),
                                            ("s_act", F.c["s_act"]),
                                            ("s_dve", F.c["s_dve"])] if h == 0 else (),
                                     inc=("s_pe", 1) if h == 15 else None)
                            xw = [("s_pe", F.c["s_pe"]), xres_ready]
                            if oc == 0:
                                xw.append(("s_stx", stx_free))
                            F.tt(xres[:, oc, :], xres[:, oc, :], po[:, :], ALU.add,
                                 waits=xw, inc=("s_dve", 1))
                        xres_free = F.c["s_dve"]
                        ps_guard = F.c["s_dve"]
                        qsl_free[qr] = F.c["s_pe"]
                        F.gdma(x2r[:, :, QT * i:QT * (i + 1)], xres[:, :, :],
                               waits=[("s_dve", F.c["s_dve"])], inc=("s_stx", 16))
                        stx_free = F.c["s_stx"]
                        if debug and i == 0:
                            F.dbg_tap("x20", dbg["x20"][:, :, :], xres[:, :, :],
                                   waits=[("s_dve", F.c["s_dve"])])

                F.variants(var_body)

            run_block(blk3, body2b)
            x2_st = F.c["s_stx"]

        CAP = 384
        CAPF = 2 * CAP
        NSB = CAP // 128
        NBH = HALF // 128
        BOUND = NEXP * CAPF - 1
        with ExitStack() as p3s:
            e4 = p3s.enter_context
            wg_sb = e4(nc.sbuf_tensor("wg_sb", [128, NCH, NEXP], BF16))
            eq1s = e4(nc.sbuf_tensor("eq1s", [128, 2 * NBH, NEXP], F32))
            eq2s = e4(nc.sbuf_tensor("eq2s", [128, 2 * NBH, NEXP], F32))
            wp = e4(nc.sbuf_tensor("wp", [128, 2 * NBH, 2], F32))
            gis = e4(nc.sbuf_tensor("gis", [128, 2 * NBH, 2], I32))
            p3st = {}

            with ExitStack() as pa:
                ea = pa.enter_context
                acc = ea(nc.sbuf_tensor("accA", [128, NCH, HALF], F32))
                x2bh = ea(nc.sbuf_tensor("x2bh", [128, NCH, HALF], BF16))
                xn3 = ea(nc.sbuf_tensor("xn3", [128, NCH, HALF], BF16))
                xntm = ea(nc.sbuf_tensor("xntm", [128, NBH, DM], BF16))
                ln3 = {
                    "sq": ea(nc.sbuf_tensor("sq3", [128, NCH, QT], BF16)),
                    "mean": ea(nc.sbuf_tensor("mean3", [1, QT], F32)),
                    "var": ea(nc.sbuf_tensor("var3", [1, QT], F32)),
                    "tsc": ea(nc.sbuf_tensor("tsc3", [1, QT], F32)),
                    "rstd": ea(nc.sbuf_tensor("rstd3", [1, QT], F32)),
                    "tmp": ea(nc.sbuf_tensor("tmp3", [128, QT], F32)),
                }
                lgb = ea(nc.sbuf_tensor("lgb", [NEXP, QT], BF16))
                lgtm = ea(nc.sbuf_tensor("lgtm", [128, NEXP], F32))
                m1t = ea(nc.sbuf_tensor("m1t", [128, 1], F32))
                m2t = ea(nc.sbuf_tensor("m2t", [128, 1], F32))
                selb = ea(nc.sbuf_tensor("selb", [128, NEXP], BF16))
                zt = ea(nc.sbuf_tensor("zt", [128, 1], F32))
                rt = ea(nc.sbuf_tensor("rt", [128, 1], F32))
                selT = ea(nc.sbuf_tensor("selT", [NEXP, 2 * HALF], F32))
                posT = ea(nc.sbuf_tensor("posT", [NEXP, 2 * HALF], F32))
                offtm = ea(nc.sbuf_tensor("offtm", [128, NEXP], F32))
                gf = ea(nc.sbuf_tensor("gf", [128, 2], F32))
                ztile = ea(nc.sbuf_tensor("ztile", [128, DM], BF16))
                pU = [ea(nc.psum_tensor(f"pUa{r}", [128, 512], F32)) for r in range(2)]
                pV = [ea(nc.psum_tensor(f"pVa{r}", [128, 512], F32)) for r in range(2)]
                pX = ea(nc.psum_tensor("pXa", [128, 512], F32))
                pT = [ea(nc.psum_tensor(f"pTa{r}", [128, 1024], BF16))
                      for r in range(2)]
                ln3.update(pS1=pU[0], pS2=pU[1], pB1=pV[0], pB2=pV[1])
                blk4a = ea(nc.Block())

                def body3a():
                    F.gdma(wg_sb[:, :, :], wgt.rearrange("(c p) n -> p c n", p=128),
                           inc=("s_cg", 16))
                    F.gdma(ecap_sb[:, :], ecap[:, :], inc=("s_cg", 16))
                    n_cg4 = F.c["s_cg"]
                    ln3.update(free=0, sq_free=0, bc_free=0, afree=0)
                    emit_ln = make_ln(ln3, QT)
                    x2r = x2_scr.rearrange("(c p) t -> p c t", p=128)
                    gsr_a = g_scr.rearrange("(a p) n -> p a n", p=128)
                    F.gmemset(ztile[:, :], 0.0)
                    zdone = F.c["s_mk"]
                    for a_i in range(NEXP * CAPF // 128):
                        F.gdma(gsr_a[:, a_i, :], ztile[:, :],
                               waits=[("s_mk", zdone)] if a_i == 0 else (),
                               inc=("s_sc", 16))
                    zfill = F.c["s_sc"]
                    a_cast = 0
                    for hf in (range(2) if P3MODE & 32 else []):
                        sc_prev = F.c["s_sc"]
                        F.sdma(acc[:, :, :], x2r[:, :, HALF * hf:HALF * (hf + 1)],
                               waits=[("s_stx", x2_st), ("s_dve", a_cast)],
                               inc=("s_x2l", 16))
                        F.vcopy(x2bh[:, :, :], acc[:, :, :],
                                waits=[("s_x2l", F.c["s_x2l"])])
                        a_cast = F.c["s_dve"]
                        cast_done = F.c["s_dve"]
                        for tt_i in range(NTT):
                            sl = slice(QT * tt_i, QT * (tt_i + 1))
                            if not (P3MODE & 128):
                                continue
                            xn_done = emit_ln(
                                lambda kc, _s=sl: x2bh[:, :, _s] if kc is None
                                else x2bh[:, kc, _s],
                                lambda kc, _s=sl: xn3[:, kc, _s],
                                [("s_dve", cast_done)])
                            if not (P3MODE & 256):
                                continue
                            for kc in range(NCH):
                                F.mm(pU[0][0:NEXP, 0:QT], wg_sb[:, kc, :],
                                     xn3[:, kc, sl],
                                     start=(kc == 0), stop=(kc == NCH - 1),
                                     waits=[("s_dve", xn_done), ("s_cg", n_cg4),
                                            ("s_act", F.c["s_act"])]
                                     if kc == 0 else (),
                                     inc=("s_pe", 1) if kc == NCH - 1 else None)
                            F.acopy(lgb[:, :], pU[0][0:NEXP, 0:QT],
                                    waits=[("s_pe", F.c["s_pe"])], inc=("s_act", 1))
                            ln3["afree"] = F.c["s_act"]
                            lg_done = F.c["s_act"]
                            for tb in (range(QT // 128) if P3MODE & 8 else []):
                                B = NBH * hf + (QT // 128) * tt_i + tb
                                bb = (QT // 128) * tt_i + tb
                                tsl = slice(128 * tb, 128 * (tb + 1))
                                F.tp(pT[0][:, 0:NEXP], lgb[0:NEXP, tsl],
                                     ident_b[0:NEXP, 0:NEXP],
                                     waits=[("s_act", lg_done),
                                            ("s_dve", F.c["s_dve"])],
                                     inc=("s_pe", 1))
                                F.vcopy(lgtm[:, :], pT[0][:, 0:NEXP],
                                        waits=[("s_pe", F.c["s_pe"])])
                                F.vmax(m1t[:, :], lgtm[:, :])
                                F.ts(eq1s[:, B, :], lgtm[:, :], m1t[:, 0:1],
                                     ALU.is_ge)
                                F.ts(eq2s[:, B, :], eq1s[:, B, :], -1e30, ALU.mult)
                                F.tt(eq2s[:, B, :], lgtm[:, :], eq2s[:, B, :],
                                     ALU.add)
                                F.vmax(m2t[:, :], eq2s[:, B, :])
                                F.tt(zt[:, :], m2t[:, :], m1t[:, :], ALU.subtract,
                                     inc=("s_dve", 1))
                                F.act(zt[:, :], zt[:, :], AF.Exp,
                                      waits=[("s_dve", F.c["s_dve"])],
                                      inc=("s_act", 1))
                                F.ts(rt[:, :], zt[:, :], 1.0, ALU.add,
                                     waits=[("s_act", F.c["s_act"])])
                                F.vrecip(rt[:, :], rt[:, :])
                                F.ts(eq2s[:, B, :], lgtm[:, :], m2t[:, 0:1],
                                     ALU.is_ge)
                                F.tt(eq2s[:, B, :], eq2s[:, B, :], eq1s[:, B, :],
                                     ALU.subtract)
                                F.vcopy(wp[:, B, 0:1], rt[:, :])
                                F.tt(wp[:, B, 1:2], zt[:, :], rt[:, :], ALU.mult)
                                F.tt(selb[:, :], eq1s[:, B, :], eq2s[:, B, :],
                                     ALU.add, inc=("s_dve", 1))
                                F.tp(pT[0][0:NEXP, 16:144], selb[:, :],
                                     ident_b[:, :],
                                     waits=[("s_dve", F.c["s_dve"])],
                                     inc=("s_pe", 1))
                                F.vcopy(selT[:, 128 * B:128 * (B + 1)],
                                        pT[0][0:NEXP, 16:144],
                                        waits=[("s_pe", F.c["s_pe"])])
                                for kc in range(NCH):
                                    F.tp(pT[1][:, 128 * kc:128 * (kc + 1)],
                                         xn3[:, kc, sl][:, tsl], ident_b[:, :],
                                         waits=[("s_dve", xn_done),
                                                ("s_dve", F.c["s_dve"])]
                                         if kc == 0 else (),
                                         inc=("s_pe", 1) if kc == NCH - 1 else None)
                                F.vcopy(xntm[:, bb, :], pT[1][:, 0:1024],
                                        waits=[("s_pe", F.c["s_pe"]),
                                               ("s_sc", sc_prev)],
                                        inc=("s_dve", 1))
                        if not (P3MODE & 16):
                            continue
                        hs = slice(HALF * hf, HALF * (hf + 1))
                        F.vscan(posT[:, hs], selT[:, hs], selT[:, hs], 0.0,
                                ALU.add, ALU.bypass)
                        F.ts(selT[:, hs], posT[:, hs], ecap_sb[:, hf:hf + 1],
                             ALU.add)
                        F.ts(posT[:, hs], posT[:, hs], float(CAP), ALU.is_gt)
                        F.ts(posT[:, hs], posT[:, hs], 1e7, ALU.mult)
                        F.tt(selT[:, hs], selT[:, hs], posT[:, hs], ALU.add,
                             inc=("s_dve", 1))
                        off_done = F.c["s_dve"]
                        for Bl in range(NBH * hf, NBH * (hf + 1)):
                            F.tp(pX[:, 0:NEXP], selT[:, 128 * Bl:128 * (Bl + 1)],
                                 ident_f[0:NEXP, 0:NEXP],
                                 waits=[("s_dve", off_done),
                                        ("s_dve", F.c["s_dve"])], inc=("s_pe", 1))
                            F.vcopy(offtm[:, :], pX[:, 0:NEXP],
                                    waits=[("s_pe", F.c["s_pe"])])
                            F.tt(lgtm[:, :], offtm[:, :], eq1s[:, Bl, :], ALU.mult)
                            F.vsum(gf[:, 0:1], lgtm[:, :])
                            F.tt(lgtm[:, :], offtm[:, :], eq2s[:, Bl, :], ALU.mult)
                            F.vsum(gf[:, 1:2], lgtm[:, :])
                            F.vcopy(gis[:, Bl, :], gf[:, :], inc=("s_dve", 1))
                            gdone = F.c["s_dve"]
                            if P3MODE & 1:
                                for rk in range(2):
                                    F.scat(g_scr[:, :], gis[:, Bl, rk:rk + 1],
                                           xntm[:, Bl - NBH * hf, :], BOUND,
                                           waits=[("s_dve", gdone),
                                                  ("s_sc", zfill)],
                                           inc=("s_sc", 16))
                    p3st["sc_all"] = F.c["s_sc"]

                run_block(blk4a, body3a)

            with ExitStack() as pb:
                eb = pb.enter_context
                w1_sb = eb(nc.sbuf_tensor("w1_sb", [128, NCH, FFND], BF16))
                w2_sb = eb(nc.sbuf_tensor("w2_sb", [128, NCH, FFND], BF16))
                w3_sb = eb(nc.sbuf_tensor("w3_sb", [128, NFC, DM], BF16))
                gtm = eb(nc.sbuf_tensor("gtm", [128, 2 * NSB, DM], BF16))
                gxn = eb(nc.sbuf_tensor("gxn", [128, NCH, CAPF], BF16))
                h_sb = eb(nc.sbuf_tensor("h_sb", [128, NFC, CAP], BF16))
                su_sb = [eb(nc.sbuf_tensor(f"su{r}", [128, CAP], BF16))
                         for r in range(2)]
                yf_sb = eb(nc.sbuf_tensor("yf_sb", [128, CAP], BF16))
                ytm = eb(nc.sbuf_tensor("ytm", [128, NSB, DM], BF16))
                pU = [eb(nc.psum_tensor(f"pUb{r}", [128, 512], F32))
                      for r in range(2)]
                pV = [eb(nc.psum_tensor(f"pVb{r}", [128, 512], F32))
                      for r in range(2)]
                pYF = [eb(nc.psum_tensor(f"pYb{r}", [128, 512], F32))
                       for r in range(2)]
                pT = [eb(nc.psum_tensor(f"pTb{r}", [128, 1024], BF16))
                      for r in range(2)]
                blk4b = eb(nc.Block())

                def body3b():
                    w1r = [w1b_scr[ex, :, :].rearrange("(c p) n -> p c n", p=128)
                           for ex in range(NEXP)]
                    w2r = [w2b_scr[ex, :, :].rearrange("(c p) n -> p c n", p=128)
                           for ex in range(NEXP)]
                    w3r = [w3[ex, :, :].rearrange("(c p) n -> p c n", p=128)
                           for ex in range(NEXP)]
                    wc_done = [("s_wc", 16 * 2 * NEXP)]
                    gsr = g_scr.rearrange("(a p) n -> p a n", p=128)
                    ysr = y_scr.rearrange("(a p) n -> p a n", p=128)
                    sc_all = [("s_sc", p3st["sc_all"])]
                    tp_free = {k: 0 for k in range(2)}
                    tslot = [0]

                    def tp_ring(in_ap, waits=()):
                        k = tslot[0] % 2
                        tslot[0] += 1
                        reg = pT[k][:, 512:640]
                        F.tp(reg, in_ap, ident_b[:, :],
                             waits=list(waits) + [("s_dve", tp_free[k])],
                             inc=("s_pe", 1))
                        return reg, k

                    w_free = {"w1": 0, "w2": 0, "w3": 0}
                    wgen = {"w1": 0, "w2": 0, "w3": 0}

                    def load_weights(ex):
                        for nm, src, dst in (("w1", w1r[ex], w1_sb),
                                             ("w2", w2r[ex], w2_sb)):
                            F.sdma(dst[:, :, :], src,
                                   waits=[("s_pe", w_free[nm])] + wc_done,
                                   inc=(f"s_{nm}", 16))
                            wgen[nm] += 1
                        F.gdma(w3_sb[:, :, :], w3r[ex],
                               waits=[("s_pe", w_free["w3"])], inc=("s_w3", 16))
                        wgen["w3"] += 1

                    load_weights(0)
                    gxn_free = 0
                    gtm_free = 0
                    ytm_stored = 0
                    silu_free = {0: F.c["s_act"], 1: F.c["s_act"]}
                    hh_free = {0: F.c["s_dve"], 1: F.c["s_dve"]}
                    yf_freed = {0: F.c["s_act"], 1: F.c["s_act"]}
                    for ex in (range(NEXP) if P3MODE & 2 else []):
                        w1_ready = ("s_w1", 16 * wgen["w1"])
                        w2_ready = ("s_w2", 16 * wgen["w2"])
                        w3_ready = ("s_w3", 16 * wgen["w3"])
                        F.sdma(gtm[:, :, :],
                               gsr[:, 2 * NSB * ex:2 * NSB * (ex + 1), :],
                               waits=sc_all + [("s_pe", gtm_free)],
                               inc=("s_gt", 16))
                        gt_rdy = ("s_gt", F.c["s_gt"])
                        for sb in range(2 * NSB):
                            for kc in range(NCH):
                                F.tp(pT[sb % 2][:, 128 * kc:128 * (kc + 1)],
                                     gtm[:, sb, 128 * kc:128 * (kc + 1)],
                                     ident_b[:, :],
                                     waits=[gt_rdy, ("s_dve", F.c["s_dve"]),
                                            ("s_pe", gxn_free)]
                                     if kc == 0 else (),
                                     inc=("s_pe", 1) if kc == NCH - 1 else None)
                            F.vcopy(gxn[:, :, 128 * sb:128 * (sb + 1)],
                                    pT[sb % 2][:, 0:1024]
                                    .rearrange("p (c n) -> p c n", n=128),
                                    waits=[("s_pe", F.c["s_pe"])],
                                    inc=("s_dve", 1))
                        gtm_free = F.c["s_pe"]
                        gxn_rdy = F.c["s_dve"]
                        for st in (range(2) if P3MODE & 512 else []):
                            ssl = slice(CAP * st, CAP * (st + 1))
                            for fc in range(NFC):
                                fr = fc % 2
                                uw = [("s_act", silu_free[fr])]
                                if fc == 0 and st == 0:
                                    uw += [w1_ready, w2_ready, ("s_dve", gxn_rdy)]
                                for kc in range(NCH):
                                    F.mm(pU[fr][:, 0:CAP],
                                         w1_sb[:, kc, 128 * fc:128 * (fc + 1)],
                                         gxn[:, kc, ssl],
                                         start=(kc == 0), stop=(kc == NCH - 1),
                                         waits=uw if kc == 0 else ())
                                    F.mm(pV[fr][:, 0:CAP],
                                         w2_sb[:, kc, 128 * fc:128 * (fc + 1)],
                                         gxn[:, kc, ssl],
                                         start=(kc == 0), stop=(kc == NCH - 1),
                                         waits=[("s_dve", hh_free[fr])]
                                         if kc == 0 else (),
                                         inc=("s_pe", 1) if kc == NCH - 1 else None)
                                uv_done = F.c["s_pe"]
                                F.act(su_sb[fr][:, :], pU[fr][:, 0:CAP], AF.Silu,
                                      waits=[("s_pe", uv_done)], inc=("s_act", 1))
                                silu_free[fr] = F.c["s_act"]
                                F.tt(h_sb[:, fc, :], su_sb[fr][:, :],
                                     pV[fr][:, 0:CAP], ALU.mult,
                                     waits=[("s_act", F.c["s_act"])],
                                     inc=("s_dve", 1))
                                hh_free[fr] = F.c["s_dve"]
                            h_done = F.c["s_dve"]
                            if st == 1:
                                w_free["w1"] = F.c["s_pe"]
                                w_free["w2"] = F.c["s_pe"]
                            for oc in (range(NCH) if P3MODE & 1024 else []):
                                yr = oc % 2
                                yw = [("s_dve", h_done),
                                      ("s_act", yf_freed[yr])]
                                if oc == 0 and st == 0:
                                    yw.append(w3_ready)
                                for fc in range(NFC):
                                    F.mm(pYF[yr][:, 0:CAP],
                                         w3_sb[:, fc, 128 * oc:128 * (oc + 1)],
                                         h_sb[:, fc, :],
                                         start=(fc == 0), stop=(fc == NFC - 1),
                                         waits=yw if fc == 0 else (),
                                         inc=("s_pe", 1) if fc == NFC - 1 else None)
                                F.acopy(yf_sb[:, :], pYF[yr][:, 0:CAP],
                                        waits=[("s_pe", F.c["s_pe"]),
                                               ("s_dve", F.c["s_dve"])],
                                        inc=("s_act", 1))
                                yf_freed[yr] = F.c["s_act"]
                                ycp = F.c["s_act"]
                                for sb in (range(NSB) if P3MODE & 2048 else []):
                                    reg, k = tp_ring(
                                        yf_sb[:, 128 * sb:128 * (sb + 1)],
                                        waits=[("s_act", ycp)])
                                    F.vcopy(ytm[:, sb, 128 * oc:128 * (oc + 1)],
                                            reg,
                                            waits=[("s_pe", F.c["s_pe"]),
                                                   ("s_sy", ytm_stored)]
                                            if oc == 0 else
                                            [("s_pe", F.c["s_pe"])],
                                            inc=("s_dve", 1))
                                    tp_free[k] = F.c["s_dve"]
                            F.gdma(ysr[:, 2 * NSB * ex + NSB * st:
                                       2 * NSB * ex + NSB * (st + 1), :],
                                   ytm[:, :, :],
                                   waits=[("s_dve", F.c["s_dve"])],
                                   inc=("s_sy", 16))
                            ytm_stored = F.c["s_sy"]
                        w_free["w3"] = F.c["s_pe"]
                        if ex + 1 < NEXP:
                            load_weights(ex + 1)
                    p3st["sy_all"] = F.c["s_sy"]

                run_block(blk4b, body3b)

            with ExitStack() as pc:
                ec = pc.enter_context
                acc = ec(nc.sbuf_tensor("accC", [128, NCH, HALF], F32))
                yg = [ec(nc.sbuf_tensor(f"yg{r}", [128, DM], BF16))
                      for r in range(2)]
                ysum = ec(nc.sbuf_tensor("ysum", [128, DM], BF16))
                pT = [ec(nc.psum_tensor(f"pTc{r}", [128, 1024], BF16))
                      for r in range(2)]
                blk4c = ec(nc.Block())

                def body3c():
                    x2r = x2_scr.rearrange("(c p) t -> p c t", p=128)
                    sy_all = [("s_sy", p3st["sy_all"])]
                    tp_free = {k: 0 for k in range(2)}
                    tslot = [0]

                    def tp_ring(in_ap, waits=()):
                        k = tslot[0] % 2
                        tslot[0] += 1
                        reg = pT[k][:, 0:128]
                        F.tp(reg, in_ap, ident_b[:, :],
                             waits=list(waits) + [("s_dve", tp_free[k])],
                             inc=("s_pe", 1))
                        return reg, k

                    for hf in range(2):
                        F.sdma(acc[:, :, :], x2r[:, :, HALF * hf:HALF * (hf + 1)],
                               waits=[("s_out", F.c["s_out"]),
                                      ("s_dve", F.c["s_dve"])],
                               inc=("s_x2l", 16))
                        x2_ready = ("s_x2l", F.c["s_x2l"])
                        for Bl in (range(NBH * hf, NBH * (hf + 1))
                                   if P3MODE & 64 else []):
                            bq = Bl - NBH * hf
                            for rk in range(2):
                                F.vmemset(yg[rk][:, :], 0.0)
                            ms2 = F.c["s_dve"]
                            for rk in (range(2) if P3MODE & 4 else []):
                                F.gath(yg[rk][:, :], y_scr[:, :],
                                       gis[:, Bl, rk:rk + 1], BOUND,
                                       waits=sy_all + [("s_dve", ms2)],
                                       inc=("s_yg", 16))
                            yg_rdy = ("s_yg", F.c["s_yg"])
                            if not (P3MODE & 4):
                                yg_rdy = ("s_yg", 0)
                            F.ts(yg[0][:, :], yg[0][:, :], wp[:, Bl, 0:1],
                                 ALU.mult, waits=[yg_rdy])
                            F.ts(yg[1][:, :], yg[1][:, :], wp[:, Bl, 1:2],
                                 ALU.mult)
                            F.tt(ysum[:, :], yg[0][:, :], yg[1][:, :], ALU.add,
                                 inc=("s_dve", 1))
                            ys_done = F.c["s_dve"]
                            for kc in range(NCH):
                                reg, k = tp_ring(ysum[:, 128 * kc:128 * (kc + 1)],
                                                 waits=[("s_dve", ys_done)])
                                F.tt(acc[:, kc, 128 * bq:128 * (bq + 1)],
                                     acc[:, kc, 128 * bq:128 * (bq + 1)], reg,
                                     ALU.add,
                                     waits=[("s_pe", F.c["s_pe"]), x2_ready],
                                     inc=("s_dve", 1))
                                tp_free[k] = F.c["s_dve"]
                        out_done = F.c["s_dve"]
                        for kc in range(NCH):
                            F.sdma(yf[128 * kc:128 * (kc + 1),
                                      HALF * hf:HALF * (hf + 1)],
                                   acc[:, kc, :], waits=[("s_dve", out_done)],
                                   inc=("s_out", 16))
                    if debug:
                        for n in ["xn0", "krot0", "vext0", "attnU0", "attnb0",
                                  "x20", "wtm0", "hw0"]:
                            F.wait("sync", "s_dbg_" + n, F.c["s_dbg_" + n])
                    F.wait("sync", "s_out", F.c["s_out"])

                run_block(blk4c, body3c)


    return nc, dbg


def host_constants(T, QT):
    NB = QT // 128
    p = np.arange(128)
    d = p % HD
    i = d // 2
    om = 10000.0 ** (-(i.astype(np.float64)) / 32.0)
    t = np.arange(T, dtype=np.float64)
    ang = om[:, None] * t[None, :]
    ctab = np.cos(ang).astype(np.float32)
    sgn = np.where(d % 2 == 0, -1.0, 1.0)
    stab = (np.sin(ang) * sgn[:, None]).astype(np.float32)
    pswap = np.zeros((128, 128), np.float32)
    for m in range(128):
        pswap[m ^ 1, m] = 1.0
    s_loc = np.arange(128)
    q_loc = np.arange(QT)
    dmaskv = np.zeros((128, NB, QT), np.float32)
    for j in range(NB):
        iq = q_loc // 128
        p0 = (s_loc[:, None] < ((q_loc[None, :] % 128) // PAGE + 1) * PAGE)
        dmaskv[:, j, :] = np.where(iq[None, :] < j, 0.0,
                                   np.where(iq[None, :] == j, p0, 1.0))
    identf = np.eye(128, dtype=np.float32)
    sel16v = np.zeros((16, 16, 64), np.float32)
    for h in range(16):
        sel16v[h, h, :] = 1.0
    ehv = np.zeros((128, 16, 16), np.float32)
    for h in range(16):
        ehv[64, h, h] = 1.0
    return ctab, stab, pswap, dmaskv, identf, sel16v, ehv


def make_in_maps(x, Wq, Wk, Wv, lat_k, lat_v, W_lk, W_lv, W_out, Wg, W1, W2, W3,
                 T, QT):
    ctab, stab, pswap, dmaskv, identf, sel16v, ehv = host_constants(T, QT)
    shared = {
        "wq": np.ascontiguousarray(Wq, dtype=np.float32),
        "wk": np.ascontiguousarray(Wk, dtype=np.float32),
        "wv": np.ascontiguousarray(Wv, dtype=np.float32),
        "wo": np.ascontiguousarray(W_out, dtype=np.float32),
        "wlk": np.ascontiguousarray(W_lk, dtype=np.float32),
        "wlv": np.ascontiguousarray(W_lv, dtype=np.float32),
        "latk": np.ascontiguousarray(np.asarray(lat_k)[0].T, dtype=np.float32),
        "latv": np.ascontiguousarray(np.asarray(lat_v)[0].T, dtype=np.float32),
        "wgt": np.ascontiguousarray(Wg, dtype=np.float32),
        "w1": np.ascontiguousarray(W1, dtype=np.float32),
        "w2": np.ascontiguousarray(W2, dtype=np.float32),
        "w3": np.ascontiguousarray(W3, dtype=np.float32),
        "ctab": ctab, "stab": stab, "pswap": pswap, "dmask": dmaskv,
        "identf": identf, "sel16i": sel16v, "ehi": ehv,
        "ecap": np.array([[e_ * 768 + h_ * 384 - 1 for h_ in range(2)]
                          for e_ in range(8)], np.float32),
    }
    xfs = [np.ascontiguousarray(np.asarray(x)[b].T, dtype=np.float32)
           for b in range(B)]
    in_maps = []
    for c in range(8):
        b, q = c // 4, c % 4
        m = dict(shared)
        m["xf"] = xfs[b]
        m["qparam"] = np.array([[q]], np.int32)
        in_maps.append(m)
    return in_maps


def assemble(results, T, QT):
    out = np.empty((B, T, DM), np.float32)
    for c in range(8):
        b, q = c // 4, c % 4
        yfv = np.asarray(results[c]["yf"])
        for i, Q in enumerate(qset(q)):
            out[b, QT * Q:QT * (Q + 1), :] = yfv[:, QT * i:QT * (i + 1)].T
    return out


def kernel(L0_attn_x, L0_attn_Wq, L0_attn_Wk, L0_attn_Wv, L0_attn_lat_k,
           L0_attn_lat_v, L0_attn_W_lk, L0_attn_W_lv, L0_attn_W_out,
           L0_ffn_Wg, L0_ffn_W1, L0_ffn_W2, L0_ffn_W3):
    T, QT = 8192, 512
    in_maps = make_in_maps(L0_attn_x, L0_attn_Wq, L0_attn_Wk, L0_attn_Wv,
                           L0_attn_lat_k, L0_attn_lat_v, L0_attn_W_lk,
                           L0_attn_W_lv, L0_attn_W_out, L0_ffn_Wg, L0_ffn_W1,
                           L0_ffn_W2, L0_ffn_W3, T, QT)
    nc, _ = build_nc(T, QT, debug=False)
    res = run_bass_kernel_spmd(nc, in_maps, core_ids=list(range(8)))
    return assemble(res.results, T, QT)
